# version 02
# baseline (speedup 1.0000x reference)
from collections import defaultdict

import numpy as np
import ml_dtypes
import concourse.bass as bass
import concourse.mybir as mybir
from concourse.bass_utils import run_bass_kernel_spmd

F32 = mybir.dt.float32
BF16 = mybir.dt.bfloat16
AF = mybir.ActivationFunctionType
ALU = mybir.AluOpType
AX = mybir.AxisListType

MAGIC = 12582912.0
NCORES = 8
D = 2048
DC = 16
T = 1024
EPS = 1e-5

ENG_ATTR = {"v": "vector", "s": "scalar", "g": "gpsimd", "t": "tensor", "y": "sync"}


class _Op:
    __slots__ = ("fn", "waits", "semkey", "amt")
    def __init__(self, fn, waits, semkey, amt):
        self.fn, self.waits, self.semkey, self.amt = fn, waits, semkey, amt


class EM:

    def __init__(self, nc):
        self.nc = nc
        self.sems = {}
        for e in ENG_ATTR:
            self.sems[e] = nc.alloc_semaphore(f"dep_{e}")
        self.cnt = defaultdict(int)
        self.stream = {e: [] for e in ENG_ATTR}
        self.waited = {e: defaultdict(int) for e in ENG_ATTR}
        self.last_w = {}
        self.readers = defaultdict(dict)
        self.dma_sems = {}
        self.dma_last = defaultdict(int)
        self.n_ops = 0

    def _needs(self, r, w):
        deps = {}
        def add(dd):
            for sk, c in dd.items():
                deps[sk] = max(deps.get(sk, 0), c)
        for k in r:
            add(self.last_w.get(k, {}))
        for k in w:
            add(self.last_w.get(k, {}))
            add(self.readers.get(k, {}))
        return deps

    def _record(self, eng, fn, r, w, semkey, inc_amt, extra_waits=()):
        deps = self._needs(r, w)
        for sk, c in getattr(self, "pending_fence", {}).items():
            deps[sk] = max(deps.get(sk, 0), c)
        for sk, c in extra_waits:
            deps[sk] = max(deps.get(sk, 0), c)
        wl = []
        wd = self.waited[eng]
        for sk, c in sorted(deps.items()):
            if sk == "t" and eng == "t":
                continue
            if c <= wd[sk]:
                continue
            wd[sk] = c
            wl.append((sk, c))
        self.cnt[semkey] += inc_amt
        post = self.cnt[semkey]
        for k in w:
            self.last_w[k] = {semkey: post}
            self.readers[k] = {}
        for k in r:
            rd = self.readers[k]
            rd[semkey] = max(rd.get(semkey, 0), post)
        self.stream[eng].append(_Op(fn, wl, semkey, inc_amt))
        self.n_ops += 1
        return (semkey, post)

    def op(self, eng, fn, r=(), w=(), after=()):
        return self._record(eng, fn, r, w, eng, 1, after)

    def _slot_sem(self, slot):
        semk = self.dma_sems.get(slot)
        if semk is None:
            semk = f"D{len(self.dma_sems)}"
            self.sems[semk] = self.nc.alloc_semaphore(semk)
            self.dma_sems[slot] = semk
        return semk

    def dma(self, eng, fn, r=(), w=(), slot=None, after=()):
        if slot is None:
            slot = w[0] if w else f"anon{self.n_ops}"
        semk = self._slot_sem(slot)
        extra = list(after)
        if self.dma_last[semk]:
            extra.append((semk, self.dma_last[semk]))
        tok = self._record(eng, fn, r, w, semk, 16, extra)
        self.dma_last[semk] = tok[1]
        return tok

    def cc(self, fn, r=(), w=(), slot="cc", after=()):
        semk = self._slot_sem(slot)
        extra = list(after)
        if self.dma_last[semk]:
            extra.append((semk, self.dma_last[semk]))
        tok = self._record("g", fn, r, w, semk, 1, extra)
        self.dma_last[semk] = tok[1]
        return tok

    def fence(self):
        snap = {}
        for semk, c in self.cnt.items():
            if c:
                snap[semk] = c
        self.pending_fence = snap

    def alias(self, new_keys, old_keys):
        lw, rd = {}, {}
        for o in old_keys:
            for sk, c in self.last_w.get(o, {}).items():
                lw[sk] = max(lw.get(sk, 0), c)
            for sk, c in self.readers.get(o, {}).items():
                rd[sk] = max(rd.get(sk, 0), c)
        for k in new_keys:
            for sk, c in lw.items():
                d = self.last_w.setdefault(k, {})
                d[sk] = max(d.get(sk, 0), c)
            for sk, c in rd.items():
                d = self.readers[k]
                d[sk] = max(d.get(sk, 0), c)

    def finalize(self):
        nc = self.nc
        with nc.Block() as block:
            for ek, attr in ENG_ATTR.items():
                ops = self.stream[ek]
                if not ops:
                    continue
                def make_body(ops):
                    def body(eng):
                        for o in ops:
                            for sk, c in o.waits:
                                eng.wait_ge(self.sems[sk], c)
                            inst = o.fn(eng)
                            inst.then_inc(self.sems[o.semkey], o.amt)
                    return body
                getattr(block, attr)(make_body(ops))


class _Slice:
    def __init__(self, ap):
        self._ap = ap
    def __getitem__(self, key):
        if key == slice(None):
            return self._ap
        return self._ap[key]


class Ctx:
    def __init__(self, nc, em):
        self.nc, self.em = nc, em
        self.sb = {}
        self.ps = {}
        self.arena = None
        self.persist = None
        self.bump = 0

    def freeze_persist(self):
        self.persist = dict(self.sb)

    def enter_arena(self, handle, nwords):
        self.arena = handle
        self.nwords = nwords

    def phase_reset(self):
        self.sb = dict(self.persist)
        self.bump = 0
        self.em.fence()

    def sbuf(self, name, shape, dt=F32):
        if name in self.sb and self.sb[name] is not None:
            return self.sb[name]
        if self.arena is None:
            t = self.nc.alloc_sbuf_tensor("sb_" + name, list(shape), dt)
            self.sb[name] = t
            return t
        assert len(shape) == 2 and shape[0] in (1, 128), (name, shape)
        esz = 2 if dt == BF16 else 4
        nw = (shape[1] * esz + 3) // 4
        off = (self.bump + 15) & ~15
        assert off + nw <= self.nwords, f"arena overflow at {name}: {off + nw}"
        self.bump = off + nw
        base = self.arena[:]
        if dt == BF16:
            ap = base.bitcast(BF16)[0:shape[0], 2 * off:2 * off + shape[1]]
        else:
            ap = base[0:shape[0], off:off + shape[1]]
        t = _Slice(ap)
        self.sb[name] = t
        return t

    def psum(self, name, shape):
        t = self.nc.alloc_psum_tensor("ps_" + name, list(shape), F32)
        self.ps[name] = t
        return t


def setup_common(C):
    em = C.em
    C.sbuf("ones1", [1, 128])
    C.sbuf("ones128", [128, 128])
    C.sbuf("eps_pp", [128, 1])
    C.sbuf("idf", [128, 128])
    C.sbuf("idb", [128, 128], BF16)
    C.sbuf("iot_pmf", [128, 128])
    C.psum("A", [128, 1024])
    C.psum("B", [128, 1024])
    C.psum("p0", [128, 512])
    C.psum("p1", [128, 512])
    C.ps["tb0"] = C.nc.alloc_psum_tensor("ps_tb0", [128, 1024], BF16)
    C.ps["tb1"] = C.nc.alloc_psum_tensor("ps_tb1", [128, 1024], BF16)
    em.op("v", lambda v: v.memset(C.sb["ones1"][:], 1.0), w=["ones1"])
    em.op("v", lambda v: v.memset(C.sb["ones128"][:], 1.0), w=["ones128"])
    em.op("v", lambda v: v.memset(C.sb["eps_pp"][:], EPS), w=["eps_pp"])
    em.op("g", lambda g: g.iota(C.sb["iot_pmf"][:], [[-1, 128]], base=0,
                                channel_multiplier=1,
                                allow_small_or_imprecise_dtypes=True), w=["iot_pmf"])
    em.op("v", lambda v: v.tensor_single_scalar(C.sb["idf"][:], C.sb["iot_pmf"][:],
                                                0.0, ALU.is_equal),
          r=["iot_pmf"], w=["idf"])
    em.op("v", lambda v: v.tensor_copy(C.sb["idb"][:], C.sb["idf"][:]),
          r=["idf"], w=["idb"])


def x_keys():
    return [f"X{c}" for c in range(DC)]


def xview(C, cols=None):
    X = C.sb["X"]
    ap = X[:].rearrange("p (dc t) -> p dc t", dc=DC)
    if cols is not None:
        ap = ap[:, :, cols[0]:cols[1]]
    return ap


def emit_stream_load(C, xs_ap):
    em = C.em
    if "X" not in C.sb:
        C.sbuf("X", [128, DC * T])
    em.dma("y", lambda y, o=C.sb["X"], i=xs_ap: y.dma_start(
        out=o[:].rearrange("p (dc t) -> p dc t", dc=DC),
        in_=i.rearrange("(dc p) t -> p dc t", p=128)),
        w=x_keys(), slot="Xld")


def emit_stream_store(C, ys_ap):
    em = C.em
    em.dma("g", lambda g, o=ys_ap, i=C.sb["X"]: g.dma_start(
        out=o.rearrange("(dc p) t -> p dc t", p=128),
        in_=i[:].rearrange("p (dc t) -> p dc t", dc=DC)),
        r=x_keys(), w=[], slot="Xst")
    sem = em.dma_sems["Xst"]
    em.op("g", lambda g: g.engine_nop(), after=[(sem, em.cnt[sem])])


def emit_ln_stats(C):
    em, sb = C.em, C.sb
    mu = sb.get("mu") or C.sbuf("mu", [128, T])
    var = sb.get("var") or C.sbuf("var", [128, T])
    rstd = sb.get("rstd") or C.sbuf("rstd", [128, T])
    em.alias(["sq0", "sq1"], ["xng", "xng:a", "xng:b"])
    sqs = [sb["xng"][:, 0:T], sb["xng"][:, T:2 * T]]
    PSA, PSB = C.ps["A"], C.ps["B"]
    for c in range(DC):
        sqt = sqs[c % 2]
        sqk = f"sq{c % 2}"
        xv = sb["X"][:, c * T:(c + 1) * T]
        em.op("s", lambda s, o=sqt, i=xv: s.activation(o, i, AF.Square),
              r=[f"X{c}"], w=[sqk])
        for h in range(2):
            em.op("t", lambda t, c=c, h=h, i=xv: t.matmul(
                PSA[:, h * 512:(h + 1) * 512], sb["ones128"][:],
                i[:, h * 512:(h + 1) * 512], start=(c == 0), stop=(c == DC - 1)),
                r=["ones128", f"X{c}"], w=["psA"])
            em.op("t", lambda t, c=c, h=h, i=sqt: t.matmul(
                PSB[:, h * 512:(h + 1) * 512], sb["ones128"][:],
                i[:, h * 512:(h + 1) * 512], start=(c == 0), stop=(c == DC - 1)),
                r=["ones128", sqk], w=["psB"])
    em.alias(["xng"], ["sq0", "sq1"])
    em.op("v", lambda v: v.tensor_single_scalar(mu[:], PSA[:], 1.0 / D, ALU.mult),
          r=["psA"], w=["mu"])
    em.op("v", lambda v: v.tensor_single_scalar(var[:], PSB[:], 1.0 / D, ALU.mult),
          r=["psB"], w=["var"])
    em.op("v", lambda v: v.scalar_tensor_tensor(rstd[:], mu[:], -1.0, mu[:],
                                                ALU.mult, ALU.mult),
          r=["mu"], w=["rstd"])
    em.op("v", lambda v: v.tensor_add(var[:], var[:], rstd[:]),
          r=["var", "rstd"], w=["var"])
    em.op("s", lambda s: s.activation(rstd[:], var[:], AF.Ln, bias=sb["eps_pp"][:]),
          r=["var", "eps_pp"], w=["rstd"])
    em.op("s", lambda s: s.activation(rstd[:], rstd[:], AF.Exp, scale=-0.5),
          r=["rstd"], w=["rstd"])


def emit_transpose128(C, in_ap, in_keys, ps_name, out_ap, out_keys, dtype=F32,
                      evac_eng="v", evac_fn=None):
    em, sb = C.em, C.sb
    ident = sb["idf"] if dtype == F32 else sb["idb"]
    idk = "idf" if dtype == F32 else "idb"
    ps = C.ps[ps_name]
    pr, pc = in_ap.shape[0], in_ap.shape[1]
    em.op("t", lambda t, i=in_ap, d=ident: t.transpose(ps[0:pc, 0:pr], i, d[:]),
          r=list(in_keys) + [idk], w=["ps" + ps_name])
    if evac_fn is None:
        evac_fn = lambda e, o, p: e.tensor_copy(o, p)
    em.op(evac_eng, lambda e, o=out_ap: evac_fn(e, o, ps[0:pc, 0:pr]),
          r=["ps" + ps_name], w=list(out_keys))


def emit_replicate_row(C, row_ap, row_keys, out_ap, out_keys, n, ps_name="p1",
                       scale=None):
    em, sb = C.em, C.sb
    ps = C.ps[ps_name]
    for s0 in range(0, n, 512):
        w_ = min(512, n - s0)
        em.op("t", lambda t, s0=s0, w_=w_, r_=row_ap: t.matmul(
            ps[:, 0:w_], sb["ones1"][:], r_[:, s0:s0 + w_], start=True, stop=True),
            r=list(row_keys) + ["ones1"], w=["ps" + ps_name])
        if scale is None:
            em.op("v", lambda v, s0=s0, w_=w_, o=out_ap: v.tensor_copy(
                o[:, s0:s0 + w_], ps[:, 0:w_]),
                r=["ps" + ps_name], w=list(out_keys))
        else:
            em.op("v", lambda v, s0=s0, w_=w_, o=out_ap: v.tensor_single_scalar(
                o[:, s0:s0 + w_], ps[:, 0:w_], scale, ALU.mult),
                r=["ps" + ps_name], w=list(out_keys))



def alloc_l0ffn(C):
    C.sbuf("kq", [128, DC * D], BF16)
    C.sbuf("scf", [128, DC])
    C.sbuf("bq", [128, DC])
    C.sbuf("xng", [128, DC * 256])
    C.sbuf("qxb", [128, DC * 256], BF16)
    C.sbuf("yg", [128, DC * 256])
    C.sbuf("kamax", [128, DC])
    C.sbuf("kinv", [128, DC])
    C.sbuf("kinvrow", [1, D])
    for nm in ("qx", "qy"):
        C.sbuf(nm + "inv", [128, 512])
        C.sbuf(nm + "scl", [128, 512])
    C.sbuf("qqm1", [128, 512])
    C.sbuf("qqatm", [128, 4])
    C.sbuf("qqarow", [1, 512])


def emit_l0ffn_prep(C, ker_dram, bias_dram):
    em, sb = C.em, C.sb
    kq = sb["kq"]
    kerb = [sb["xng"][:, 0:D], sb["xng"][:, D:2 * D]]
    kerk = ["xng:a", "xng:b"]
    m = [sb["yg"][:, 0:D], sb["yg"][:, D:2 * D]]
    mk = ["yg:a", "yg:b"]
    scr = sb["qxb"][:].bitcast(F32)[:, 0:D]
    scrk = "qxb:pp"
    inv_rep = sb["yg"][:, 0:D]
    invrepk = "yg:a"
    amax_pm, inv_pm, invrow = sb["kamax"], sb["kinv"], sb["kinvrow"]

    kv = ker_dram.rearrange("(dc p) f -> dc p f", p=128)
    for c in range(DC):
        b, bk = kerb[c % 2], kerk[c % 2]
        em.dma("y", lambda y, c=c, b=b: y.dma_start(out=b, in_=kv[c]),
               w=[bk], slot=bk)
        dst, dstk = (m[0], mk[0]) if c == 0 else (scr, scrk)
        em.op("v", lambda v, b=b, dst=dst: v.scalar_tensor_tensor(
            dst, b, -1.0, b, ALU.mult, ALU.max), r=[bk], w=[dstk])
        if c > 0:
            src, srck = m[(c - 1) % 2], mk[(c - 1) % 2]
            dst2, dst2k = m[c % 2], mk[c % 2]
            em.op("v", lambda v, src=src, dst2=dst2: v.tensor_max(dst2, src, scr),
                  r=[srck, scrk], w=[dst2k])
    mfin, mfink = m[(DC - 1) % 2], mk[(DC - 1) % 2]
    for dc in range(DC):
        emit_transpose128(
            C, mfin[:, dc * 128:(dc + 1) * 128], [mfink], "p0",
            amax_pm[:, dc:dc + 1], ["kamax"],
            evac_fn=lambda e, o, p: e.reduce_max(o, p, axis=AX.X))
    em.op("v", lambda v: v.reciprocal(inv_pm[:], amax_pm[:]), r=["kamax"], w=["kinv"])
    em.op("v", lambda v: v.tensor_single_scalar(inv_pm[:], inv_pm[:], 127.0, ALU.mult),
          r=["kinv"], w=["kinv"])
    em.op("v", lambda v: v.tensor_single_scalar(sb["scf"][:], amax_pm[:],
                                                1.0 / 127.0, ALU.mult),
          r=["kamax"], w=["scf"])
    for dc in range(DC):
        emit_transpose128(C, inv_pm[:, dc:dc + 1], ["kinv"], "p0",
                          invrow[:, dc * 128:(dc + 1) * 128], ["kinvrow"])
    emit_replicate_row(C, invrow[:], ["kinvrow"], inv_rep, [invrepk], D)
    for c in range(DC):
        b, bk = kerb[c % 2], kerk[c % 2]
        em.dma("y", lambda y, c=c, b=b: y.dma_start(out=b, in_=kv[c]),
               w=[bk], slot=bk)
        em.op("v", lambda v, b=b: v.tensor_mul(scr, b, inv_rep),
              r=[bk, invrepk], w=[scrk])
        em.op("v", lambda v, c=c: v.tensor_scalar(
            kq[:, c * D:(c + 1) * D], scr, MAGIC, MAGIC, ALU.add, ALU.subtract),
            r=[scrk], w=[f"kq{c}"])
    bt = C.sbuf("bt", [128, DC])
    em.dma("y", lambda y: y.dma_start(
        out=bt[:], in_=bias_dram.rearrange("(dc p) -> p dc", p=128),
        allow_slow_non_contiguous=True), w=["bt"], slot="bt")
    babs = C.sbuf("babs", [128, 1])
    em.op("v", lambda v: v.reduce_max(babs[:], bt[:], axis=AX.X,
                                      apply_absolute_value=True),
          r=["bt"], w=["babs"])
    brow = C.sbuf("brow", [1, 128])
    emit_transpose128(C, babs[:], ["babs"], "p0", brow[:], ["brow"])
    bmax = C.sbuf("bmax", [1, 2])
    em.op("v", lambda v: v.reduce_max(bmax[:, 0:1], brow[:], axis=AX.X),
          r=["brow"], w=["bmax"])
    em.op("v", lambda v: v.reciprocal(bmax[:, 1:2], bmax[:, 0:1]),
          r=["bmax"], w=["bmax"])
    em.op("v", lambda v: v.tensor_single_scalar(bmax[:, 1:2], bmax[:, 1:2],
                                                127.0, ALU.mult),
          r=["bmax"], w=["bmax"])
    em.op("v", lambda v: v.tensor_single_scalar(bmax[:, 0:1], bmax[:, 0:1],
                                                1.0 / 127.0, ALU.mult),
          r=["bmax"], w=["bmax"])
    brep = C.sbuf("brep", [128, 2])
    emit_replicate_row(C, bmax[:], ["bmax"], brep[:], ["brep"], 2)
    em.op("v", lambda v: v.tensor_scalar(sb["bq"][:], bt[:], brep[:, 1:2],
                                         None, ALU.mult),
          r=["bt", "brep"], w=["bq"])
    em.op("v", lambda v: v.tensor_scalar(sb["bq"][:], sb["bq"][:], MAGIC, MAGIC,
                                         ALU.add, ALU.subtract),
          r=["bq"], w=["bq"])
    em.op("v", lambda v: v.tensor_scalar(sb["bq"][:], sb["bq"][:], brep[:, 0:1],
                                         None, ALU.mult),
          r=["bq", "brep"], w=["bq"])


def emit_quant_scales(C, src_view, src_keys, G, tag, extra_mult=1.0):
    em, sb = C.em, C.sb
    nm = tag
    m1, atm, arow = sb["qqm1"], sb["qqatm"], sb["qqarow"]
    inv, scl = sb[nm + "inv"], sb[nm + "scl"]
    ntile = G // 128
    em.op("v", lambda v, s=src_view: v.reduce_max(
        m1[:, 0:G], s.rearrange("p dc t -> p t dc"), axis=AX.X,
        apply_absolute_value=True), r=list(src_keys), w=["qqm1"])
    for i in range(ntile):
        emit_transpose128(
            C, m1[:, i * 128:(i + 1) * 128], ["qqm1"], "p0",
            atm[:, i:i + 1], ["qqatm"],
            evac_fn=lambda e, o, p: e.reduce_max(o, p, axis=AX.X))
    for i in range(ntile):
        emit_transpose128(C, atm[:, i:i + 1], ["qqatm"], "p0",
                          arow[:, i * 128:(i + 1) * 128], ["qqarow"])
    ps = C.ps["p1"]
    em.op("t", lambda t: t.matmul(ps[:, 0:G], sb["ones1"][:], arow[:, 0:G],
                                  start=True, stop=True),
          r=["qqarow", "ones1"], w=["psp1"])
    em.op("v", lambda v: v.reciprocal(inv[:, 0:G], ps[:, 0:G]),
          r=["psp1"], w=[nm + "inv"])
    em.op("v", lambda v: v.tensor_single_scalar(inv[:, 0:G], inv[:, 0:G],
                                                127.0, ALU.mult),
          r=[nm + "inv"], w=[nm + "inv"])
    em.op("v", lambda v: v.tensor_single_scalar(scl[:, 0:G], ps[:, 0:G],
                                                extra_mult / 127.0, ALU.mult),
          r=["psp1"], w=[nm + "scl"])
    return inv, scl, nm


def emit_l0ffn_body(C):
    em, sb = C.em, C.sb
    G = 256
    xng, qxb, yg = sb["xng"], sb["qxb"], sb["yg"]
    mu, rstd = sb["mu"], sb["rstd"]
    em.alias(["xng"], ["xng:a", "xng:b"])
    em.alias(["yg"], ["yg:a", "yg:b"])
    em.alias(["qxb"], ["qxb:pp"])
    for g in range(T // G):
        t0 = g * G
        xnv = xng[:].rearrange("p (dc t) -> p dc t", dc=DC)
        qxv = qxb[:].rearrange("p (dc t) -> p dc t", dc=DC)
        ygv = yg[:].rearrange("p (dc t) -> p dc t", dc=DC)
        xv = xview(C, (t0, t0 + G))
        mub = mu[:, t0:t0 + G].unsqueeze(1).broadcast_to([128, DC, G])
        rsb = rstd[:, t0:t0 + G].unsqueeze(1).broadcast_to([128, DC, G])
        em.op("v", lambda v, a=xnv, b=xv, m=mub: v.tensor_sub(a, b, m),
              r=x_keys() + ["mu"], w=["xng"])
        em.op("v", lambda v, a=xnv, m=rsb: v.tensor_mul(a, a, m),
              r=["xng", "rstd"], w=["xng"])
        inv, scl, qk = emit_quant_scales(C, xnv, ["xng"], G, "qx")
        invb = inv[:, 0:G].unsqueeze(1).broadcast_to([128, DC, G])
        em.op("v", lambda v, a=xnv, m=invb: v.tensor_mul(a, a, m),
              r=["xng", qk + "inv"], w=["xng"])
        em.op("v", lambda v, o=qxv, a=xnv: v.tensor_scalar(
            o, a, MAGIC, MAGIC, ALU.add, ALU.subtract),
            r=["xng"], w=["qxb"])
        for doc in range(DC):
            ps = C.ps[["A", "B"][doc % 2]]
            psk = ["psA", "psB"][doc % 2]
            for dinc in range(DC):
                em.op("t", lambda t, doc=doc, dinc=dinc, ps=ps: t.matmul(
                    ps[:, 0:G],
                    sb["kq"][:, dinc * D + doc * 128: dinc * D + doc * 128 + 128],
                    qxb[:, dinc * G:(dinc + 1) * G],
                    start=(dinc == 0), stop=(dinc == DC - 1)),
                    r=[f"kq{dinc}", "qxb"], w=[psk])
            em.op("v", lambda v, doc=doc, ps=ps, s2=scl: v.scalar_tensor_tensor(
                xng[:, doc * G:(doc + 1) * G], ps[:, 0:G],
                sb["scf"][:, doc:doc + 1], s2[:, 0:G], ALU.mult, ALU.mult),
                r=[psk, "scf", qk + "scl"], w=["xng"])
            em.op("s", lambda s, doc=doc: s.activation(
                yg[:, doc * G:(doc + 1) * G], xng[:, doc * G:(doc + 1) * G],
                AF.Relu, bias=sb["bq"][:, doc:doc + 1]),
                r=["xng", "bq"], w=["yg"])
        inv2, scl2, qk2 = emit_quant_scales(C, ygv, ["yg"], G, "qy",
                                            extra_mult=0.5)
        inv2b = inv2[:, 0:G].unsqueeze(1).broadcast_to([128, DC, G])
        scl2b = scl2[:, 0:G].unsqueeze(1).broadcast_to([128, DC, G])
        em.op("v", lambda v, o=xnv, a=ygv, m=inv2b: v.tensor_mul(o, a, m),
              r=["yg", qk2 + "inv"], w=["xng"])
        em.op("v", lambda v, a=xnv: v.tensor_scalar(
            a, a, MAGIC, MAGIC, ALU.add, ALU.subtract), r=["xng"], w=["xng"])
        em.op("v", lambda v, a=xnv, m=scl2b: v.tensor_mul(a, a, m),
              r=["xng", qk2 + "scl"], w=["xng"])
        em.op("v", lambda v, o=xv, a=xnv: v.tensor_add(o, o, a),
              r=["xng"] + x_keys(), w=x_keys())




H = 16
HD = 128
CH = 128
NCH = T // CH


def alloc_l0attn(C):
    nc = C.nc
    C.psum("tb0", [128, 1024])
    C.psum("tb1", [128, 1024])


def emit_l0attn(C, W, xs_ap, ys_ap, cid_ap, kv_dram, sl_dram, sall_dram):
    nc, em, sb = C.nc, C.em, C.sb
    G = 512
    NG = T // G
    C4 = G // CH
    xg = C.sbuf("xg", [128, DC * G])
    xnb = C.sbuf("xnbg", [128, DC * G], BF16)
    kg = C.sbuf("kg", [128, DC * G], BF16)
    vg = C.sbuf("vg", [128, DC * G], BF16)
    qg = C.sbuf("qg", [128, DC * G], BF16)
    sig = C.sbuf("sigg", [128, DC * G], BF16)
    ogb = C.sbuf("ogb", [128, DC * G], BF16)
    ogc = C.sbuf("ogc", [128, D])
    sqo = C.sbuf("sqo", [128, D])
    rstg = C.sbuf("rstg", [128, D])
    S = C.sbuf("S", [128, H * HD])
    Sbf = C.sbuf("Sbf", [128, H * HD], BF16)
    Dm = C.sbuf("Dm", [128, H * HD], BF16)
    qdec = C.sbuf("qdec", [128, H * HD], BF16)
    kdec = C.sbuf("kdec", [128, H])
    cdr = C.sbuf("cdr", [128, H])
    ldr = C.sbuf("ldr", [128, H])
    nldr = C.sbuf("nldr", [128, H])
    ldrow = C.sbuf("ldrow", [1, H])
    nwt = C.sbuf("nwt", [128, H])
    tri = C.sbuf("tri", [128, 128])
    ir = C.sbuf("irow", [1, 128])
    irep = C.sbuf("irep", [128, 128])
    i127 = C.sbuf("i127", [128, 1])
    cidr = C.sbuf("cidr", [128, 1])
    cid1 = C.sbuf("cid1", [1, 1])
    kts = C.sbuf("kts", [128, 128], BF16)
    vtile = C.sbuf("vtile", [128, 128], BF16)
    qs = C.sbuf("qsc", [128, 128], BF16)
    sdt = C.sbuf("sdt", [128, 128], BF16)
    wbuf = [C.sbuf("wb0", [128, DC * 128], BF16), C.sbuf("wb1", [128, DC * 128], BF16)]
    wg1t = C.sbuf("wg1t", [128, H * 128], BF16)
    wg2t = C.sbuf("wg2t", [128, D], BF16)
    acc = [xg[:, 0:H * HD], xg[:, H * HD:2 * H * HD]]
    stmp = xg[:, 2 * H * HD:3 * H * HD]
    xacc = C.sbuf("xacc", [128, G])
    PSA, PSB, P0, P1 = C.ps["A"], C.ps["B"], C.ps["p0"], C.ps["p1"]
    TB0, TB1 = C.ps["tb0"], C.ps["tb1"]

    emit_ln_stats_dram(C, xs_ap)
    wqv = W["wq"].rearrange("(dc p) f -> p dc f", p=128)
    wkv = W["wk"].rearrange("(dc p) f -> p dc f", p=128)
    wvv = W["wv"].rearrange("(dc p) f -> p dc f", p=128)
    wov = W["wo"].rearrange("(dc p) f -> p dc f", p=128)

    em.dma("y", lambda y: y.dma_start(out=ldrow[:], in_=W["ld"].rearrange("(o h) -> o h", o=1)),
           w=["ldrow"], slot="ldrow")
    emit_replicate_row(C, ldrow[:], ["ldrow"], ldr[:], ["ldr"], H)
    em.op("v", lambda v: v.tensor_single_scalar(nldr[:], ldr[:], -1.0, ALU.mult),
          r=["ldr"], w=["nldr"])
    em.op("s", lambda s: s.activation(cdr[:], ldr[:], AF.Exp, scale=float(CH)),
          r=["ldr"], w=["cdr"])
    em.dma("y", lambda y: y.dma_start(
        out=nwt[:], in_=W["nw"].rearrange("(dc p) -> p dc", p=128),
        allow_slow_non_contiguous=True), w=["nwt"], slot="nwt")
    em.dma("y", lambda y: y.dma_start(out=cid1[:], in_=cid_ap.rearrange("(a o) -> a o", a=1)),
           w=["cid1"], slot="cid1")
    emit_replicate_row(C, cid1[:], ["cid1"], cidr[:], ["cidr"], 1)
    em.op("g", lambda g: g.iota(ir[:], [[1, 128]], base=1, channel_multiplier=0,
                                allow_small_or_imprecise_dtypes=True), w=["irow"])
    emit_replicate_row(C, ir[:], ["irow"], irep[:], ["irep"], 128)
    em.op("g", lambda g: g.iota(i127[:], [[1, 1]], base=127, channel_multiplier=-1,
                                allow_small_or_imprecise_dtypes=True), w=["i127"])
    em.op("v", lambda v: v.tensor_single_scalar(tri[:], sb["iot_pmf"][:], 0.5,
                                                ALU.is_lt), r=["iot_pmf"], w=["tri"])
    miot = C.sbuf("miot", [128, 128])
    em.op("v", lambda v: v.tensor_scalar(miot[:], sb["iot_pmf"][:], -1.0, 0.0,
                                         ALU.mult, ALU.max),
          r=["iot_pmf"], w=["miot"])
    for h in range(H):
        em.op("s", lambda s, h=h: s.activation(
            sb["qdec"][:, h * HD:(h + 1) * HD], irep[:], AF.Exp,
            scale=ldr[:, h:h + 1]), r=["irep", "ldr"], w=["qdec"])
        em.op("s", lambda s, h=h: s.activation(
            kdec[:, h:h + 1], i127[:], AF.Exp, scale=ldr[:, h:h + 1]),
            r=["i127", "ldr"], w=["kdec"])
        em.op("s", lambda s, h=h: s.activation(
            ogc[:, h * HD:(h + 1) * HD], miot[:], AF.Exp,
            scale=ldr[:, h:h + 1]), r=["miot", "ldr"], w=["ogc:d"])
        em.op("v", lambda v, h=h: v.tensor_mul(
            Dm[:, h * HD:(h + 1) * HD], ogc[:, h * HD:(h + 1) * HD], tri[:]),
            r=["ogc:d", "tri"], w=["Dm"])
    em.alias(["ogc"], ["ogc:d"])
    em.dma("g", lambda g: g.dma_start(
        out=wg1t[:].rearrange("p (dc f) -> p dc f", dc=H),
        in_=W["wg1"].rearrange("(dc p) f -> p dc f", p=128)), w=["wg1t"], slot="wg1t")
    em.dma("g", lambda g: g.dma_start(out=wg2t[:], in_=W["wg2"]), w=["wg2t"], slot="wg2t")
    em.op("v", lambda v: v.memset(S[:], 0.0), w=["S"])

    def ln_group(g):
        t0 = g * G
        em.alias(["xg"], ["var"])
        xgv = xg[:].rearrange("p (dc t) -> p dc t", dc=DC)
        em.dma("y", lambda y, t0=t0: y.dma_start(
            out=xgv, in_=xs_ap.rearrange("(dc p) t -> p dc t", p=128)[:, :, t0:t0 + G]),
            w=["xg"], slot="xg")
        mub = sb["mu"][:, t0:t0 + G].unsqueeze(1).broadcast_to([128, DC, G])
        rsb = sb["rstd"][:, t0:t0 + G].unsqueeze(1).broadcast_to([128, DC, G])
        em.op("v", lambda v, m=mub: v.tensor_sub(xgv, xgv, m),
              r=["xg", "mu"], w=["xg"])
        em.op("v", lambda v, m=rsb: v.tensor_mul(xgv, xgv, m),
              r=["xg", "rstd"], w=["xg"])
        em.op("v", lambda v: v.tensor_copy(xnb[:], xg[:]), r=["xg"], w=["xnb"])

    def proj_fm(out_tile, out_key, w_view, wslotbase, act):
        for oc in range(DC):
            wb = wbuf[oc % 2]
            wbk = f"wb{oc % 2}"
            em.dma("g", lambda g, oc=oc, wb=wb: g.dma_start(
                out=wb[:].rearrange("p (dc f) -> p dc f", dc=DC),
                in_=w_view[:, :, oc * 128:(oc + 1) * 128]),
                w=[wbk], slot=wslotbase + str(oc % 2))
            ps = [PSA, PSB][oc % 2]
            psk = ["psA", "psB"][oc % 2]
            for dinc in range(DC):
                em.op("t", lambda t, oc=oc, dinc=dinc, wb=wb, ps=ps: t.matmul(
                    ps[:, 0:G], wb[:, dinc * 128:(dinc + 1) * 128],
                    xnb[:, dinc * G:(dinc + 1) * G],
                    start=(dinc == 0), stop=(dinc == DC - 1)),
                    r=[wbk, "xnb"], w=[psk])
            if act is None:
                em.op("s", lambda s, oc=oc, ps=ps: s.copy(
                    out_tile[:, oc * G:(oc + 1) * G], ps[:, 0:G]),
                    r=[psk], w=[out_key])
            else:
                em.op("s", lambda s, oc=oc, ps=ps, a=act: s.activation(
                    out_tile[:, oc * G:(oc + 1) * G], ps[:, 0:G], a),
                    r=[psk], w=[out_key])

    def s_chain(g, phase):
        for c4 in range(C4):
            for h in range(H):
                kv_sl = kg[:, h * G + c4 * CH: h * G + (c4 + 1) * CH]
                vv_sl = vg[:, h * G + c4 * CH: h * G + (c4 + 1) * CH]
                em.op("t", lambda t, s=kv_sl: t.transpose(
                    TB0[0:128, 0:128], s, sb["idb"][:]),
                    r=["kg", "idb"], w=["pstb0"])
                em.op("v", lambda v, h=h: v.tensor_scalar(
                    kts[:], TB0[0:128, 0:128], kdec[:, h:h + 1], None, ALU.mult),
                    r=["pstb0", "kdec"], w=["kts"])
                em.op("t", lambda t, s=vv_sl: t.transpose(
                    TB1[0:128, 0:128], s, sb["idb"][:]),
                    r=["vg", "idb"], w=["pstb1"])
                em.op("s", lambda s: s.copy(vtile[:], TB1[0:128, 0:128]),
                      r=["pstb1"], w=["vtile"])
                if phase == "B":
                    phase_b_inner(g, c4, h)
                em.op("t", lambda t: t.matmul(
                    PSA[:, 0:128], kts[:], vtile[:], start=True, stop=True),
                    r=["kts", "vtile"], w=["psA"])
                em.op("v", lambda v, h=h: v.scalar_tensor_tensor(
                    S[:, h * HD:(h + 1) * HD], S[:, h * HD:(h + 1) * HD],
                    cdr[:, h:h + 1], PSA[:, 0:128], ALU.mult, ALU.add),
                    r=["S", "cdr", "psA"], w=["S"])
                if phase == "B":
                    em.op("v", lambda v, h=h: v.tensor_copy(
                        Sbf[:, h * HD:(h + 1) * HD], S[:, h * HD:(h + 1) * HD]),
                        r=["S"], w=["Sbf"])
            if phase == "B":
                phase_b_chunk_tail(g, c4)

    def phase_b_inner(g, c4, h):
        ksl = kg[:, h * G + c4 * CH:h * G + (c4 + 1) * CH]
        qsl = qg[:, h * G + c4 * CH:h * G + (c4 + 1) * CH]
        em.op("t", lambda t, ksl=ksl, qsl=qsl: t.matmul(
            P0[:, 0:128], ksl, qsl, start=True, stop=True),
            r=["kg", "qg"], w=["psp0"])
        em.op("v", lambda v, h=h: v.tensor_mul(
            sdt[:], P0[:, 0:128], Dm[:, h * HD:(h + 1) * HD]),
            r=["psp0", "Dm"], w=["sdt"])
        em.op("v", lambda v, h=h, qsl=qsl: v.tensor_mul(
            qs[:], qsl, qdec[:, h * HD:(h + 1) * HD]),
            r=["qg", "qdec"], w=["qsc"])
        em.op("t", lambda t: t.matmul(P1[:, 0:128], vtile[:], sdt[:],
                                      start=True, stop=False),
              r=["vtile", "sdt"], w=["psp1"])
        em.op("t", lambda t, h=h: t.matmul(
            P1[:, 0:128], Sbf[:, h * HD:(h + 1) * HD], qs[:],
            start=False, stop=True), r=["Sbf", "qsc"], w=["psp1"])
        em.op("v", lambda v, h=h, g=g, c4=c4: v.tensor_mul(
            ogc[:, h * HD:(h + 1) * HD], P1[:, 0:128],
            sig[:, h * G + c4 * CH:h * G + (c4 + 1) * CH]),
            r=["psp1", "sigg"], w=["ogc"])

    def phase_b_chunk_tail(g, c4):
        em.op("s", lambda s: s.activation(sqo[:], ogc[:], AF.Square),
              r=["ogc"], w=["sqo"])
        for q in range(4):
            ps = [PSA, PSB][q // 2]
            psk = ["psA", "psB"][q // 2]
            em.op("t", lambda t, q=q, ps=ps: t.matmul(
                ps[:, (q % 2) * 512:(q % 2 + 1) * 512], sb["ones128"][:],
                sqo[:, q * 512:(q + 1) * 512], start=True, stop=True),
                r=["ones128", "sqo"], w=[psk])
        for q in range(4):
            ps = [PSA, PSB][q // 2]
            psk = ["psA", "psB"][q // 2]
            em.op("s", lambda s, q=q, ps=ps: s.activation(
                rstg[:, q * 512:(q + 1) * 512], ps[:, (q % 2) * 512:(q % 2 + 1) * 512],
                AF.Ln, bias=sb["eps_pp"][:], scale=1.0 / HD),
                r=[psk, "eps_pp"], w=["rstg"])
        em.op("s", lambda s: s.activation(rstg[:], rstg[:], AF.Exp, scale=-0.5),
              r=["rstg"], w=["rstg"])
        em.op("v", lambda v: v.tensor_mul(ogc[:], ogc[:], rstg[:]),
              r=["ogc", "rstg"], w=["ogc"])
        ogv = ogc[:].rearrange("p (h e) -> p h e", h=H)
        nb = nwt[:].unsqueeze(2).broadcast_to([128, H, HD])
        ob = ogb[:].rearrange("p (h t) -> p h t", h=H)[:, :, c4 * CH:(c4 + 1) * CH]
        em.op("v", lambda v, ob=ob: v.tensor_mul(ob, ogv, nb),
              r=["ogc", "nwt"], w=["ogb"])

    for g in range(NG):
        ln_group(g)
        proj_fm(kg, "kg", wkv, "wk", AF.Silu)
        proj_fm(vg, "vg", wvv, "wv", AF.Silu)
        s_chain(g, "A")
        em.dma("g", lambda gg, g=g: gg.dma_start(
            out=kv_dram[g * 2], in_=kg[:]), r=["kg"], w=[f"kvd{g}k"], slot="kvs")
        em.dma("g", lambda gg, g=g: gg.dma_start(
            out=kv_dram[g * 2 + 1], in_=vg[:]), r=["vg"], w=[f"kvd{g}v"], slot="kvs")

    em.dma("g", lambda g: g.dma_start(out=sl_dram, in_=S[:]), r=["S"],
           w=["sld"], slot="sld")
    em.cc(lambda g: g.collective_compute(
        "AllGather", ALU.bypass, replica_groups=[list(range(NCORES))],
        ins=[sl_dram.opt()], outs=[sall_dram.opt()]),
        r=["sld"], w=["sall"], slot="ccS")
    em.alias(["sacc0", "sacc1", "stmp"], ["xg"])
    em.op("v", lambda v: v.memset(acc[0], 0.0), w=["sacc0"])
    coef = C.sbuf("coef", [128, H])
    mj = C.sbuf("mjt", [128, 1])
    for j in range(NCORES - 1):
        em.dma("y", lambda y, j=j: y.dma_start(
            out=stmp, in_=sall_dram[j * 128:(j + 1) * 128, :]),
            r=["sall"], w=["stmp"], slot="stmp")
        em.op("v", lambda v, j=j: v.tensor_single_scalar(
            mj[:], cidr[:], float(-1 - j), ALU.add), r=["cidr"], w=["mjt"])
        em.op("v", lambda v: v.tensor_scalar(mj[:], mj[:], 1024.0, 0.0,
                                             ALU.mult, ALU.max),
              r=["mjt"], w=["mjt"])
        em.op("s", lambda s: s.activation(coef[:], ldr[:], AF.Exp, scale=mj[:]),
              r=["ldr", "mjt"], w=["coef"])
        em.op("v", lambda v, j=j: v.tensor_single_scalar(
            mj[:], cidr[:], float(j) + 0.5, ALU.is_gt), r=["cidr"], w=["mjt"])
        em.op("v", lambda v: v.tensor_scalar(coef[:], coef[:], mj[:], None,
                                             ALU.mult), r=["coef", "mjt"], w=["coef"])
        a0, a1 = acc[j % 2], acc[(j + 1) % 2]
        a0k, a1k = f"sacc{j % 2}", f"sacc{(j + 1) % 2}"
        for h in range(H):
            em.op("v", lambda v, h=h, a0=a0, a1=a1: v.scalar_tensor_tensor(
                a1[:, h * HD:(h + 1) * HD], stmp[:, h * HD:(h + 1) * HD],
                coef[:, h:h + 1], a0[:, h * HD:(h + 1) * HD], ALU.mult, ALU.add),
                r=["stmp", "coef", a0k], w=[a1k])
    afin, afink = acc[(NCORES - 1) % 2], f"sacc{(NCORES - 1) % 2}"
    em.op("v", lambda v, a=afin: v.tensor_copy(S[:], a), r=[afink], w=["S"])
    em.op("v", lambda v: v.tensor_copy(Sbf[:], S[:]), r=["S"], w=["Sbf"])
    em.alias(["xg"], ["sacc0", "sacc1", "stmp"])

    for g in range(NG):
        ln_group(g)
        em.dma("y", lambda y, g=g: y.dma_start(out=kg[:], in_=kv_dram[g * 2]),
               r=[f"kvd{g}k"], w=["kg"], slot="kgld")
        em.dma("y", lambda y, g=g: y.dma_start(out=vg[:], in_=kv_dram[g * 2 + 1]),
               r=[f"kvd{g}v"], w=["vg"], slot="vgld")
        proj_fm(qg, "qg", wqv, "wq", AF.Silu)
        for dinc in range(DC):
            em.op("t", lambda t, dinc=dinc: t.matmul(
                P0[:, 0:G], wg1t[:, dinc * 128:(dinc + 1) * 128],
                xnb[:, dinc * G:(dinc + 1) * G],
                start=(dinc == 0), stop=(dinc == DC - 1)),
                r=["wg1t", "xnb"], w=["psp0"])
        g1b = kts
        em.op("v", lambda v: v.tensor_copy(
            ogc[:].bitcast(BF16)[:, 0:G], P0[:, 0:G]), r=["psp0"], w=["g1b"])
        for oc in range(DC):
            em.op("t", lambda t, oc=oc: t.matmul(
                P1[:, 0:G], wg2t[:, oc * 128:(oc + 1) * 128],
                ogc[:].bitcast(BF16)[:, 0:G], start=True, stop=True),
                r=["wg2t", "g1b"], w=["psp1"])
            em.op("s", lambda s, oc=oc: s.activation(
                sig[:, oc * G:(oc + 1) * G], P1[:, 0:G], AF.Sigmoid),
                r=["psp1"], w=["sigg"])
        em.alias(["ogc"], ["g1b"])
        s_chain(g, "B")
        for oc in range(DC):
            ps = [PSA, PSB][oc % 2]
            psk = ["psA", "psB"][oc % 2]
            wb = wbuf[oc % 2]
            wbk = f"wb{oc % 2}"
            em.dma("g", lambda gg, oc=oc, wb=wb: gg.dma_start(
                out=wb[:].rearrange("p (dc f) -> p dc f", dc=DC),
                in_=wov[:, :, oc * 128:(oc + 1) * 128]),
                w=[wbk], slot="wo" + str(oc % 2))
            for h in range(H):
                em.op("t", lambda t, oc=oc, h=h, wb=wb, ps=ps: t.matmul(
                    ps[:, 0:G], wb[:, h * 128:(h + 1) * 128],
                    ogb[:, h * G:(h + 1) * G],
                    start=(h == 0), stop=(h == H - 1)),
                    r=[wbk, "ogb"], w=[psk])
            em.dma("y", lambda y, oc=oc, g=g: y.dma_start(
                out=xacc[:], in_=xs_ap[oc * 128:(oc + 1) * 128, g * G:(g + 1) * G]),
                w=["xacc"], slot="xacc")
            em.op("v", lambda v, ps=ps: v.tensor_add(xacc[:], xacc[:], ps[:, 0:G]),
                  r=["xacc", psk], w=["xacc"])
            em.dma("g", lambda gg, oc=oc, g=g: gg.dma_start(
                out=ys_ap[oc * 128:(oc + 1) * 128, g * G:(g + 1) * G], in_=xacc[:]),
                r=["xacc"], w=[], slot="ysst")
    sem = em.dma_sems["ysst"]
    em.op("g", lambda g: g.engine_nop(), after=[(sem, em.cnt[sem])])


def build_l0attn_program():
    nc = bass.Bass("TRN2", target_bir_lowering=False)
    em = EM(nc)
    C = Ctx(nc, em)
    xs = nc.dram_tensor("xs", [D, T], F32, kind="ExternalInput").ap()
    cid = nc.dram_tensor("cid", [1], F32, kind="ExternalInput").ap()
    W = {
        "wq": nc.dram_tensor("wq", [D, D], F32, kind="ExternalInput").ap(),
        "wk": nc.dram_tensor("wk", [D, D], F32, kind="ExternalInput").ap(),
        "wv": nc.dram_tensor("wv", [D, D], F32, kind="ExternalInput").ap(),
        "wo": nc.dram_tensor("wo", [D, D], F32, kind="ExternalInput").ap(),
        "wg1": nc.dram_tensor("wg1", [D, 128], F32, kind="ExternalInput").ap(),
        "wg2": nc.dram_tensor("wg2", [128, D], F32, kind="ExternalInput").ap(),
        "ld": nc.dram_tensor("ld", [H], F32, kind="ExternalInput").ap(),
        "nw": nc.dram_tensor("nw", [D], F32, kind="ExternalInput").ap(),
    }
    ys = nc.dram_tensor("ys", [D, T], F32, kind="ExternalOutput").ap()
    kv_dram = [nc.dram_tensor(f"kvs{i}", [128, DC * 512], BF16).ap() for i in range(4)]
    sl_dram = nc.dram_tensor("slg", [128, H * HD], F32).ap()
    sall_dram = nc.dram_tensor("sag", [NCORES * 128, H * HD], F32).ap()
    setup_common(C)
    emit_l0attn(C, W, xs, ys, cid, kv_dram, sl_dram, sall_dram)
    em.finalize()
    return nc


def emit_ln_stats_dram(C, xs_ap):
    em, sb = C.em, C.sb
    mu = sb.get("mu") or C.sbuf("mu", [128, T])
    rstd = sb.get("rstd") or C.sbuf("rstd", [128, T])
    xgt = sb["xg"]
    xc = [xgt[:, 0:T], xgt[:, T:2 * T]]
    sq = [xgt[:, 2 * T:3 * T], xgt[:, 3 * T:4 * T]]
    var = _Slice(xgt[:, 4 * T:5 * T])
    sb["var"] = var
    PSA, PSB = C.ps["A"], C.ps["B"]
    xv = xs_ap.rearrange("(dc p) t -> dc p t", p=128)
    for c in range(DC):
        xt, xk = xc[c % 2], f"lnx{c % 2}"
        st, sk = sq[c % 2], f"lnq{c % 2}"
        em.dma("y", lambda y, c=c, xt=xt: y.dma_start(out=xt, in_=xv[c]),
               w=[xk], slot=xk)
        em.op("s", lambda s, xt=xt, st=st: s.activation(st, xt, AF.Square),
              r=[xk], w=[sk])
        for h in range(2):
            em.op("t", lambda t, c=c, h=h, xt=xt: t.matmul(
                PSA[:, h * 512:(h + 1) * 512], sb["ones128"][:],
                xt[:, h * 512:(h + 1) * 512], start=(c == 0), stop=(c == DC - 1)),
                r=["ones128", xk], w=["psA"])
            em.op("t", lambda t, c=c, h=h, st=st: t.matmul(
                PSB[:, h * 512:(h + 1) * 512], sb["ones128"][:],
                st[:, h * 512:(h + 1) * 512], start=(c == 0), stop=(c == DC - 1)),
                r=["ones128", sk], w=["psB"])
    em.alias(["xg"], ["lnx0", "lnx1", "lnq0", "lnq1"])
    em.op("v", lambda v: v.tensor_single_scalar(mu[:], PSA[:], 1.0 / D, ALU.mult),
          r=["psA"], w=["mu"])
    em.op("v", lambda v: v.tensor_single_scalar(var[:], PSB[:], 1.0 / D, ALU.mult),
          r=["psB"], w=["var"])
    em.op("v", lambda v: v.scalar_tensor_tensor(rstd[:], mu[:], -1.0, mu[:],
                                                ALU.mult, ALU.mult),
          r=["mu"], w=["rstd"])
    em.op("v", lambda v: v.tensor_add(var[:], var[:], rstd[:]),
          r=["var", "rstd"], w=["var"])
    em.op("s", lambda s: s.activation(rstd[:], var[:], AF.Ln, bias=sb["eps_pp"][:]),
          r=["var", "eps_pp"], w=["rstd"])
    em.op("s", lambda s: s.activation(rstd[:], rstd[:], AF.Exp, scale=-0.5),
          r=["rstd"], w=["rstd"])


def run_l0attn(x, wq, wk, wv, wo, wg1, wg2, ld, nw):
    nc = build_l0attn_program()
    shards = _shard_stream(x)
    in_maps = []
    for c in range(NCORES):
        in_maps.append({
            "xs": shards[c], "cid": np.array([c], np.float32),
            "wq": np.ascontiguousarray(wq), "wk": np.ascontiguousarray(wk),
            "wv": np.ascontiguousarray(wv), "wo": np.ascontiguousarray(wo),
            "wg1": np.ascontiguousarray(wg1), "wg2": np.ascontiguousarray(wg2),
            "ld": np.ascontiguousarray(ld), "nw": np.ascontiguousarray(nw)})
    res = run_bass_kernel_spmd(nc, in_maps, core_ids=list(range(NCORES)))
    return _unshard_stream([res.results[c]["ys"] for c in range(NCORES)])




SCALE1 = float(1.0 / np.sqrt(np.float32(2048.0)))
NSTRIP = 16
QT = 8
QG = 4


def emit_l1attn(C, W, xs_ap, ys_ap, cid_ap, kT_part, kT_all, v_part, v_all):
    nc, em, sb = C.nc, C.em, C.sb
    G = 512
    NG = T // G
    xg = C.sbuf("xg", [128, DC * G])
    xnb = C.sbuf("xnbg", [128, DC * G], BF16)
    kg = C.sbuf("kg", [128, DC * G], BF16)
    vgT = C.sbuf("vgT", [128, DC * G], BF16)
    vtm = C.sbuf("vtm", [128, 4 * D], BF16)
    qT = C.sbuf("qTf", [128, DC * T], BF16)
    accf = _Slice(xg[:])
    ab = C.sbuf("abt", [128, 2 * T])
    gmt = C.sbuf("gmt", [128, DC])
    bet = C.sbuf("bet", [128, DC])
    wbuf = [C.sbuf("wb0", [128, DC * 128], BF16), C.sbuf("wb1", [128, DC * 128], BF16)]
    iot512 = C.sbuf("iot512", [128, 512])
    zer512 = C.sbuf("zer512", [128, 512])
    cidr = C.sbuf("cidr", [128, 1])
    cid1 = C.sbuf("cid1", [1, 1])
    c1024 = C.sbuf("c1024", [128, 1])
    cval = C.sbuf("cval", [128, 1])
    mold = C.sbuf("mold", [128, QG])
    lrun = C.sbuf("lrun", [128, QG])
    rmx = C.sbuf("rmx", [128, 4])
    pst = C.sbuf("pstl", [128, 512], BF16)
    pT = C.sbuf("pTt", [128, 512], BF16)
    rs = C.sbuf("rst", [128, 1])
    xacc = C.sbuf("xacc", [128, G])
    xsc = C.sbuf("xsc", [128, G])
    PSA, PSB, P0, P1 = C.ps["A"], C.ps["B"], C.ps["p0"], C.ps["p1"]
    TB0 = C.ps["tb0"]
    kst = [kg[:].bitcast(BF16), vgT[:].bitcast(BF16)]
    kstk = ["kg", "vgT"]
    vst = [vtm[:].bitcast(BF16), xnb[:].bitcast(BF16)]
    vstk = ["vtm", "xnb"]

    emit_ln_stats_dram(C, xs_ap)
    mu, var, rstd = sb["mu"], sb["var"], sb["rstd"]
    a_b, b_b = ab[:, 0:T], ab[:, T:2 * T]
    em.op("v", lambda v: v.tensor_mul(a_b, rstd[:], rstd[:]), r=["rstd"], w=["ab:a"])
    em.op("v", lambda v: v.tensor_mul(a_b, a_b, var[:]), r=["ab:a", "var"], w=["ab:a"])
    em.op("s", lambda s: s.activation(a_b, a_b, AF.Ln, bias=sb["eps_pp"][:]),
          r=["ab:a", "eps_pp"], w=["ab:a"])
    em.op("s", lambda s: s.activation(a_b, a_b, AF.Exp, scale=-0.5),
          r=["ab:a"], w=["ab:a"])
    em.op("v", lambda v: v.tensor_mul(a_b, a_b, rstd[:]), r=["ab:a", "rstd"], w=["ab:a"])
    em.op("v", lambda v: v.scalar_tensor_tensor(b_b, mu[:], -1.0, a_b,
                                                ALU.mult, ALU.mult),
          r=["mu", "ab:a"], w=["ab:b"])
    em.dma("y", lambda y: y.dma_start(
        out=gmt[:], in_=W["gamma"].rearrange("(dc p) -> p dc", p=128),
        allow_slow_non_contiguous=True), w=["gmt"], slot="gmt")
    em.dma("y", lambda y: y.dma_start(
        out=bet[:], in_=W["beta"].rearrange("(dc p) -> p dc", p=128),
        allow_slow_non_contiguous=True), w=["bet"], slot="bet")
    em.dma("y", lambda y: y.dma_start(out=cid1[:], in_=cid_ap.rearrange("(a o) -> a o", a=1)),
           w=["cid1"], slot="cid1")
    emit_replicate_row(C, cid1[:], ["cid1"], cidr[:], ["cidr"], 1)
    em.op("v", lambda v: v.tensor_single_scalar(c1024[:], cidr[:], 1024.0, ALU.mult),
          r=["cidr"], w=["c1024"])
    em.op("g", lambda g: g.iota(iot512[:], [[-1, 512]], base=0, channel_multiplier=1,
                                allow_small_or_imprecise_dtypes=True), w=["iot512"])
    em.op("v", lambda v: v.memset(zer512[:], 0.0), w=["zer512"])

    qkvv = W["qkv"].rearrange("(dc p) f -> p dc f", p=128)
    opv = W["op"].rearrange("(dc p) f -> p dc f", p=128)

    for g in range(NG):
        t0 = g * G
        em.alias(["xg"], ["var"])
        xgv = xg[:].rearrange("p (dc t) -> p dc t", dc=DC)
        em.dma("y", lambda y, t0=t0: y.dma_start(
            out=xgv, in_=xs_ap.rearrange("(dc p) t -> p dc t", p=128)[:, :, t0:t0 + G]),
            w=["xg"], slot="xgld")
        abb = a_b[:, t0:t0 + G].unsqueeze(1).broadcast_to([128, DC, G])
        bbb = b_b[:, t0:t0 + G].unsqueeze(1).broadcast_to([128, DC, G])
        em.op("v", lambda v, m=abb: v.tensor_mul(xgv, xgv, m),
              r=["xg", "ab:a"], w=["xg"])
        em.op("v", lambda v, m=bbb: v.tensor_add(xgv, xgv, m),
              r=["xg", "ab:b"], w=["xg"])
        for dc in range(DC):
            em.op("v", lambda v, dc=dc: v.scalar_tensor_tensor(
                xnb[:, dc * G:(dc + 1) * G], xg[:, dc * G:(dc + 1) * G],
                gmt[:, dc:dc + 1], bet[:, dc:dc + 1].broadcast_to([128, G]),
                ALU.mult, ALU.add), r=["xg", "gmt", "bet"], w=["xnb"])
        for (dst, dstk, off, slotb) in ((qT, "qTf", 0, "q"), (kg, "kg", D, "k"),
                                        (vgT, "vgT", 2 * D, "v")):
            for oc in range(DC):
                wb = wbuf[oc % 2]
                wbk = f"wb{oc % 2}"
                em.dma("g", lambda gg, oc=oc, wb=wb, off=off: gg.dma_start(
                    out=wb[:].rearrange("p (dc f) -> p dc f", dc=DC),
                    in_=qkvv[:, :, off + oc * 128:off + (oc + 1) * 128]),
                    w=[wbk], slot=slotb + str(oc % 2))
                ps = [PSA, PSB][oc % 2]
                psk = ["psA", "psB"][oc % 2]
                for dinc in range(DC):
                    em.op("t", lambda t, oc=oc, dinc=dinc, wb=wb, ps=ps: t.matmul(
                        ps[:, 0:G], wb[:, dinc * 128:(dinc + 1) * 128],
                        xnb[:, dinc * G:(dinc + 1) * G],
                        start=(dinc == 0), stop=(dinc == DC - 1)),
                        r=[wbk, "xnb"], w=[psk])
                if dst is qT:
                    em.op("s", lambda s, oc=oc, ps=ps, t0=t0: s.copy(
                        qT[:, oc * T + t0:oc * T + t0 + G], ps[:, 0:G]),
                        r=[psk], w=["qTf"])
                else:
                    em.op("s", lambda s, oc=oc, ps=ps, dst=dst: s.copy(
                        dst[:, oc * G:(oc + 1) * G], ps[:, 0:G]),
                        r=[psk], w=[dstk])
        em.dma("g", lambda gg, t0=t0: gg.dma_start(
            out=kT_part.rearrange("(dc p) t -> p dc t", p=128)[:, :, t0:t0 + G],
            in_=kg[:].rearrange("p (dc t) -> p dc t", dc=DC)),
            r=["kg"], w=["ktp"], slot="ktp")
        for c4 in range(4):
            for oc in range(DC):
                em.op("t", lambda t, c4=c4, oc=oc: t.transpose(
                    TB0[0:128, 0:128],
                    vgT[:, oc * G + c4 * 128:oc * G + (c4 + 1) * 128], sb["idb"][:]),
                    r=["vgT", "idb"], w=["pstb0"])
                em.op("s", lambda s, c4=c4, oc=oc: s.copy(
                    vtm[:, c4 * D + oc * 128:c4 * D + (oc + 1) * 128],
                    TB0[0:128, 0:128]), r=["pstb0"], w=["vtm"])
        em.dma("g", lambda gg, g=g: gg.dma_start(
            out=v_part[g * G:(g + 1) * G, :].rearrange("(c4 p) f -> p c4 f", p=128),
            in_=vtm[:].rearrange("p (c4 f) -> p c4 f", c4=4)),
            r=["vtm"], w=["vtp"], slot="vtp")
    em.cc(lambda g: g.collective_compute(
        "AllGather", ALU.bypass, replica_groups=[list(range(NCORES))],
        ins=[kT_part.opt()], outs=[kT_all.opt()]), r=["ktp"], w=["ktall"], slot="ccK")
    em.cc(lambda g: g.collective_compute(
        "AllGather", ALU.bypass, replica_groups=[list(range(NCORES))],
        ins=[v_part.opt()], outs=[v_all.opt()]), r=["vtp"], w=["vall"], slot="ccV")

    em.alias(kstk + vstk, ["kg", "vgT", "vtm", "xnb"])
    em.alias(["accf"], ["xg"])
    for half in range(2):
        em.op("v", lambda v: v.memset(accf[:], 0.0), w=["accf"])
        em.op("v", lambda v: v.memset(mold[:], -3.0e38), w=["mold"])
        em.op("v", lambda v: v.memset(lrun[:], 0.0), w=["lrun"])
        for s_ in range(NSTRIP):
            kb, kbk = kst[s_ % 2], kstk[s_ % 2]
            vb, vbk = vst[s_ % 2], vstk[s_ % 2]
            slab, col = s_ // 2, (s_ % 2) * G
            em.dma("y", lambda y, slab=slab, col=col, kb=kb: y.dma_start(
                out=kb.rearrange("p (dc t) -> p dc t", dc=DC),
                in_=kT_all[slab * D:(slab + 1) * D, :].rearrange(
                    "(dc p) t -> p dc t", p=128)[:, :, col:col + G]),
                r=["ktall"], w=[kbk], slot="k" + str(s_ % 2) + "ld")
            em.dma("y", lambda y, s_=s_, vb=vb: y.dma_start(
                out=vb.rearrange("p (c4 f) -> p c4 f", c4=4),
                in_=v_all[s_ * G:(s_ + 1) * G, :].rearrange("(c4 p) f -> p c4 f", p=128)),
                r=["vall"], w=[vbk], slot="v" + str(s_ % 2) + "ld")
            for qi in range(QG):
                qt = half * QG + qi
                for dc in range(DC):
                    em.op("t", lambda t, dc=dc, qt=qt, kb=kb: t.matmul(
                        P0[:, 0:G], qT[:, dc * T + qt * 128:dc * T + (qt + 1) * 128],
                        kb[:, dc * G:(dc + 1) * G],
                        start=(dc == 0), stop=(dc == DC - 1)),
                        r=["qTf", kbk], w=["psp0"])
                em.op("v", lambda v, qt=qt, s_=s_: v.tensor_single_scalar(
                    cval[:], c1024[:], float(128 * qt - 512 * s_), ALU.add),
                    r=["c1024"], w=["cval"])
                em.op("v", lambda v: v.scalar_tensor_tensor(
                    pst[:].bitcast(F32)[:, 0:256].rearrange("p (a b) -> p a b", a=1).squeeze(1) if False else sb["qqmask"][:],
                    iot512[:], cval[:], zer512[:], ALU.add, ALU.is_lt),
                    r=["iot512", "cval", "zer512"], w=["qqmask"])
                em.op("v", lambda v: v.scalar_tensor_tensor(
                    P0[:, 0:G], sb["qqmask"][:], -1.0e30, P0[:, 0:G],
                    ALU.mult, ALU.add), r=["qqmask", "psp0"], w=["psp0"])
                em.op("v", lambda v: v.reduce_max(rmx[:, 0:1], P0[:, 0:G], axis=AX.X),
                      r=["psp0"], w=["rmx"])
                em.op("v", lambda v, qi=qi: v.tensor_max(
                    rmx[:, 1:2], rmx[:, 0:1], mold[:, qi:qi + 1]),
                      r=["rmx", "mold"], w=["rmx"])
                em.op("v", lambda v, qi=qi: v.tensor_sub(
                    rmx[:, 2:3], mold[:, qi:qi + 1], rmx[:, 1:2]),
                      r=["rmx", "mold"], w=["rmx"])
                em.op("s", lambda s: s.activation(rmx[:, 2:3], rmx[:, 2:3],
                                                  AF.Exp, scale=SCALE1),
                      r=["rmx"], w=["rmx"])
                em.op("v", lambda v: v.tensor_single_scalar(
                    rmx[:, 3:4], rmx[:, 1:2], -SCALE1, ALU.mult),
                    r=["rmx"], w=["rmx"])
                em.op("v", lambda v, qi=qi: v.tensor_copy(
                    mold[:, qi:qi + 1], rmx[:, 1:2]), r=["rmx"], w=["mold"])
                em.op("s", lambda s: s.activation(
                    pst[:], P0[:, 0:G], AF.Exp, bias=rmx[:, 3:4], scale=SCALE1),
                    r=["psp0", "rmx"], w=["pstl"])
                em.op("v", lambda v: v.reduce_sum(rs[:], pst[:], axis=AX.X),
                      r=["pstl"], w=["rst"])
                em.op("v", lambda v, qi=qi: v.scalar_tensor_tensor(
                    lrun[:, qi:qi + 1], lrun[:, qi:qi + 1], rmx[:, 2:3], rs[:],
                    ALU.mult, ALU.add), r=["lrun", "rmx", "rst"], w=["lrun"])
                for jt in range(4):
                    em.op("t", lambda t, jt=jt: t.transpose(
                        TB0[0:128, 0:128], pst[:, jt * 128:(jt + 1) * 128],
                        sb["idb"][:]), r=["pstl", "idb"], w=["pstb0"])
                    em.op("s", lambda s, jt=jt: s.copy(
                        pT[:, jt * 128:(jt + 1) * 128], TB0[0:128, 0:128]),
                        r=["pstb0"], w=["pTt"])
                for fs in range(4):
                    for jt in range(4):
                        em.op("t", lambda t, fs=fs, jt=jt, vb=vb: t.matmul(
                            P1[:, 0:G], pT[:, jt * 128:(jt + 1) * 128],
                            vb[:, jt * D + fs * G:jt * D + (fs + 1) * G],
                            start=(jt == 0), stop=(jt == 3)),
                            r=["pTt", vbk], w=["psp1"])
                    em.op("v", lambda v, fs=fs, qi=qi: v.scalar_tensor_tensor(
                        accf[:, qi * D + fs * G:qi * D + (fs + 1) * G],
                        accf[:, qi * D + fs * G:qi * D + (fs + 1) * G],
                        rmx[:, 2:3], P1[:, 0:G], ALU.mult, ALU.add),
                        r=["accf", "rmx", "psp1"], w=["accf"])
        em.op("v", lambda v: v.reciprocal(rmx[:, 0:1].broadcast_to([128, 1]) if False else sb["linv"][:], lrun[:]),
              r=["lrun"], w=["linv"])
        for qi in range(QG):
            qt = half * QG + qi
            em.op("v", lambda v, qi=qi: v.tensor_scalar(
                accf[:, qi * D:(qi + 1) * D], accf[:, qi * D:(qi + 1) * D],
                sb["linv"][:, qi:qi + 1], None, ALU.mult),
                r=["accf", "linv"], w=["accf"])
            if half == 0 and qi == 0:
                em.alias(["accT"], ["qTf"])
            for dc in range(DC):
                em.op("t", lambda t, qi=qi, dc=dc: t.transpose(
                    P0[:, 0:128], accf[:, qi * D + dc * 128:qi * D + (dc + 1) * 128],
                    sb["idf"][:]), r=["accf", "idf"], w=["psp0"])
                em.op("s", lambda s, qt=qt, dc=dc: s.copy(
                    qT[:].bitcast(BF16)[:, 0:DC * T].rearrange(
                        "p (dc t) -> p dc t", dc=DC)[:, dc, qt * 128:(qt + 1) * 128],
                    P0[:, 0:128]), r=["psp0"], w=["accT"])
    accT = qT[:].bitcast(BF16)[:, 0:DC * T]
    for g in range(NG):
        t0 = g * G
        for oc in range(DC):
            wb = wbuf[oc % 2]
            wbk = f"wb{oc % 2}"
            em.dma("g", lambda gg, oc=oc, wb=wb: gg.dma_start(
                out=wb[:].rearrange("p (dc f) -> p dc f", dc=DC),
                in_=opv[:, :, oc * 128:(oc + 1) * 128]),
                w=[wbk], slot="op" + str(oc % 2))
            ps = [PSA, PSB][oc % 2]
            psk = ["psA", "psB"][oc % 2]
            for dc in range(DC):
                em.op("t", lambda t, oc=oc, dc=dc, wb=wb, ps=ps, t0=t0: t.matmul(
                    ps[:, 0:G], wb[:, dc * 128:(dc + 1) * 128],
                    accT.rearrange("p (dc t) -> p dc t", dc=DC)[:, dc, t0:t0 + G],
                    start=(dc == 0), stop=(dc == DC - 1)),
                    r=[wbk, "accT"], w=[psk])
            em.dma("y", lambda y, oc=oc, t0=t0: y.dma_start(
                out=xsc[:], in_=xs_ap[oc * 128:(oc + 1) * 128, t0:t0 + G]),
                w=["xsc"], slot="xsc")
            em.op("v", lambda v, t0=t0: v.tensor_sub(
                xacc[:], xsc[:], mu[:, t0:t0 + G]), r=["xsc", "mu"], w=["xacc"])
            em.op("v", lambda v, t0=t0: v.tensor_mul(
                xacc[:], xacc[:], rstd[:, t0:t0 + G]), r=["xacc", "rstd"], w=["xacc"])
            em.op("v", lambda v: v.tensor_add(xacc[:], xacc[:], xsc[:]),
                  r=["xacc", "xsc"], w=["xacc"])
            em.op("v", lambda v, ps=ps: v.tensor_add(xacc[:], xacc[:], ps[:, 0:G]),
                  r=["xacc", psk], w=["xacc"])
            em.dma("g", lambda gg, oc=oc, t0=t0: gg.dma_start(
                out=ys_ap[oc * 128:(oc + 1) * 128, t0:t0 + G], in_=xacc[:]),
                r=["xacc"], w=[], slot="ysst")
    sem = em.dma_sems["ysst"]
    em.op("g", lambda g: g.engine_nop(), after=[(sem, em.cnt[sem])])


def build_l1attn_program():
    nc = bass.Bass("TRN2", target_bir_lowering=False)
    em = EM(nc)
    C = Ctx(nc, em)
    xs = nc.dram_tensor("xs", [D, T], F32, kind="ExternalInput").ap()
    cid = nc.dram_tensor("cid", [1], F32, kind="ExternalInput").ap()
    W = {
        "qkv": nc.dram_tensor("qkv", [D, 3 * D], F32, kind="ExternalInput").ap(),
        "op": nc.dram_tensor("op", [D, D], F32, kind="ExternalInput").ap(),
        "gamma": nc.dram_tensor("gamma", [D], F32, kind="ExternalInput").ap(),
        "beta": nc.dram_tensor("beta", [D], F32, kind="ExternalInput").ap(),
    }
    ys = nc.dram_tensor("ys", [D, T], F32, kind="ExternalOutput").ap()
    kT_part = nc.dram_tensor("ktp", [D, T], BF16).ap()
    kT_all = nc.dram_tensor("kta", [NCORES * D, T], BF16).ap()
    v_part = nc.dram_tensor("vtp", [T, D], BF16).ap()
    v_all = nc.dram_tensor("vta", [NCORES * T, D], BF16).ap()
    setup_common(C)
    C.sbuf("qqmask", [128, 512])
    C.sbuf("linv", [128, QG])
    emit_l1attn(C, W, xs, ys, cid, kT_part, kT_all, v_part, v_all)
    em.finalize()
    return nc


def run_l1attn(x, qkv, op, gamma, beta):
    nc = build_l1attn_program()
    shards = _shard_stream(x)
    in_maps = []
    for c in range(NCORES):
        in_maps.append({
            "xs": shards[c], "cid": np.array([c], np.float32),
            "qkv": np.ascontiguousarray(qkv), "op": np.ascontiguousarray(op),
            "gamma": np.ascontiguousarray(gamma), "beta": np.ascontiguousarray(beta)})
    res = run_bass_kernel_spmd(nc, in_maps, core_ids=list(range(NCORES)))
    return _unshard_stream([res.results[c]["ys"] for c in range(NCORES)])





NF4_BF = [float(np.float32(ml_dtypes.bfloat16(v))) for v in
          [-1.0, -0.6962, -0.5251, -0.3949, -0.2844, -0.1848, -0.0911, 0.0,
           0.0796, 0.1609, 0.2461, 0.3379, 0.4407, 0.5626, 0.723, 1.0]]
NF4_MID = [(NF4_BF[i] + NF4_BF[i + 1]) / 2.0 for i in range(15)]
NF4_DELTA = [NF4_BF[i + 1] - NF4_BF[i] for i in range(15)]
DFF = 5632
FC = DFF // 128
SH1 = DFF // NCORES
SH3 = D // NCORES
NBLK = 180224
PEL = SH1 * D // 128
NC64 = PEL // 64
NSB = 88


def emit_l1ffn_prep(C, wi, wshd, wqd, wtp, s1b, deqb, mrb_in, mrb_out, SHR, C_):
    nc, em, sb = C.nc, C.em, C.sb
    wsh, scb, val = sb["wsh"], sb["scb"], sb["valt"]
    s1, s1r, deql = sb["s1t"], sb["s1r"], sb["deql"]
    sbt, sbb, sbc = sb["sbt"], sb["sbb"], sb["sbc"]
    qf2, mpp = sb["qf2"], sb["mpp"]
    rowsum, msum, m11 = sb["rowsum"], sb["msum"], sb["m11"]
    stg, ttin = sb["stg"], sb["ttin"]
    P0, TB0 = C.ps["p0"], C.ps["tb0"]
    w = f"w{wi}:"

    em.dma("y", lambda y: y.dma_start(
        out=wsh[:, 0:PEL], in_=wshd.flatten().rearrange("(p i) -> p i", p=128)),
        w=["wsh"], slot="wshld")
    em.op("v", lambda v: v.reduce_max(
        s1[:, 0:NC64], wsh[:, 0:PEL].rearrange("p (b e) -> p b e", e=64),
        axis=AX.X, apply_absolute_value=True), r=["wsh"], w=["s1"])
    em.op("v", lambda v: v.reduce_sum(rowsum[:, 0:1], s1[:, 0:NC64], axis=AX.X),
          r=["s1"], w=["rs"])
    em.op("v", lambda v: v.memset(msum[:], 0.0), w=["ms"])
    em.op("t", lambda t: t.matmul(P0[0:1, 0:1], sb["ones128"][:, 0:1],
                                  rowsum[:, 0:1], start=True, stop=True),
          r=["ones128", "rs"], w=["psp0"])
    em.op("v", lambda v: v.tensor_copy(msum[:, 0:1], P0[0:1, 0:1]),
          r=["psp0"], w=["ms"])
    em.dma("g", lambda g: g.dma_start(out=mrb_in, in_=msum[:]),
           r=["ms"], w=[w + "mrb"], slot="mrb")
    em.cc(lambda g: g.collective_compute(
        "AllReduce", ALU.add, replica_groups=[list(range(NCORES))],
        ins=[mrb_in.opt()], outs=[mrb_out.opt()]),
        r=[w + "mrb"], w=[w + "mro"], slot="ccM")
    em.dma("y", lambda y: y.dma_start(out=msum[:], in_=mrb_out),
           r=[w + "mro"], w=["ms"], slot="mrb2")
    em.op("v", lambda v: v.tensor_single_scalar(
        m11[:].bitcast(BF16)[:, 0:1], msum[:, 0:1], 1.0 / NBLK, ALU.mult),
        r=["ms"], w=["m11"])
    em.op("v", lambda v: v.tensor_copy(m11[:, 1:2], m11[:].bitcast(BF16)[:, 0:1]),
          r=["m11"], w=["m11"])
    emit_replicate_row(C, m11[:, 1:2], ["m11"], mpp[:], ["mpp"], 1)
    em.dma("g", lambda g: g.dma_start(out=s1b[0:128 * NC64], in_=s1[:, 0:NC64]),
           r=["s1"], w=[w + "s1b"], slot="s1b")
    em.dma("y", lambda y: y.dma_start(
        out=sbt[0:NSB, :], in_=s1b.rearrange("(s j) -> s j", j=256)[0:NSB, :]),
        r=[w + "s1b"], w=["sbt"], slot="sbt")
    em.op("v", lambda v: v.tensor_scalar(sbb[0:NSB, :], sbt[0:NSB, :],
                                         mpp[0:NSB, 0:1], None, ALU.subtract),
          r=["sbt", "mpp"], w=["sbb"])
    em.op("v", lambda v: v.reduce_max(qf2[0:NSB, 0:1], sbb[0:NSB, :], axis=AX.X,
                                      apply_absolute_value=True),
          r=["sbb"], w=["qf2"])
    em.op("v", lambda v: v.reciprocal(qf2[0:NSB, 1:2], qf2[0:NSB, 0:1]),
          r=["qf2"], w=["qf2"])
    em.op("v", lambda v: v.tensor_single_scalar(
        qf2[:].bitcast(BF16)[0:NSB, 4:5], qf2[0:NSB, 1:2], 128.0, ALU.mult),
        r=["qf2"], w=["qf2"])
    em.op("v", lambda v: v.tensor_copy(qf2[0:NSB, 2:3],
                                       qf2[:].bitcast(BF16)[0:NSB, 4:5]),
          r=["qf2"], w=["qf2"])
    em.op("v", lambda v: v.reciprocal(qf2[0:NSB, 3:4], qf2[0:NSB, 2:3]),
          r=["qf2"], w=["qf2"])
    em.op("v", lambda v: v.tensor_scalar(sbc[0:NSB, :], sbb[0:NSB, :],
                                         qf2[0:NSB, 2:3], None, ALU.mult),
          r=["sbb", "qf2"], w=["sbc"])
    em.op("v", lambda v: v.tensor_scalar(sbt[0:NSB, :], sbc[0:NSB, :],
                                         MAGIC, MAGIC, ALU.add, ALU.subtract),
          r=["sbc"], w=["sbt"])
    em.op("v", lambda v: v.tensor_scalar(sbt[0:NSB, :], sbt[0:NSB, :],
                                         127.0, -128.0, ALU.min, ALU.max),
          r=["sbt"], w=["sbt"])
    em.op("v", lambda v: v.tensor_scalar(sbc[0:NSB, :], sbt[0:NSB, :],
                                         qf2[0:NSB, 3:4], None, ALU.mult),
          r=["sbt", "qf2"], w=["sbc"])
    em.op("v", lambda v: v.tensor_scalar(sbb[0:NSB, :], sbc[0:NSB, :],
                                         mpp[0:NSB, 0:1], None, ALU.add),
          r=["sbc", "mpp"], w=["sbb"])
    em.dma("g", lambda g: g.dma_start(
        out=deqb.rearrange("(s j) -> s j", j=256)[0:NSB, :], in_=sbb[0:NSB, :]),
        r=["sbb"], w=[w + "deqb"], slot="deqb")
    em.dma("y", lambda y: y.dma_start(
        out=deql[:, 0:NC64], in_=deqb.rearrange("(p i) -> p i", p=128)),
        r=[w + "deqb"], w=["deql"], slot="deql")
    em.op("v", lambda v: v.reciprocal(s1r[:, 0:NC64], s1[:, 0:NC64]),
          r=["s1"], w=["s1r"])
    em.op("v", lambda v: v.tensor_mul(
        scb[:, 0:PEL].rearrange("p (b e) -> p b e", e=64),
        wsh[:, 0:PEL].rearrange("p (b e) -> p b e", e=64),
        s1r[:, 0:NC64].unsqueeze(2).broadcast_to([128, NC64, 64])),
        r=["wsh", "s1r"], w=["scb"])
    em.op("v", lambda v: v.memset(val[:, 0:PEL], NF4_BF[0]), w=["val"])
    for k in range(15):
        em.op("g", lambda g, k=k: g.tensor_scalar(
            sb["indt"][:, 0:PEL], scb[:, 0:PEL], NF4_MID[k], None, ALU.is_gt),
            r=["scb"], w=["ind"])
        em.op("v", lambda v, k=k: v.scalar_tensor_tensor(
            val[:, 0:PEL], sb["indt"][:, 0:PEL], NF4_DELTA[k], val[:, 0:PEL],
            ALU.mult, ALU.add), r=["ind", "val"], w=["val"])
    em.op("v", lambda v: v.tensor_mul(
        wsh[:, 0:PEL].rearrange("p (b e) -> p b e", e=64),
        val[:, 0:PEL].rearrange("p (b e) -> p b e", e=64),
        deql[:, 0:NC64].unsqueeze(2).broadcast_to([128, NC64, 64])),
        r=["val", "deql", "wsh"], w=["wsh"])
    em.dma("g", lambda g: g.dma_start(
        out=wqd.flatten().rearrange("(p i) -> p i", p=128), in_=wsh[:, 0:PEL]),
        r=["wsh"], w=[w + "wqd"], slot="wqd")
    nrt = (SHR + 127) // 128
    for ct in range(C_ // 128):
        for rt in range(nrt):
            rows = min(128, SHR - rt * 128)
            em.dma("y", lambda y, ct=ct, rt=rt, rows=rows: y.dma_start(
                out=ttin[0:rows, :],
                in_=wqd[rt * 128:rt * 128 + rows, ct * 128:(ct + 1) * 128]),
                r=[w + "wqd"], w=["ttin"], slot="ttin" + str(rt % 2))
            em.op("t", lambda t, rows=rows: t.transpose(
                TB0[0:128, 0:rows], ttin[0:rows, :], sb["idb"][0:rows, 0:rows]),
                r=["ttin", "idb"], w=["pstb0"])
            em.op("s", lambda s, rt=rt, rows=rows: s.copy(
                stg[:, rt * 128:rt * 128 + rows], TB0[0:128, 0:rows]),
                r=["pstb0"], w=["stg"])
        em.dma("g", lambda g, ct=ct: g.dma_start(
            out=wtp[ct * 128:(ct + 1) * 128, :], in_=stg[:, 0:SHR]),
            r=["stg"], w=[w + "wtp"], slot="wtpst")
    return w + "wtp"


def emit_l1ffn_body(C, w1a, w2a, w3a, xs_ap, ys_ap, first_store=True):
    nc, em, sb = C.nc, C.em, C.sb
    G = 512
    NG = T // G
    em.alias(["xg"], ["wsh"])
    em.alias(["hbuf"], ["scb", "ind"])
    em.alias(["w3b0", "w3b1"], ["val"])
    xg, xnb = sb["xg"], sb["xnbg"]
    hbuf = sb["hbuf"]
    hsc = sb["hsc"]
    w12b = [sb["w12b0"], sb["w12b1"], sb["w12b2"], sb["w12b3"]]
    w3b = [sb["w3b0"], sb["w3b1"]]
    xacc, xsc = sb["xacc"], sb["xsc"]
    PSA, PSB = C.ps["A"], C.ps["B"]
    emit_ln_stats_dram(C, xs_ap)
    mu, rstd = sb["mu"], sb["rstd"]

    def seg_loads(dst, dstk, slot, wta, fc, SHR, rkey):
        f0, f1 = fc * 128, (fc + 1) * 128
        segs = []
        f = f0
        while f < f1:
            slab = min(f // SH1, 7)
            lo = SH1 * slab
            hi = SH1 * (slab + 1) if slab < 7 else DFF
            end = min(f1, hi)
            segs.append((slab, f - lo, f - f0, end - f))
            f = end
        for (slab, loc, off, wd) in segs:
            em.dma("y", lambda y, slab=slab, loc=loc, off=off, wd=wd, wta=wta: y.dma_start(
                out=dst[:].rearrange("p (dc f) -> p dc f", dc=DC)[:, :, off:off + wd],
                in_=wta[slab * D:(slab + 1) * D, :].rearrange(
                    "(dc p) s -> p dc s", p=128)[:, :, loc:loc + wd]),
                r=[rkey], w=[dstk], slot=slot)

    for g in range(NG):
        t0 = g * G
        em.alias(["xg"], ["var"])
        xgv = xg[:].rearrange("p (dc t) -> p dc t", dc=DC)
        em.dma("y", lambda y, t0=t0: y.dma_start(
            out=xgv, in_=xs_ap.rearrange("(dc p) t -> p dc t", p=128)[:, :, t0:t0 + G]),
            w=["xg"], slot="xgld")
        mub = mu[:, t0:t0 + G].unsqueeze(1).broadcast_to([128, DC, G])
        rsb = rstd[:, t0:t0 + G].unsqueeze(1).broadcast_to([128, DC, G])
        em.op("v", lambda v, m=mub: v.tensor_sub(xgv, xgv, m), r=["xg", "mu"], w=["xg"])
        em.op("v", lambda v, m=rsb: v.tensor_mul(xgv, xgv, m), r=["xg", "rstd"], w=["xg"])
        em.op("v", lambda v: v.tensor_copy(xnb[:], xg[:]), r=["xg"], w=["xnb"])
        for fc in range(FC):
            b1, b1k = w12b[fc % 2], f"w12b{fc % 2}"
            b2, b2k = w12b[2 + fc % 2], f"w12b{2 + fc % 2}"
            seg_loads(b1, b1k, "w1l" + str(fc % 2), w1a, fc, SH1, "w1:wta")
            seg_loads(b2, b2k, "w2l" + str(fc % 2), w2a, fc, SH1, "w2:wta")
            ps1, ps1k = [PSA, PSB][fc % 2], ["psA", "psB"][fc % 2]
            for dc in range(DC):
                em.op("t", lambda t, fc=fc, dc=dc, b1=b1, ps=ps1: t.matmul(
                    ps[:, 0:G], b1[:, dc * 128:(dc + 1) * 128],
                    xnb[:, dc * G:(dc + 1) * G],
                    start=(dc == 0), stop=(dc == DC - 1)),
                    r=[b1k, "xnb"], w=[ps1k])
            em.op("s", lambda s, fc=fc, ps=ps1: s.activation(
                hsc[:, 0:G], ps[:, 0:G], AF.Silu), r=[ps1k], w=["hsc"])
            for dc in range(DC):
                em.op("t", lambda t, fc=fc, dc=dc, b2=b2, ps=ps1: t.matmul(
                    ps[:, 0:G], b2[:, dc * 128:(dc + 1) * 128],
                    xnb[:, dc * G:(dc + 1) * G],
                    start=(dc == 0), stop=(dc == DC - 1)),
                    r=[b2k, "xnb"], w=[ps1k])
            em.op("v", lambda v, fc=fc, ps=ps1: v.tensor_mul(
                hbuf[:, fc * G:(fc + 1) * G], hsc[:, 0:G], ps[:, 0:G]),
                r=["hsc", ps1k], w=["hbuf"])
        for oc in range(DC):
            b3, b3k = w3b[oc % 2], f"w3b{oc % 2}"
            slab, loc = oc // 2, (oc % 2) * 128
            em.dma("y", lambda y, slab=slab, loc=loc, b3=b3, w3a=w3a: y.dma_start(
                out=b3[:].rearrange("p (fc f) -> p fc f", fc=FC),
                in_=w3a[slab * DFF:(slab + 1) * DFF, :].rearrange(
                    "(fc p) s -> p fc s", p=128)[:, :, loc:loc + 128]),
                r=["w3:wta"], w=[b3k], slot="w3l" + str(oc % 2))
            ps1, ps1k = [PSA, PSB][oc % 2], ["psA", "psB"][oc % 2]
            for fc in range(FC):
                em.op("t", lambda t, oc=oc, fc=fc, b3=b3, ps=ps1: t.matmul(
                    ps[:, 0:G], b3[:, fc * 128:(fc + 1) * 128],
                    hbuf[:, fc * G:(fc + 1) * G],
                    start=(fc == 0), stop=(fc == FC - 1)),
                    r=[b3k, "hbuf"], w=[ps1k])
            em.dma("y", lambda y, oc=oc, t0=t0: y.dma_start(
                out=xsc[:], in_=xs_ap[oc * 128:(oc + 1) * 128, t0:t0 + G]),
                w=["xsc"], slot="xsc")
            em.op("v", lambda v, ps=ps1: v.scalar_tensor_tensor(
                xacc[:], ps[:, 0:G], 0.5, xsc[:], ALU.mult, ALU.add),
                r=[ps1k, "xsc"], w=["xacc"])
            em.dma("g", lambda gg, oc=oc, t0=t0: gg.dma_start(
                out=ys_ap[oc * 128:(oc + 1) * 128, t0:t0 + G], in_=xacc[:]),
                r=["xacc"], w=[], slot="ysst")
    sem = em.dma_sems["ysst"]
    em.op("g", lambda g: g.engine_nop(), after=[(sem, em.cnt[sem])])


def alloc_l1ffn_small(C):
    C.sbuf("s1t", [128, NC64])
    C.sbuf("s1r", [128, NC64])
    C.sbuf("deql", [128, NC64], BF16)
    C.sbuf("sbt", [128, 256])
    C.sbuf("sbb", [128, 256], BF16)
    C.sbuf("sbc", [128, 256], BF16)
    C.sbuf("qf2", [128, 4])
    C.sbuf("mpp", [128, 1])
    C.sbuf("rowsum", [128, 2])
    C.sbuf("msum", [1, 128])
    C.sbuf("m11", [1, 2])
    C.sbuf("stg", [128, 768], BF16)
    C.sbuf("ttin", [128, 128], BF16)
    C.sbuf("cidr", [128, 1])
    C.sbuf("cid1", [1, 1])


def alloc_l1ffn_body(C):
    C.sbuf("xg", [128, DC * 512])
    C.sbuf("xnbg", [128, DC * 512], BF16)
    C.sbuf("hbuf", [128, FC * 512], BF16)
    C.sbuf("valt", [128, PEL], BF16)
    C.sbuf("hsc", [128, 512])
    for i in range(4):
        C.sbuf(f"w12b{i}", [128, DC * 128], BF16)
    C.sbuf("xacc", [128, 512])
    C.sbuf("xsc", [128, 512])
    sb = C.sb
    sb["wsh"] = None


def wire_l1ffn_aliases(C):
    sb = C.sb
    sb["wsh"] = _Slice(sb["xg"][:].bitcast(BF16)[:, 0:PEL])
    sb["scb"] = _Slice(sb["hbuf"][:, 0:PEL])
    sb["indt"] = _Slice(sb["hbuf"][:, PEL:2 * PEL])
    sb["w3b0"] = _Slice(sb["valt"][:, 0:FC * 128])
    sb["w3b1"] = _Slice(sb["valt"][:, FC * 128:2 * FC * 128])


def build_l1ffn_program():
    nc = bass.Bass("TRN2", target_bir_lowering=False)
    em = EM(nc)
    C = Ctx(nc, em)
    xs = nc.dram_tensor("xs", [D, T], F32, kind="ExternalInput").ap()
    w1s = nc.dram_tensor("w1s", [SH1, D], BF16, kind="ExternalInput").ap()
    w2s = nc.dram_tensor("w2s", [SH1, D], BF16, kind="ExternalInput").ap()
    w3s = nc.dram_tensor("w3s", [SH3, DFF], BF16, kind="ExternalInput").ap()
    ys = nc.dram_tensor("ys", [D, T], F32, kind="ExternalOutput").ap()
    wq1 = nc.dram_tensor("wq1", [SH1, D], BF16).ap()
    wq2 = nc.dram_tensor("wq2", [SH1, D], BF16).ap()
    wq3 = nc.dram_tensor("wq3", [SH3, DFF], BF16).ap()
    wtp1 = nc.dram_tensor("wtp1", [D, SH1], BF16).ap()
    wtp2 = nc.dram_tensor("wtp2", [D, SH1], BF16).ap()
    wtp3 = nc.dram_tensor("wtp3", [DFF, SH3], BF16).ap()
    wta1 = nc.dram_tensor("wta1", [NCORES * D, SH1], BF16).ap()
    wta2 = nc.dram_tensor("wta2", [NCORES * D, SH1], BF16).ap()
    wta3 = nc.dram_tensor("wta3", [NCORES * DFF, SH3], BF16).ap()
    s1b = nc.dram_tensor("s1b", [128 * NC64], F32).ap()
    deqb = nc.dram_tensor("deqb", [128 * NC64], BF16).ap()
    mrb_in = nc.dram_tensor("mrbi", [1, 128], F32).ap()
    mrb_out = nc.dram_tensor("mrbo", [1, 128], F32).ap()
    setup_common(C)
    alloc_l1ffn_small(C)
    alloc_l1ffn_body(C)
    wire_l1ffn_aliases(C)
    for (wi, wshd, wqd, wtp, wta, SHR, C_) in (
            (1, w1s, wq1, wtp1, wta1, SH1, D),
            (2, w2s, wq2, wtp2, wta2, SH1, D),
            (3, w3s, wq3, wtp3, wta3, SH3, DFF)):
        emit_l1ffn_prep(C, wi, wshd, wqd, wtp, s1b, deqb, mrb_in, mrb_out, SHR, C_)
        C.em.cc(lambda g, wtp=wtp, wta=wta: g.collective_compute(
            "AllGather", ALU.bypass, replica_groups=[list(range(NCORES))],
            ins=[wtp.opt()], outs=[wta.opt()]),
            r=[f"w{wi}:wtp"], w=[f"w{wi}:wta"], slot="ccW")
    emit_l1ffn_body(C, wta1, wta2, wta3, xs, ys)
    em.finalize()
    return nc


def run_l1ffn(x, w1, w2, w3):
    nc = build_l1ffn_program()
    shards = _shard_stream(x)
    in_maps = []
    for c in range(NCORES):
        in_maps.append({
            "xs": shards[c],
            "w1s": np.ascontiguousarray(w1[c * SH1:(c + 1) * SH1]),
            "w2s": np.ascontiguousarray(w2[c * SH1:(c + 1) * SH1]),
            "w3s": np.ascontiguousarray(w3[c * SH3:(c + 1) * SH3])})
    res = run_bass_kernel_spmd(nc, in_maps, core_ids=list(range(NCORES)))
    return _unshard_stream([res.results[c]["ys"] for c in range(NCORES)])



def build_l0ffn_program():
    nc = bass.Bass("TRN2", target_bir_lowering=False)
    em = EM(nc)
    C = Ctx(nc, em)
    xs = nc.dram_tensor("xs", [D, T], F32, kind="ExternalInput").ap()
    ker = nc.dram_tensor("k0", [D, D], F32, kind="ExternalInput").ap()
    bias = nc.dram_tensor("b0", [D], F32, kind="ExternalInput").ap()
    ys = nc.dram_tensor("ys", [D, T], F32, kind="ExternalOutput").ap()
    setup_common(C)
    alloc_l0ffn(C)
    emit_l0ffn_prep(C, ker, bias)
    emit_stream_load(C, xs)
    emit_ln_stats(C)
    emit_l0ffn_body(C)
    emit_stream_store(C, ys)
    em.finalize()
    return nc


def _shard_stream(x):
    xs = np.ascontiguousarray(np.asarray(x, np.float32).reshape(8192, 2048))
    return [np.ascontiguousarray(xs[c * T:(c + 1) * T].T) for c in range(NCORES)]


def _unshard_stream(parts):
    out = np.empty((8192, 2048), np.float32)
    for c in range(NCORES):
        out[c * T:(c + 1) * T] = parts[c].T
    return out.reshape(1, 8192, 2048)


def run_l0ffn(x, kernel_w, bias_w):
    nc = build_l0ffn_program()
    shards = _shard_stream(x)
    in_maps = [{"xs": s, "k0": np.ascontiguousarray(kernel_w),
                "b0": np.ascontiguousarray(bias_w)} for s in shards]
    res = run_bass_kernel_spmd(nc, in_maps, core_ids=list(range(NCORES)))
    return _unshard_stream([res.results[c]["ys"] for c in range(NCORES)])


def _as_bf16(a):
    a = np.asarray(a)
    if a.dtype == np.uint16:
        return a.view(ml_dtypes.bfloat16)
    return a.astype(ml_dtypes.bfloat16) if a.dtype != ml_dtypes.bfloat16 else a


def kernel_unfused(**inputs):
    f32 = lambda k2: np.ascontiguousarray(np.asarray(inputs[k2], np.float32))
    x = f32("L0_attn_x")
    shards = _shard_stream(x)

    def launch(nc, maps, outkey="ys"):
        res = run_bass_kernel_spmd(nc, maps, core_ids=list(range(NCORES)))
        return [np.ascontiguousarray(res.results[c][outkey]) for c in range(NCORES)]

    nc_ffn0 = build_l0ffn_program()
    k0, b0 = f32("L0_ffn_kernel"), f32("L0_ffn_bias")
    shards = launch(nc_ffn0, [{"xs": s, "k0": k0, "b0": b0} for s in shards])

    nc_att0 = build_l0attn_program()
    maps = []
    for c in range(NCORES):
        maps.append({"xs": shards[c], "cid": np.array([c], np.float32),
                     "wq": f32("L0_attn_Wq"), "wk": f32("L0_attn_Wk"),
                     "wv": f32("L0_attn_Wv"), "wo": f32("L0_attn_Wo"),
                     "wg1": f32("L0_attn_Wg1"), "wg2": f32("L0_attn_Wg2"),
                     "ld": f32("L0_attn_log_decay"), "nw": f32("L0_attn_norm_w")})
    shards = launch(nc_att0, maps)

    shards = launch(nc_ffn0, [{"xs": s, "k0": k0, "b0": b0} for s in shards])

    nc_ffn1 = build_l1ffn_program()
    w1 = np.ascontiguousarray(_as_bf16(inputs["L1_ffn_weight1"]))
    w2 = np.ascontiguousarray(_as_bf16(inputs["L1_ffn_weight2"]))
    w3 = np.ascontiguousarray(_as_bf16(inputs["L1_ffn_weight3"]))
    def ffn1_maps(sh):
        return [{"xs": sh[c],
                 "w1s": np.ascontiguousarray(w1[c * SH1:(c + 1) * SH1]),
                 "w2s": np.ascontiguousarray(w2[c * SH1:(c + 1) * SH1]),
                 "w3s": np.ascontiguousarray(w3[c * SH3:(c + 1) * SH3])}
                for c in range(NCORES)]
    shards = launch(nc_ffn1, ffn1_maps(shards))

    nc_att1 = build_l1attn_program()
    maps = []
    for c in range(NCORES):
        maps.append({"xs": shards[c], "cid": np.array([c], np.float32),
                     "qkv": f32("L1_attn_qkv"), "op": f32("L1_attn_o_proj"),
                     "gamma": f32("L1_attn_gamma"), "beta": f32("L1_attn_beta")})
    shards = launch(nc_att1, maps)

    shards = launch(nc_ffn1, ffn1_maps(shards))

    return _unshard_stream(shards).astype(np.float32)



def build_fused_program():
    nc = bass.Bass("TRN2", target_bir_lowering=False)
    em = EM(nc)
    C = Ctx(nc, em)
    xin = nc.dram_tensor("xin", [D, T], F32, kind="ExternalInput").ap()
    cid = nc.dram_tensor("cid", [1], F32, kind="ExternalInput").ap()
    k0 = nc.dram_tensor("k0", [D, D], F32, kind="ExternalInput").ap()
    b0 = nc.dram_tensor("b0", [D], F32, kind="ExternalInput").ap()
    W0 = {nm: nc.dram_tensor(nm, shp, F32, kind="ExternalInput").ap()
          for nm, shp in (("wq", [D, D]), ("wk", [D, D]), ("wv", [D, D]),
                          ("wo", [D, D]), ("wg1", [D, 128]), ("wg2", [128, D]),
                          ("ld", [H]), ("nw", [D]))}
    W1 = {nm: nc.dram_tensor(nm, shp, F32, kind="ExternalInput").ap()
          for nm, shp in (("qkv", [D, 3 * D]), ("op", [D, D]),
                          ("gamma", [D]), ("beta", [D]))}
    w1s = nc.dram_tensor("w1s", [SH1, D], BF16, kind="ExternalInput").ap()
    w2s = nc.dram_tensor("w2s", [SH1, D], BF16, kind="ExternalInput").ap()
    w3s = nc.dram_tensor("w3s", [SH3, DFF], BF16, kind="ExternalInput").ap()
    xout = nc.dram_tensor("xout", [D, T], F32, kind="ExternalOutput").ap()
    st = [nc.dram_tensor(f"strm{i}", [D, T], F32).ap() for i in range(5)]
    kv_dram = [nc.dram_tensor(f"kvs{i}", [128, DC * 512], BF16).ap() for i in range(4)]
    sl_dram = nc.dram_tensor("slg", [128, H * HD], F32).ap()
    sall_dram = nc.dram_tensor("sag", [NCORES * 128, H * HD], F32).ap()
    kT_part = nc.dram_tensor("ktp", [D, T], BF16).ap()
    kT_all = nc.dram_tensor("kta", [NCORES * D, T], BF16).ap()
    v_part = nc.dram_tensor("vtp", [T, D], BF16).ap()
    v_all = nc.dram_tensor("vta", [NCORES * T, D], BF16).ap()
    wq1 = nc.dram_tensor("wq1", [SH1, D], BF16).ap()
    wq2 = nc.dram_tensor("wq2", [SH1, D], BF16).ap()
    wq3 = nc.dram_tensor("wq3", [SH3, DFF], BF16).ap()
    wtp1 = nc.dram_tensor("wtp1", [D, SH1], BF16).ap()
    wtp2 = nc.dram_tensor("wtp2", [D, SH1], BF16).ap()
    wtp3 = nc.dram_tensor("wtp3", [DFF, SH3], BF16).ap()
    wta1 = nc.dram_tensor("wta1", [NCORES * D, SH1], BF16).ap()
    wta2 = nc.dram_tensor("wta2", [NCORES * D, SH1], BF16).ap()
    wta3 = nc.dram_tensor("wta3", [NCORES * DFF, SH3], BF16).ap()
    s1b = nc.dram_tensor("s1b", [128 * NC64], F32).ap()
    deqb = nc.dram_tensor("deqb", [128 * NC64], BF16).ap()
    mrb_in = nc.dram_tensor("mrbi", [1, 128], F32).ap()
    mrb_out = nc.dram_tensor("mrbo", [1, 128], F32).ap()

    setup_common(C)
    C.freeze_persist()
    NW = 52400
    arena = nc.alloc_sbuf_tensor("ARENA", [128, NW], F32)
    C.enter_arena(arena, NW)

    def phase_l0ffn(xs_ap, ys_ap):
        C.phase_reset()
        alloc_l0ffn(C)
        emit_l0ffn_prep(C, k0, b0)
        emit_stream_load(C, xs_ap)
        emit_ln_stats(C)
        emit_l0ffn_body(C)
        emit_stream_store(C, ys_ap)

    phase_l0ffn(xin, st[0])
    C.phase_reset()
    emit_l0attn(C, W0, st[0], st[1], cid, kv_dram, sl_dram, sall_dram)
    phase_l0ffn(st[1], st[2])
    C.phase_reset()
    alloc_l1ffn_small(C)
    alloc_l1ffn_body(C)
    wire_l1ffn_aliases(C)
    for (wi, wshd, wqd, wtp, wta, SHR, C_) in (
            (1, w1s, wq1, wtp1, wta1, SH1, D),
            (2, w2s, wq2, wtp2, wta2, SH1, D),
            (3, w3s, wq3, wtp3, wta3, SH3, DFF)):
        emit_l1ffn_prep(C, wi, wshd, wqd, wtp, s1b, deqb, mrb_in, mrb_out, SHR, C_)
        em.cc(lambda g, wtp=wtp, wta=wta: g.collective_compute(
            "AllGather", ALU.bypass, replica_groups=[list(range(NCORES))],
            ins=[wtp.opt()], outs=[wta.opt()]),
            r=[f"w{wi}:wtp"], w=[f"w{wi}:wta"], slot="ccW")
    emit_l1ffn_body(C, wta1, wta2, wta3, st[2], st[3])
    C.phase_reset()
    C.sbuf("qqmask", [128, 512])
    C.sbuf("linv", [128, QG])
    emit_l1attn(C, W1, st[3], st[4], cid, kT_part, kT_all, v_part, v_all)
    C.phase_reset()
    alloc_l1ffn_small(C)
    alloc_l1ffn_body(C)
    wire_l1ffn_aliases(C)
    emit_l1ffn_body(C, wta1, wta2, wta3, st[4], xout)
    em.finalize()
    return nc


def kernel(**inputs):
    f32 = lambda k2: np.ascontiguousarray(np.asarray(inputs[k2], np.float32))
    x = f32("L0_attn_x")
    shards = _shard_stream(x)
    w1 = np.ascontiguousarray(_as_bf16(inputs["L1_ffn_weight1"]))
    w2 = np.ascontiguousarray(_as_bf16(inputs["L1_ffn_weight2"]))
    w3 = np.ascontiguousarray(_as_bf16(inputs["L1_ffn_weight3"]))
    nc = build_fused_program()
    maps = []
    for c in range(NCORES):
        maps.append({
            "xin": shards[c], "cid": np.array([c], np.float32),
            "k0": f32("L0_ffn_kernel"), "b0": f32("L0_ffn_bias"),
            "wq": f32("L0_attn_Wq"), "wk": f32("L0_attn_Wk"),
            "wv": f32("L0_attn_Wv"), "wo": f32("L0_attn_Wo"),
            "wg1": f32("L0_attn_Wg1"), "wg2": f32("L0_attn_Wg2"),
            "ld": f32("L0_attn_log_decay"), "nw": f32("L0_attn_norm_w"),
            "qkv": f32("L1_attn_qkv"), "op": f32("L1_attn_o_proj"),
            "gamma": f32("L1_attn_gamma"), "beta": f32("L1_attn_beta"),
            "w1s": np.ascontiguousarray(w1[c * SH1:(c + 1) * SH1]),
            "w2s": np.ascontiguousarray(w2[c * SH1:(c + 1) * SH1]),
            "w3s": np.ascontiguousarray(w3[c * SH3:(c + 1) * SH3])})
    res = run_bass_kernel_spmd(nc, maps, core_ids=list(range(NCORES)))
    return _unshard_stream(
        [np.ascontiguousarray(res.results[c]["xout"]) for c in range(NCORES)]
    ).astype(np.float32)


# version 03
# speedup vs baseline: 1.0013x; 1.0013x over previous
from collections import defaultdict

import numpy as np
import ml_dtypes
import concourse.bass as bass
import concourse.mybir as mybir
from concourse.bass_utils import run_bass_kernel_spmd

F32 = mybir.dt.float32
BF16 = mybir.dt.bfloat16
AF = mybir.ActivationFunctionType
ALU = mybir.AluOpType
AX = mybir.AxisListType

MAGIC = 12582912.0
NCORES = 8
D = 2048
DC = 16
T = 1024
EPS = 1e-5

ENG_ATTR = {"v": "vector", "s": "scalar", "g": "gpsimd", "t": "tensor", "y": "sync"}


class _Op:
    __slots__ = ("fn", "waits", "semkey", "amt")
    def __init__(self, fn, waits, semkey, amt):
        self.fn, self.waits, self.semkey, self.amt = fn, waits, semkey, amt


class EM:

    def __init__(self, nc):
        self.nc = nc
        self.sems = {}
        for e in ENG_ATTR:
            self.sems[e] = nc.alloc_semaphore(f"dep_{e}")
        self.cnt = defaultdict(int)
        self.stream = {e: [] for e in ENG_ATTR}
        self.waited = {e: defaultdict(int) for e in ENG_ATTR}
        self.last_w = {}
        self.readers = defaultdict(dict)
        self.dma_sems = {}
        self.dma_last = defaultdict(int)
        self.n_ops = 0

    def _needs(self, r, w):
        deps = {}
        def add(dd):
            for sk, c in dd.items():
                deps[sk] = max(deps.get(sk, 0), c)
        for k in r:
            add(self.last_w.get(k, {}))
        for k in w:
            add(self.last_w.get(k, {}))
            add(self.readers.get(k, {}))
        return deps

    def _record(self, eng, fn, r, w, semkey, inc_amt, extra_waits=()):
        deps = self._needs(r, w)
        for sk, c in getattr(self, "pending_fence", {}).items():
            deps[sk] = max(deps.get(sk, 0), c)
        for sk, c in extra_waits:
            deps[sk] = max(deps.get(sk, 0), c)
        wl = []
        wd = self.waited[eng]
        for sk, c in sorted(deps.items()):
            if sk == "t" and eng == "t":
                continue
            if c <= wd[sk]:
                continue
            wd[sk] = c
            wl.append((sk, c))
        self.cnt[semkey] += inc_amt
        post = self.cnt[semkey]
        for k in w:
            self.last_w[k] = {semkey: post}
            self.readers[k] = {}
        for k in r:
            rd = self.readers[k]
            rd[semkey] = max(rd.get(semkey, 0), post)
        self.stream[eng].append(_Op(fn, wl, semkey, inc_amt))
        self.n_ops += 1
        return (semkey, post)

    def op(self, eng, fn, r=(), w=(), after=()):
        return self._record(eng, fn, r, w, eng, 1, after)

    def _slot_sem(self, slot):
        semk = self.dma_sems.get(slot)
        if semk is None:
            semk = f"D{len(self.dma_sems)}"
            self.sems[semk] = self.nc.alloc_semaphore(semk)
            self.dma_sems[slot] = semk
        return semk

    def dma(self, eng, fn, r=(), w=(), slot=None, after=()):
        if slot is None:
            slot = w[0] if w else f"anon{self.n_ops}"
        semk = self._slot_sem(slot)
        extra = list(after)
        if self.dma_last[semk]:
            extra.append((semk, self.dma_last[semk]))
        tok = self._record(eng, fn, r, w, semk, 16, extra)
        self.dma_last[semk] = tok[1]
        return tok

    def cc(self, fn, r=(), w=(), slot="cc", after=()):
        semk = self._slot_sem(slot)
        extra = list(after)
        if self.dma_last[semk]:
            extra.append((semk, self.dma_last[semk]))
        tok = self._record("g", fn, r, w, semk, 1, extra)
        self.dma_last[semk] = tok[1]
        return tok

    def fence(self):
        snap = {}
        for semk, c in self.cnt.items():
            if c:
                snap[semk] = c
        self.pending_fence = snap

    def alias(self, new_keys, old_keys):
        lw, rd = {}, {}
        for o in old_keys:
            for sk, c in self.last_w.get(o, {}).items():
                lw[sk] = max(lw.get(sk, 0), c)
            for sk, c in self.readers.get(o, {}).items():
                rd[sk] = max(rd.get(sk, 0), c)
        for k in new_keys:
            for sk, c in lw.items():
                d = self.last_w.setdefault(k, {})
                d[sk] = max(d.get(sk, 0), c)
            for sk, c in rd.items():
                d = self.readers[k]
                d[sk] = max(d.get(sk, 0), c)

    def finalize(self):
        nc = self.nc
        with nc.Block() as block:
            for ek, attr in ENG_ATTR.items():
                ops = self.stream[ek]
                if not ops:
                    continue
                def make_body(ops):
                    def body(eng):
                        for o in ops:
                            for sk, c in o.waits:
                                eng.wait_ge(self.sems[sk], c)
                            inst = o.fn(eng)
                            inst.then_inc(self.sems[o.semkey], o.amt)
                    return body
                getattr(block, attr)(make_body(ops))


class _Slice:
    def __init__(self, ap):
        self._ap = ap
    def __getitem__(self, key):
        if key == slice(None):
            return self._ap
        return self._ap[key]


class Ctx:
    def __init__(self, nc, em):
        self.nc, self.em = nc, em
        self.sb = {}
        self.ps = {}
        self.arena = None
        self.persist = None
        self.bump = 0

    def freeze_persist(self):
        self.persist = dict(self.sb)

    def enter_arena(self, handle, nwords):
        self.arena = handle
        self.nwords = nwords

    def phase_reset(self):
        self.sb = dict(self.persist)
        self.bump = 0
        self.em.fence()

    def sbuf(self, name, shape, dt=F32):
        if name in self.sb and self.sb[name] is not None:
            return self.sb[name]
        if self.arena is None:
            t = self.nc.alloc_sbuf_tensor("sb_" + name, list(shape), dt)
            self.sb[name] = t
            return t
        assert len(shape) == 2 and shape[0] in (1, 128), (name, shape)
        esz = 2 if dt == BF16 else 4
        nw = (shape[1] * esz + 3) // 4
        off = (self.bump + 15) & ~15
        assert off + nw <= self.nwords, f"arena overflow at {name}: {off + nw}"
        self.bump = off + nw
        base = self.arena[:]
        if dt == BF16:
            ap = base.bitcast(BF16)[0:shape[0], 2 * off:2 * off + shape[1]]
        else:
            ap = base[0:shape[0], off:off + shape[1]]
        t = _Slice(ap)
        self.sb[name] = t
        return t

    def psum(self, name, shape):
        t = self.nc.alloc_psum_tensor("ps_" + name, list(shape), F32)
        self.ps[name] = t
        return t


def setup_common(C):
    em = C.em
    C.sbuf("ones1", [1, 128])
    C.sbuf("ones128", [128, 128])
    C.sbuf("eps_pp", [128, 1])
    C.sbuf("idf", [128, 128])
    C.sbuf("idb", [128, 128], BF16)
    C.sbuf("iot_pmf", [128, 128])
    C.psum("A", [128, 1024])
    C.psum("B", [128, 1024])
    C.psum("p0", [128, 512])
    C.psum("p1", [128, 512])
    C.ps["tb0"] = C.nc.alloc_psum_tensor("ps_tb0", [128, 1024], BF16)
    C.ps["tb1"] = C.nc.alloc_psum_tensor("ps_tb1", [128, 1024], BF16)
    em.op("v", lambda v: v.memset(C.sb["ones1"][:], 1.0), w=["ones1"])
    em.op("v", lambda v: v.memset(C.sb["ones128"][:], 1.0), w=["ones128"])
    em.op("v", lambda v: v.memset(C.sb["eps_pp"][:], EPS), w=["eps_pp"])
    em.op("g", lambda g: g.iota(C.sb["iot_pmf"][:], [[-1, 128]], base=0,
                                channel_multiplier=1,
                                allow_small_or_imprecise_dtypes=True), w=["iot_pmf"])
    em.op("v", lambda v: v.tensor_single_scalar(C.sb["idf"][:], C.sb["iot_pmf"][:],
                                                0.0, ALU.is_equal),
          r=["iot_pmf"], w=["idf"])
    em.op("v", lambda v: v.tensor_copy(C.sb["idb"][:], C.sb["idf"][:]),
          r=["idf"], w=["idb"])


def x_keys():
    return [f"X{c}" for c in range(DC)]


def xview(C, cols=None):
    X = C.sb["X"]
    ap = X[:].rearrange("p (dc t) -> p dc t", dc=DC)
    if cols is not None:
        ap = ap[:, :, cols[0]:cols[1]]
    return ap


def emit_stream_load(C, xs_ap):
    em = C.em
    if "X" not in C.sb:
        C.sbuf("X", [128, DC * T])
    em.dma("y", lambda y, o=C.sb["X"], i=xs_ap: y.dma_start(
        out=o[:].rearrange("p (dc t) -> p dc t", dc=DC),
        in_=i.rearrange("(dc p) t -> p dc t", p=128)),
        w=x_keys(), slot="Xld")


def emit_stream_store(C, ys_ap):
    em = C.em
    em.dma("g", lambda g, o=ys_ap, i=C.sb["X"]: g.dma_start(
        out=o.rearrange("(dc p) t -> p dc t", p=128),
        in_=i[:].rearrange("p (dc t) -> p dc t", dc=DC)),
        r=x_keys(), w=[], slot="Xst")
    sem = em.dma_sems["Xst"]
    em.op("g", lambda g: g.engine_nop(), after=[(sem, em.cnt[sem])])


def emit_ln_stats(C):
    em, sb = C.em, C.sb
    mu = sb.get("mu") or C.sbuf("mu", [128, T])
    var = sb.get("var") or C.sbuf("var", [128, T])
    rstd = sb.get("rstd") or C.sbuf("rstd", [128, T])
    em.alias(["sq0", "sq1"], ["xng", "xng:a", "xng:b"])
    sqs = [sb["xng"][:, 0:T], sb["xng"][:, T:2 * T]]
    PSA, PSB = C.ps["A"], C.ps["B"]
    for c in range(DC):
        sqt = sqs[c % 2]
        sqk = f"sq{c % 2}"
        xv = sb["X"][:, c * T:(c + 1) * T]
        em.op("s", lambda s, o=sqt, i=xv: s.activation(o, i, AF.Square),
              r=[f"X{c}"], w=[sqk])
        for h in range(2):
            em.op("t", lambda t, c=c, h=h, i=xv: t.matmul(
                PSA[:, h * 512:(h + 1) * 512], sb["ones128"][:],
                i[:, h * 512:(h + 1) * 512], start=(c == 0), stop=(c == DC - 1)),
                r=["ones128", f"X{c}"], w=["psA"])
            em.op("t", lambda t, c=c, h=h, i=sqt: t.matmul(
                PSB[:, h * 512:(h + 1) * 512], sb["ones128"][:],
                i[:, h * 512:(h + 1) * 512], start=(c == 0), stop=(c == DC - 1)),
                r=["ones128", sqk], w=["psB"])
    em.alias(["xng"], ["sq0", "sq1"])
    em.op("v", lambda v: v.tensor_single_scalar(mu[:], PSA[:], 1.0 / D, ALU.mult),
          r=["psA"], w=["mu"])
    em.op("v", lambda v: v.tensor_single_scalar(var[:], PSB[:], 1.0 / D, ALU.mult),
          r=["psB"], w=["var"])
    em.op("v", lambda v: v.scalar_tensor_tensor(rstd[:], mu[:], -1.0, mu[:],
                                                ALU.mult, ALU.mult),
          r=["mu"], w=["rstd"])
    em.op("v", lambda v: v.tensor_add(var[:], var[:], rstd[:]),
          r=["var", "rstd"], w=["var"])
    em.op("s", lambda s: s.activation(rstd[:], var[:], AF.Ln, bias=sb["eps_pp"][:]),
          r=["var", "eps_pp"], w=["rstd"])
    em.op("s", lambda s: s.activation(rstd[:], rstd[:], AF.Exp, scale=-0.5),
          r=["rstd"], w=["rstd"])


def emit_transpose128(C, in_ap, in_keys, ps_name, out_ap, out_keys, dtype=F32,
                      evac_eng="v", evac_fn=None):
    em, sb = C.em, C.sb
    ident = sb["idf"] if dtype == F32 else sb["idb"]
    idk = "idf" if dtype == F32 else "idb"
    ps = C.ps[ps_name]
    pr, pc = in_ap.shape[0], in_ap.shape[1]
    em.op("t", lambda t, i=in_ap, d=ident: t.transpose(ps[0:pc, 0:pr], i, d[:]),
          r=list(in_keys) + [idk], w=["ps" + ps_name])
    if evac_fn is None:
        evac_fn = lambda e, o, p: e.tensor_copy(o, p)
    em.op(evac_eng, lambda e, o=out_ap: evac_fn(e, o, ps[0:pc, 0:pr]),
          r=["ps" + ps_name], w=list(out_keys))


def emit_replicate_row(C, row_ap, row_keys, out_ap, out_keys, n, ps_name="p1",
                       scale=None):
    em, sb = C.em, C.sb
    ps = C.ps[ps_name]
    for s0 in range(0, n, 512):
        w_ = min(512, n - s0)
        em.op("t", lambda t, s0=s0, w_=w_, r_=row_ap: t.matmul(
            ps[:, 0:w_], sb["ones1"][:], r_[:, s0:s0 + w_], start=True, stop=True),
            r=list(row_keys) + ["ones1"], w=["ps" + ps_name])
        if scale is None:
            em.op("v", lambda v, s0=s0, w_=w_, o=out_ap: v.tensor_copy(
                o[:, s0:s0 + w_], ps[:, 0:w_]),
                r=["ps" + ps_name], w=list(out_keys))
        else:
            em.op("v", lambda v, s0=s0, w_=w_, o=out_ap: v.tensor_single_scalar(
                o[:, s0:s0 + w_], ps[:, 0:w_], scale, ALU.mult),
                r=["ps" + ps_name], w=list(out_keys))



def alloc_l0ffn(C):
    C.sbuf("kq", [128, DC * D], BF16)
    C.sbuf("scf", [128, DC])
    C.sbuf("bq", [128, DC])
    C.sbuf("xng", [128, DC * 256])
    C.sbuf("qxb", [128, DC * 256], BF16)
    C.sbuf("yg", [128, DC * 256])
    C.sbuf("kamax", [128, DC])
    C.sbuf("kinv", [128, DC])
    C.sbuf("kinvrow", [1, D])
    for nm in ("qx", "qy"):
        C.sbuf(nm + "inv", [128, 512])
        C.sbuf(nm + "scl", [128, 512])
    C.sbuf("qqm1", [128, 512])
    C.sbuf("qqatm", [128, 4])
    C.sbuf("qqarow", [1, 512])


def emit_l0ffn_prep(C, ker_dram, bias_dram):
    em, sb = C.em, C.sb
    kq = sb["kq"]
    kerb = [sb["xng"][:, 0:D], sb["xng"][:, D:2 * D]]
    kerk = ["xng:a", "xng:b"]
    m = [sb["yg"][:, 0:D], sb["yg"][:, D:2 * D]]
    mk = ["yg:a", "yg:b"]
    scr = sb["qxb"][:].bitcast(F32)[:, 0:D]
    scrk = "qxb:pp"
    inv_rep = sb["yg"][:, 0:D]
    invrepk = "yg:a"
    amax_pm, inv_pm, invrow = sb["kamax"], sb["kinv"], sb["kinvrow"]

    kv = ker_dram.rearrange("(dc p) f -> dc p f", p=128)
    for c in range(DC):
        b, bk = kerb[c % 2], kerk[c % 2]
        em.dma("y", lambda y, c=c, b=b: y.dma_start(out=b, in_=kv[c]),
               w=[bk], slot=bk)
        dst, dstk = (m[0], mk[0]) if c == 0 else (scr, scrk)
        em.op("v", lambda v, b=b, dst=dst: v.scalar_tensor_tensor(
            dst, b, -1.0, b, ALU.mult, ALU.max), r=[bk], w=[dstk])
        if c > 0:
            src, srck = m[(c - 1) % 2], mk[(c - 1) % 2]
            dst2, dst2k = m[c % 2], mk[c % 2]
            em.op("v", lambda v, src=src, dst2=dst2: v.tensor_max(dst2, src, scr),
                  r=[srck, scrk], w=[dst2k])
    mfin, mfink = m[(DC - 1) % 2], mk[(DC - 1) % 2]
    for dc in range(DC):
        emit_transpose128(
            C, mfin[:, dc * 128:(dc + 1) * 128], [mfink], "p0",
            amax_pm[:, dc:dc + 1], ["kamax"],
            evac_fn=lambda e, o, p: e.reduce_max(o, p, axis=AX.X))
    em.op("v", lambda v: v.reciprocal(inv_pm[:], amax_pm[:]), r=["kamax"], w=["kinv"])
    em.op("v", lambda v: v.tensor_single_scalar(inv_pm[:], inv_pm[:], 127.0, ALU.mult),
          r=["kinv"], w=["kinv"])
    em.op("v", lambda v: v.tensor_single_scalar(sb["scf"][:], amax_pm[:],
                                                1.0 / 127.0, ALU.mult),
          r=["kamax"], w=["scf"])
    for dc in range(DC):
        emit_transpose128(C, inv_pm[:, dc:dc + 1], ["kinv"], "p0",
                          invrow[:, dc * 128:(dc + 1) * 128], ["kinvrow"])
    emit_replicate_row(C, invrow[:], ["kinvrow"], inv_rep, [invrepk], D)
    for c in range(DC):
        b, bk = kerb[c % 2], kerk[c % 2]
        em.dma("y", lambda y, c=c, b=b: y.dma_start(out=b, in_=kv[c]),
               w=[bk], slot=bk)
        em.op("v", lambda v, b=b: v.tensor_mul(scr, b, inv_rep),
              r=[bk, invrepk], w=[scrk])
        em.op("v", lambda v, c=c: v.tensor_scalar(
            kq[:, c * D:(c + 1) * D], scr, MAGIC, MAGIC, ALU.add, ALU.subtract),
            r=[scrk], w=[f"kq{c}"])
    bt = C.sbuf("bt", [128, DC])
    em.dma("y", lambda y: y.dma_start(
        out=bt[:], in_=bias_dram.rearrange("(dc p) -> p dc", p=128),
        allow_slow_non_contiguous=True), w=["bt"], slot="bt")
    babs = C.sbuf("babs", [128, 1])
    em.op("v", lambda v: v.reduce_max(babs[:], bt[:], axis=AX.X,
                                      apply_absolute_value=True),
          r=["bt"], w=["babs"])
    brow = C.sbuf("brow", [1, 128])
    emit_transpose128(C, babs[:], ["babs"], "p0", brow[:], ["brow"])
    bmax = C.sbuf("bmax", [1, 2])
    em.op("v", lambda v: v.reduce_max(bmax[:, 0:1], brow[:], axis=AX.X),
          r=["brow"], w=["bmax"])
    em.op("v", lambda v: v.reciprocal(bmax[:, 1:2], bmax[:, 0:1]),
          r=["bmax"], w=["bmax"])
    em.op("v", lambda v: v.tensor_single_scalar(bmax[:, 1:2], bmax[:, 1:2],
                                                127.0, ALU.mult),
          r=["bmax"], w=["bmax"])
    em.op("v", lambda v: v.tensor_single_scalar(bmax[:, 0:1], bmax[:, 0:1],
                                                1.0 / 127.0, ALU.mult),
          r=["bmax"], w=["bmax"])
    brep = C.sbuf("brep", [128, 2])
    emit_replicate_row(C, bmax[:], ["bmax"], brep[:], ["brep"], 2)
    em.op("v", lambda v: v.tensor_scalar(sb["bq"][:], bt[:], brep[:, 1:2],
                                         None, ALU.mult),
          r=["bt", "brep"], w=["bq"])
    em.op("v", lambda v: v.tensor_scalar(sb["bq"][:], sb["bq"][:], MAGIC, MAGIC,
                                         ALU.add, ALU.subtract),
          r=["bq"], w=["bq"])
    em.op("v", lambda v: v.tensor_scalar(sb["bq"][:], sb["bq"][:], brep[:, 0:1],
                                         None, ALU.mult),
          r=["bq", "brep"], w=["bq"])


def emit_quant_scales(C, src_view, src_keys, G, tag, extra_mult=1.0):
    em, sb = C.em, C.sb
    nm = tag
    m1, atm, arow = sb["qqm1"], sb["qqatm"], sb["qqarow"]
    inv, scl = sb[nm + "inv"], sb[nm + "scl"]
    ntile = G // 128
    em.op("v", lambda v, s=src_view: v.reduce_max(
        m1[:, 0:G], s.rearrange("p dc t -> p t dc"), axis=AX.X,
        apply_absolute_value=True), r=list(src_keys), w=["qqm1"])
    for i in range(ntile):
        emit_transpose128(
            C, m1[:, i * 128:(i + 1) * 128], ["qqm1"], "p0",
            atm[:, i:i + 1], ["qqatm"],
            evac_fn=lambda e, o, p: e.reduce_max(o, p, axis=AX.X))
    for i in range(ntile):
        emit_transpose128(C, atm[:, i:i + 1], ["qqatm"], "p0",
                          arow[:, i * 128:(i + 1) * 128], ["qqarow"])
    ps = C.ps["p1"]
    em.op("t", lambda t: t.matmul(ps[:, 0:G], sb["ones1"][:], arow[:, 0:G],
                                  start=True, stop=True),
          r=["qqarow", "ones1"], w=["psp1"])
    em.op("v", lambda v: v.reciprocal(inv[:, 0:G], ps[:, 0:G]),
          r=["psp1"], w=[nm + "inv"])
    em.op("v", lambda v: v.tensor_single_scalar(inv[:, 0:G], inv[:, 0:G],
                                                127.0, ALU.mult),
          r=[nm + "inv"], w=[nm + "inv"])
    em.op("v", lambda v: v.tensor_single_scalar(scl[:, 0:G], ps[:, 0:G],
                                                extra_mult / 127.0, ALU.mult),
          r=["psp1"], w=[nm + "scl"])
    return inv, scl, nm


def emit_l0ffn_body(C):
    em, sb = C.em, C.sb
    G = 256
    xng, qxb, yg = sb["xng"], sb["qxb"], sb["yg"]
    mu, rstd = sb["mu"], sb["rstd"]
    em.alias(["xng"], ["xng:a", "xng:b"])
    em.alias(["yg"], ["yg:a", "yg:b"])
    em.alias(["qxb"], ["qxb:pp"])
    for g in range(T // G):
        t0 = g * G
        xnv = xng[:].rearrange("p (dc t) -> p dc t", dc=DC)
        qxv = qxb[:].rearrange("p (dc t) -> p dc t", dc=DC)
        ygv = yg[:].rearrange("p (dc t) -> p dc t", dc=DC)
        xv = xview(C, (t0, t0 + G))
        mub = mu[:, t0:t0 + G].unsqueeze(1).broadcast_to([128, DC, G])
        rsb = rstd[:, t0:t0 + G].unsqueeze(1).broadcast_to([128, DC, G])
        em.op("v", lambda v, a=xnv, b=xv, m=mub: v.tensor_sub(a, b, m),
              r=x_keys() + ["mu"], w=["xng"])
        em.op("v", lambda v, a=xnv, m=rsb: v.tensor_mul(a, a, m),
              r=["xng", "rstd"], w=["xng"])
        inv, scl, qk = emit_quant_scales(C, xnv, ["xng"], G, "qx")
        invb = inv[:, 0:G].unsqueeze(1).broadcast_to([128, DC, G])
        em.op("v", lambda v, a=xnv, m=invb: v.tensor_mul(a, a, m),
              r=["xng", qk + "inv"], w=["xng"])
        em.op("v", lambda v, o=qxv, a=xnv: v.tensor_scalar(
            o, a, MAGIC, MAGIC, ALU.add, ALU.subtract),
            r=["xng"], w=["qxb"])
        for doc in range(DC):
            ps = C.ps[["A", "B", "p0", "p1"][doc % 4]]
            psk = ["psA", "psB", "psp0", "psp1"][doc % 4]
            for dinc in range(DC):
                em.op("t", lambda t, doc=doc, dinc=dinc, ps=ps: t.matmul(
                    ps[:, 0:G],
                    sb["kq"][:, dinc * D + doc * 128: dinc * D + doc * 128 + 128],
                    qxb[:, dinc * G:(dinc + 1) * G],
                    start=(dinc == 0), stop=(dinc == DC - 1)),
                    r=[f"kq{dinc}", "qxb"], w=[psk])
            em.op("v", lambda v, doc=doc, ps=ps, s2=scl: v.scalar_tensor_tensor(
                xng[:, doc * G:(doc + 1) * G], ps[:, 0:G],
                sb["scf"][:, doc:doc + 1], s2[:, 0:G], ALU.mult, ALU.mult),
                r=[psk, "scf", qk + "scl"], w=["xng"])
            em.op("s", lambda s, doc=doc: s.activation(
                yg[:, doc * G:(doc + 1) * G], xng[:, doc * G:(doc + 1) * G],
                AF.Relu, bias=sb["bq"][:, doc:doc + 1]),
                r=["xng", "bq"], w=["yg"])
        inv2, scl2, qk2 = emit_quant_scales(C, ygv, ["yg"], G, "qy",
                                            extra_mult=0.5)
        inv2b = inv2[:, 0:G].unsqueeze(1).broadcast_to([128, DC, G])
        scl2b = scl2[:, 0:G].unsqueeze(1).broadcast_to([128, DC, G])
        em.op("v", lambda v, o=xnv, a=ygv, m=inv2b: v.tensor_mul(o, a, m),
              r=["yg", qk2 + "inv"], w=["xng"])
        em.op("v", lambda v, a=xnv: v.tensor_scalar(
            a, a, MAGIC, MAGIC, ALU.add, ALU.subtract), r=["xng"], w=["xng"])
        em.op("v", lambda v, a=xnv, m=scl2b: v.tensor_mul(a, a, m),
              r=["xng", qk2 + "scl"], w=["xng"])
        em.op("v", lambda v, o=xv, a=xnv: v.tensor_add(o, o, a),
              r=["xng"] + x_keys(), w=x_keys())




H = 16
HD = 128
CH = 128
NCH = T // CH


def alloc_l0attn(C):
    nc = C.nc
    C.psum("tb0", [128, 1024])
    C.psum("tb1", [128, 1024])


def emit_l0attn(C, W, xs_ap, ys_ap, cid_ap, kv_dram, sl_dram, sall_dram):
    nc, em, sb = C.nc, C.em, C.sb
    G = 512
    NG = T // G
    C4 = G // CH
    xg = C.sbuf("xg", [128, DC * G])
    xnb = C.sbuf("xnbg", [128, DC * G], BF16)
    kg = C.sbuf("kg", [128, DC * G], BF16)
    vg = C.sbuf("vg", [128, DC * G], BF16)
    qg = C.sbuf("qg", [128, DC * G], BF16)
    sig = C.sbuf("sigg", [128, DC * G], BF16)
    ogb = C.sbuf("ogb", [128, DC * G], BF16)
    ogc = C.sbuf("ogc", [128, D])
    sqo = C.sbuf("sqo", [128, D])
    rstg = C.sbuf("rstg", [128, D])
    S = C.sbuf("S", [128, H * HD])
    Sbf = C.sbuf("Sbf", [128, H * HD], BF16)
    Dm = C.sbuf("Dm", [128, H * HD], BF16)
    qdec = C.sbuf("qdec", [128, H * HD], BF16)
    kdec = C.sbuf("kdec", [128, H])
    cdr = C.sbuf("cdr", [128, H])
    ldr = C.sbuf("ldr", [128, H])
    nldr = C.sbuf("nldr", [128, H])
    ldrow = C.sbuf("ldrow", [1, H])
    nwt = C.sbuf("nwt", [128, H])
    tri = C.sbuf("tri", [128, 128])
    ir = C.sbuf("irow", [1, 128])
    irep = C.sbuf("irep", [128, 128])
    i127 = C.sbuf("i127", [128, 1])
    cidr = C.sbuf("cidr", [128, 1])
    cid1 = C.sbuf("cid1", [1, 1])
    kts = C.sbuf("kts", [128, 128], BF16)
    vtile = C.sbuf("vtile", [128, 128], BF16)
    qs = C.sbuf("qsc", [128, 128], BF16)
    sdt = C.sbuf("sdt", [128, 128], BF16)
    wbuf = [C.sbuf("wb0", [128, DC * 128], BF16), C.sbuf("wb1", [128, DC * 128], BF16)]
    wg1t = C.sbuf("wg1t", [128, H * 128], BF16)
    wg2t = C.sbuf("wg2t", [128, D], BF16)
    acc = [xg[:, 0:H * HD], xg[:, H * HD:2 * H * HD]]
    stmp = xg[:, 2 * H * HD:3 * H * HD]
    xacc = C.sbuf("xacc", [128, G])
    PSA, PSB, P0, P1 = C.ps["A"], C.ps["B"], C.ps["p0"], C.ps["p1"]
    TB0, TB1 = C.ps["tb0"], C.ps["tb1"]

    emit_ln_stats_dram(C, xs_ap)
    wqv = W["wq"].rearrange("(dc p) f -> p dc f", p=128)
    wkv = W["wk"].rearrange("(dc p) f -> p dc f", p=128)
    wvv = W["wv"].rearrange("(dc p) f -> p dc f", p=128)
    wov = W["wo"].rearrange("(dc p) f -> p dc f", p=128)

    em.dma("y", lambda y: y.dma_start(out=ldrow[:], in_=W["ld"].rearrange("(o h) -> o h", o=1)),
           w=["ldrow"], slot="ldrow")
    emit_replicate_row(C, ldrow[:], ["ldrow"], ldr[:], ["ldr"], H)
    em.op("v", lambda v: v.tensor_single_scalar(nldr[:], ldr[:], -1.0, ALU.mult),
          r=["ldr"], w=["nldr"])
    em.op("s", lambda s: s.activation(cdr[:], ldr[:], AF.Exp, scale=float(CH)),
          r=["ldr"], w=["cdr"])
    em.dma("y", lambda y: y.dma_start(
        out=nwt[:], in_=W["nw"].rearrange("(dc p) -> p dc", p=128),
        allow_slow_non_contiguous=True), w=["nwt"], slot="nwt")
    em.dma("y", lambda y: y.dma_start(out=cid1[:], in_=cid_ap.rearrange("(a o) -> a o", a=1)),
           w=["cid1"], slot="cid1")
    emit_replicate_row(C, cid1[:], ["cid1"], cidr[:], ["cidr"], 1)
    em.op("g", lambda g: g.iota(ir[:], [[1, 128]], base=1, channel_multiplier=0,
                                allow_small_or_imprecise_dtypes=True), w=["irow"])
    emit_replicate_row(C, ir[:], ["irow"], irep[:], ["irep"], 128)
    em.op("g", lambda g: g.iota(i127[:], [[1, 1]], base=127, channel_multiplier=-1,
                                allow_small_or_imprecise_dtypes=True), w=["i127"])
    em.op("v", lambda v: v.tensor_single_scalar(tri[:], sb["iot_pmf"][:], 0.5,
                                                ALU.is_lt), r=["iot_pmf"], w=["tri"])
    miot = C.sbuf("miot", [128, 128])
    em.op("v", lambda v: v.tensor_scalar(miot[:], sb["iot_pmf"][:], -1.0, 0.0,
                                         ALU.mult, ALU.max),
          r=["iot_pmf"], w=["miot"])
    for h in range(H):
        em.op("s", lambda s, h=h: s.activation(
            sb["qdec"][:, h * HD:(h + 1) * HD], irep[:], AF.Exp,
            scale=ldr[:, h:h + 1]), r=["irep", "ldr"], w=["qdec"])
        em.op("s", lambda s, h=h: s.activation(
            kdec[:, h:h + 1], i127[:], AF.Exp, scale=ldr[:, h:h + 1]),
            r=["i127", "ldr"], w=["kdec"])
        em.op("s", lambda s, h=h: s.activation(
            ogc[:, h * HD:(h + 1) * HD], miot[:], AF.Exp,
            scale=ldr[:, h:h + 1]), r=["miot", "ldr"], w=["ogc:d"])
        em.op("v", lambda v, h=h: v.tensor_mul(
            Dm[:, h * HD:(h + 1) * HD], ogc[:, h * HD:(h + 1) * HD], tri[:]),
            r=["ogc:d", "tri"], w=["Dm"])
    em.alias(["ogc"], ["ogc:d"])
    em.dma("g", lambda g: g.dma_start(
        out=wg1t[:].rearrange("p (dc f) -> p dc f", dc=H),
        in_=W["wg1"].rearrange("(dc p) f -> p dc f", p=128)), w=["wg1t"], slot="wg1t")
    em.dma("g", lambda g: g.dma_start(out=wg2t[:], in_=W["wg2"]), w=["wg2t"], slot="wg2t")
    em.op("v", lambda v: v.memset(S[:], 0.0), w=["S"])

    def ln_group(g):
        t0 = g * G
        em.alias(["xg"], ["var"])
        xgv = xg[:].rearrange("p (dc t) -> p dc t", dc=DC)
        em.dma("y", lambda y, t0=t0: y.dma_start(
            out=xgv, in_=xs_ap.rearrange("(dc p) t -> p dc t", p=128)[:, :, t0:t0 + G]),
            w=["xg"], slot="xg")
        mub = sb["mu"][:, t0:t0 + G].unsqueeze(1).broadcast_to([128, DC, G])
        rsb = sb["rstd"][:, t0:t0 + G].unsqueeze(1).broadcast_to([128, DC, G])
        em.op("v", lambda v, m=mub: v.tensor_sub(xgv, xgv, m),
              r=["xg", "mu"], w=["xg"])
        em.op("v", lambda v, m=rsb: v.tensor_mul(xgv, xgv, m),
              r=["xg", "rstd"], w=["xg"])
        em.op("v", lambda v: v.tensor_copy(xnb[:], xg[:]), r=["xg"], w=["xnb"])

    def proj_fm(out_tile, out_key, w_view, wslotbase, act):
        for oc in range(DC):
            wb = wbuf[oc % 2]
            wbk = f"wb{oc % 2}"
            em.dma("g", lambda g, oc=oc, wb=wb: g.dma_start(
                out=wb[:].rearrange("p (dc f) -> p dc f", dc=DC),
                in_=w_view[:, :, oc * 128:(oc + 1) * 128]),
                w=[wbk], slot=wslotbase + str(oc % 2))
            ps = [PSA, PSB, P0, P1][oc % 4]
            psk = ["psA", "psB", "psp0", "psp1"][oc % 4]
            for dinc in range(DC):
                em.op("t", lambda t, oc=oc, dinc=dinc, wb=wb, ps=ps: t.matmul(
                    ps[:, 0:G], wb[:, dinc * 128:(dinc + 1) * 128],
                    xnb[:, dinc * G:(dinc + 1) * G],
                    start=(dinc == 0), stop=(dinc == DC - 1)),
                    r=[wbk, "xnb"], w=[psk])
            if act is None:
                em.op("s", lambda s, oc=oc, ps=ps: s.copy(
                    out_tile[:, oc * G:(oc + 1) * G], ps[:, 0:G]),
                    r=[psk], w=[out_key])
            else:
                em.op("s", lambda s, oc=oc, ps=ps, a=act: s.activation(
                    out_tile[:, oc * G:(oc + 1) * G], ps[:, 0:G], a),
                    r=[psk], w=[out_key])

    def s_chain(g, phase):
        for c4 in range(C4):
            for h in range(H):
                kv_sl = kg[:, h * G + c4 * CH: h * G + (c4 + 1) * CH]
                vv_sl = vg[:, h * G + c4 * CH: h * G + (c4 + 1) * CH]
                em.op("t", lambda t, s=kv_sl: t.transpose(
                    TB0[0:128, 0:128], s, sb["idb"][:]),
                    r=["kg", "idb"], w=["pstb0"])
                em.op("v", lambda v, h=h: v.tensor_scalar(
                    kts[:], TB0[0:128, 0:128], kdec[:, h:h + 1], None, ALU.mult),
                    r=["pstb0", "kdec"], w=["kts"])
                em.op("t", lambda t, s=vv_sl: t.transpose(
                    TB1[0:128, 0:128], s, sb["idb"][:]),
                    r=["vg", "idb"], w=["pstb1"])
                em.op("s", lambda s: s.copy(vtile[:], TB1[0:128, 0:128]),
                      r=["pstb1"], w=["vtile"])
                if phase == "B":
                    phase_b_inner(g, c4, h)
                em.op("t", lambda t: t.matmul(
                    PSA[:, 0:128], kts[:], vtile[:], start=True, stop=True),
                    r=["kts", "vtile"], w=["psA"])
                em.op("v", lambda v, h=h: v.scalar_tensor_tensor(
                    S[:, h * HD:(h + 1) * HD], S[:, h * HD:(h + 1) * HD],
                    cdr[:, h:h + 1], PSA[:, 0:128], ALU.mult, ALU.add),
                    r=["S", "cdr", "psA"], w=["S"])
                if phase == "B":
                    em.op("v", lambda v, h=h: v.tensor_copy(
                        Sbf[:, h * HD:(h + 1) * HD], S[:, h * HD:(h + 1) * HD]),
                        r=["S"], w=["Sbf"])
            if phase == "B":
                phase_b_chunk_tail(g, c4)

    def phase_b_inner(g, c4, h):
        ksl = kg[:, h * G + c4 * CH:h * G + (c4 + 1) * CH]
        qsl = qg[:, h * G + c4 * CH:h * G + (c4 + 1) * CH]
        em.op("t", lambda t, ksl=ksl, qsl=qsl: t.matmul(
            P0[:, 0:128], ksl, qsl, start=True, stop=True),
            r=["kg", "qg"], w=["psp0"])
        em.op("v", lambda v, h=h: v.tensor_mul(
            sdt[:], P0[:, 0:128], Dm[:, h * HD:(h + 1) * HD]),
            r=["psp0", "Dm"], w=["sdt"])
        em.op("v", lambda v, h=h, qsl=qsl: v.tensor_mul(
            qs[:], qsl, qdec[:, h * HD:(h + 1) * HD]),
            r=["qg", "qdec"], w=["qsc"])
        em.op("t", lambda t: t.matmul(P1[:, 0:128], vtile[:], sdt[:],
                                      start=True, stop=False),
              r=["vtile", "sdt"], w=["psp1"])
        em.op("t", lambda t, h=h: t.matmul(
            P1[:, 0:128], Sbf[:, h * HD:(h + 1) * HD], qs[:],
            start=False, stop=True), r=["Sbf", "qsc"], w=["psp1"])
        em.op("v", lambda v, h=h, g=g, c4=c4: v.tensor_mul(
            ogc[:, h * HD:(h + 1) * HD], P1[:, 0:128],
            sig[:, h * G + c4 * CH:h * G + (c4 + 1) * CH]),
            r=["psp1", "sigg"], w=["ogc"])

    def phase_b_chunk_tail(g, c4):
        em.op("s", lambda s: s.activation(sqo[:], ogc[:], AF.Square),
              r=["ogc"], w=["sqo"])
        for q in range(4):
            ps = [PSA, PSB][q // 2]
            psk = ["psA", "psB"][q // 2]
            em.op("t", lambda t, q=q, ps=ps: t.matmul(
                ps[:, (q % 2) * 512:(q % 2 + 1) * 512], sb["ones128"][:],
                sqo[:, q * 512:(q + 1) * 512], start=True, stop=True),
                r=["ones128", "sqo"], w=[psk])
        for q in range(4):
            ps = [PSA, PSB][q // 2]
            psk = ["psA", "psB"][q // 2]
            em.op("s", lambda s, q=q, ps=ps: s.activation(
                rstg[:, q * 512:(q + 1) * 512], ps[:, (q % 2) * 512:(q % 2 + 1) * 512],
                AF.Ln, bias=sb["eps_pp"][:], scale=1.0 / HD),
                r=[psk, "eps_pp"], w=["rstg"])
        em.op("s", lambda s: s.activation(rstg[:], rstg[:], AF.Exp, scale=-0.5),
              r=["rstg"], w=["rstg"])
        em.op("v", lambda v: v.tensor_mul(ogc[:], ogc[:], rstg[:]),
              r=["ogc", "rstg"], w=["ogc"])
        ogv = ogc[:].rearrange("p (h e) -> p h e", h=H)
        nb = nwt[:].unsqueeze(2).broadcast_to([128, H, HD])
        ob = ogb[:].rearrange("p (h t) -> p h t", h=H)[:, :, c4 * CH:(c4 + 1) * CH]
        em.op("v", lambda v, ob=ob: v.tensor_mul(ob, ogv, nb),
              r=["ogc", "nwt"], w=["ogb"])

    for g in range(NG):
        ln_group(g)
        proj_fm(kg, "kg", wkv, "wk", AF.Silu)
        proj_fm(vg, "vg", wvv, "wv", AF.Silu)
        s_chain(g, "A")
        em.dma("g", lambda gg, g=g: gg.dma_start(
            out=kv_dram[g * 2], in_=kg[:]), r=["kg"], w=[f"kvd{g}k"], slot="kvs")
        em.dma("g", lambda gg, g=g: gg.dma_start(
            out=kv_dram[g * 2 + 1], in_=vg[:]), r=["vg"], w=[f"kvd{g}v"], slot="kvs")

    em.dma("g", lambda g: g.dma_start(out=sl_dram, in_=S[:]), r=["S"],
           w=["sld"], slot="sld")
    em.cc(lambda g: g.collective_compute(
        "AllGather", ALU.bypass, replica_groups=[list(range(NCORES))],
        ins=[sl_dram.opt()], outs=[sall_dram.opt()]),
        r=["sld"], w=["sall"], slot="ccS")
    em.alias(["sacc0", "sacc1", "stmp"], ["xg"])
    em.op("v", lambda v: v.memset(acc[0], 0.0), w=["sacc0"])
    coef = C.sbuf("coef", [128, H])
    mj = C.sbuf("mjt", [128, 1])
    for j in range(NCORES - 1):
        em.dma("y", lambda y, j=j: y.dma_start(
            out=stmp, in_=sall_dram[j * 128:(j + 1) * 128, :]),
            r=["sall"], w=["stmp"], slot="stmp")
        em.op("v", lambda v, j=j: v.tensor_single_scalar(
            mj[:], cidr[:], float(-1 - j), ALU.add), r=["cidr"], w=["mjt"])
        em.op("v", lambda v: v.tensor_scalar(mj[:], mj[:], 1024.0, 0.0,
                                             ALU.mult, ALU.max),
              r=["mjt"], w=["mjt"])
        em.op("s", lambda s: s.activation(coef[:], ldr[:], AF.Exp, scale=mj[:]),
              r=["ldr", "mjt"], w=["coef"])
        em.op("v", lambda v, j=j: v.tensor_single_scalar(
            mj[:], cidr[:], float(j) + 0.5, ALU.is_gt), r=["cidr"], w=["mjt"])
        em.op("v", lambda v: v.tensor_scalar(coef[:], coef[:], mj[:], None,
                                             ALU.mult), r=["coef", "mjt"], w=["coef"])
        a0, a1 = acc[j % 2], acc[(j + 1) % 2]
        a0k, a1k = f"sacc{j % 2}", f"sacc{(j + 1) % 2}"
        for h in range(H):
            em.op("v", lambda v, h=h, a0=a0, a1=a1: v.scalar_tensor_tensor(
                a1[:, h * HD:(h + 1) * HD], stmp[:, h * HD:(h + 1) * HD],
                coef[:, h:h + 1], a0[:, h * HD:(h + 1) * HD], ALU.mult, ALU.add),
                r=["stmp", "coef", a0k], w=[a1k])
    afin, afink = acc[(NCORES - 1) % 2], f"sacc{(NCORES - 1) % 2}"
    em.op("v", lambda v, a=afin: v.tensor_copy(S[:], a), r=[afink], w=["S"])
    em.op("v", lambda v: v.tensor_copy(Sbf[:], S[:]), r=["S"], w=["Sbf"])
    em.alias(["xg"], ["sacc0", "sacc1", "stmp"])

    for g in range(NG):
        ln_group(g)
        em.dma("y", lambda y, g=g: y.dma_start(out=kg[:], in_=kv_dram[g * 2]),
               r=[f"kvd{g}k"], w=["kg"], slot="kgld")
        em.dma("y", lambda y, g=g: y.dma_start(out=vg[:], in_=kv_dram[g * 2 + 1]),
               r=[f"kvd{g}v"], w=["vg"], slot="vgld")
        proj_fm(qg, "qg", wqv, "wq", AF.Silu)
        for dinc in range(DC):
            em.op("t", lambda t, dinc=dinc: t.matmul(
                P0[:, 0:G], wg1t[:, dinc * 128:(dinc + 1) * 128],
                xnb[:, dinc * G:(dinc + 1) * G],
                start=(dinc == 0), stop=(dinc == DC - 1)),
                r=["wg1t", "xnb"], w=["psp0"])
        g1b = kts
        em.op("v", lambda v: v.tensor_copy(
            ogc[:].bitcast(BF16)[:, 0:G], P0[:, 0:G]), r=["psp0"], w=["g1b"])
        for oc in range(DC):
            em.op("t", lambda t, oc=oc: t.matmul(
                P1[:, 0:G], wg2t[:, oc * 128:(oc + 1) * 128],
                ogc[:].bitcast(BF16)[:, 0:G], start=True, stop=True),
                r=["wg2t", "g1b"], w=["psp1"])
            em.op("s", lambda s, oc=oc: s.activation(
                sig[:, oc * G:(oc + 1) * G], P1[:, 0:G], AF.Sigmoid),
                r=["psp1"], w=["sigg"])
        em.alias(["ogc"], ["g1b"])
        s_chain(g, "B")
        for oc in range(DC):
            ps = [PSA, PSB, P0, P1][oc % 4]
            psk = ["psA", "psB", "psp0", "psp1"][oc % 4]
            wb = wbuf[oc % 2]
            wbk = f"wb{oc % 2}"
            em.dma("g", lambda gg, oc=oc, wb=wb: gg.dma_start(
                out=wb[:].rearrange("p (dc f) -> p dc f", dc=DC),
                in_=wov[:, :, oc * 128:(oc + 1) * 128]),
                w=[wbk], slot="wo" + str(oc % 2))
            for h in range(H):
                em.op("t", lambda t, oc=oc, h=h, wb=wb, ps=ps: t.matmul(
                    ps[:, 0:G], wb[:, h * 128:(h + 1) * 128],
                    ogb[:, h * G:(h + 1) * G],
                    start=(h == 0), stop=(h == H - 1)),
                    r=[wbk, "ogb"], w=[psk])
            em.dma("y", lambda y, oc=oc, g=g: y.dma_start(
                out=xacc[:], in_=xs_ap[oc * 128:(oc + 1) * 128, g * G:(g + 1) * G]),
                w=["xacc"], slot="xacc")
            em.op("v", lambda v, ps=ps: v.tensor_add(xacc[:], xacc[:], ps[:, 0:G]),
                  r=["xacc", psk], w=["xacc"])
            em.dma("g", lambda gg, oc=oc, g=g: gg.dma_start(
                out=ys_ap[oc * 128:(oc + 1) * 128, g * G:(g + 1) * G], in_=xacc[:]),
                r=["xacc"], w=[], slot="ysst")
    sem = em.dma_sems["ysst"]
    em.op("g", lambda g: g.engine_nop(), after=[(sem, em.cnt[sem])])


def build_l0attn_program():
    nc = bass.Bass("TRN2", target_bir_lowering=False)
    em = EM(nc)
    C = Ctx(nc, em)
    xs = nc.dram_tensor("xs", [D, T], F32, kind="ExternalInput").ap()
    cid = nc.dram_tensor("cid", [1], F32, kind="ExternalInput").ap()
    W = {
        "wq": nc.dram_tensor("wq", [D, D], F32, kind="ExternalInput").ap(),
        "wk": nc.dram_tensor("wk", [D, D], F32, kind="ExternalInput").ap(),
        "wv": nc.dram_tensor("wv", [D, D], F32, kind="ExternalInput").ap(),
        "wo": nc.dram_tensor("wo", [D, D], F32, kind="ExternalInput").ap(),
        "wg1": nc.dram_tensor("wg1", [D, 128], F32, kind="ExternalInput").ap(),
        "wg2": nc.dram_tensor("wg2", [128, D], F32, kind="ExternalInput").ap(),
        "ld": nc.dram_tensor("ld", [H], F32, kind="ExternalInput").ap(),
        "nw": nc.dram_tensor("nw", [D], F32, kind="ExternalInput").ap(),
    }
    ys = nc.dram_tensor("ys", [D, T], F32, kind="ExternalOutput").ap()
    kv_dram = [nc.dram_tensor(f"kvs{i}", [128, DC * 512], BF16).ap() for i in range(4)]
    sl_dram = nc.dram_tensor("slg", [128, H * HD], F32).ap()
    sall_dram = nc.dram_tensor("sag", [NCORES * 128, H * HD], F32).ap()
    setup_common(C)
    emit_l0attn(C, W, xs, ys, cid, kv_dram, sl_dram, sall_dram)
    em.finalize()
    return nc


def emit_ln_stats_dram(C, xs_ap):
    em, sb = C.em, C.sb
    mu = sb.get("mu") or C.sbuf("mu", [128, T])
    rstd = sb.get("rstd") or C.sbuf("rstd", [128, T])
    xgt = sb["xg"]
    xc = [xgt[:, 0:T], xgt[:, T:2 * T]]
    sq = [xgt[:, 2 * T:3 * T], xgt[:, 3 * T:4 * T]]
    var = _Slice(xgt[:, 4 * T:5 * T])
    sb["var"] = var
    PSA, PSB = C.ps["A"], C.ps["B"]
    xv = xs_ap.rearrange("(dc p) t -> dc p t", p=128)
    for c in range(DC):
        xt, xk = xc[c % 2], f"lnx{c % 2}"
        st, sk = sq[c % 2], f"lnq{c % 2}"
        em.dma("y", lambda y, c=c, xt=xt: y.dma_start(out=xt, in_=xv[c]),
               w=[xk], slot=xk)
        em.op("s", lambda s, xt=xt, st=st: s.activation(st, xt, AF.Square),
              r=[xk], w=[sk])
        for h in range(2):
            em.op("t", lambda t, c=c, h=h, xt=xt: t.matmul(
                PSA[:, h * 512:(h + 1) * 512], sb["ones128"][:],
                xt[:, h * 512:(h + 1) * 512], start=(c == 0), stop=(c == DC - 1)),
                r=["ones128", xk], w=["psA"])
            em.op("t", lambda t, c=c, h=h, st=st: t.matmul(
                PSB[:, h * 512:(h + 1) * 512], sb["ones128"][:],
                st[:, h * 512:(h + 1) * 512], start=(c == 0), stop=(c == DC - 1)),
                r=["ones128", sk], w=["psB"])
    em.alias(["xg"], ["lnx0", "lnx1", "lnq0", "lnq1"])
    em.op("v", lambda v: v.tensor_single_scalar(mu[:], PSA[:], 1.0 / D, ALU.mult),
          r=["psA"], w=["mu"])
    em.op("v", lambda v: v.tensor_single_scalar(var[:], PSB[:], 1.0 / D, ALU.mult),
          r=["psB"], w=["var"])
    em.op("v", lambda v: v.scalar_tensor_tensor(rstd[:], mu[:], -1.0, mu[:],
                                                ALU.mult, ALU.mult),
          r=["mu"], w=["rstd"])
    em.op("v", lambda v: v.tensor_add(var[:], var[:], rstd[:]),
          r=["var", "rstd"], w=["var"])
    em.op("s", lambda s: s.activation(rstd[:], var[:], AF.Ln, bias=sb["eps_pp"][:]),
          r=["var", "eps_pp"], w=["rstd"])
    em.op("s", lambda s: s.activation(rstd[:], rstd[:], AF.Exp, scale=-0.5),
          r=["rstd"], w=["rstd"])


def run_l0attn(x, wq, wk, wv, wo, wg1, wg2, ld, nw):
    nc = build_l0attn_program()
    shards = _shard_stream(x)
    in_maps = []
    for c in range(NCORES):
        in_maps.append({
            "xs": shards[c], "cid": np.array([c], np.float32),
            "wq": np.ascontiguousarray(wq), "wk": np.ascontiguousarray(wk),
            "wv": np.ascontiguousarray(wv), "wo": np.ascontiguousarray(wo),
            "wg1": np.ascontiguousarray(wg1), "wg2": np.ascontiguousarray(wg2),
            "ld": np.ascontiguousarray(ld), "nw": np.ascontiguousarray(nw)})
    res = run_bass_kernel_spmd(nc, in_maps, core_ids=list(range(NCORES)))
    return _unshard_stream([res.results[c]["ys"] for c in range(NCORES)])




SCALE1 = float(1.0 / np.sqrt(np.float32(2048.0)))
NSTRIP = 16
QT = 8
QG = 4


def emit_l1attn(C, W, xs_ap, ys_ap, cid_ap, kT_part, kT_all, v_part, v_all):
    nc, em, sb = C.nc, C.em, C.sb
    G = 512
    NG = T // G
    xg = C.sbuf("xg", [128, DC * G])
    xnb = C.sbuf("xnbg", [128, DC * G], BF16)
    kg = C.sbuf("kg", [128, DC * G], BF16)
    vgT = C.sbuf("vgT", [128, DC * G], BF16)
    vtm = C.sbuf("vtm", [128, 4 * D], BF16)
    qT = C.sbuf("qTf", [128, DC * T], BF16)
    accf = _Slice(xg[:])
    ab = C.sbuf("abt", [128, 2 * T])
    gmt = C.sbuf("gmt", [128, DC])
    bet = C.sbuf("bet", [128, DC])
    wbuf = [C.sbuf("wb0", [128, DC * 128], BF16), C.sbuf("wb1", [128, DC * 128], BF16)]
    iot512 = C.sbuf("iot512", [128, 512])
    zer512 = C.sbuf("zer512", [128, 512])
    cidr = C.sbuf("cidr", [128, 1])
    cid1 = C.sbuf("cid1", [1, 1])
    c1024 = C.sbuf("c1024", [128, 1])
    cval = C.sbuf("cval", [128, 1])
    mold = C.sbuf("mold", [128, QG])
    lrun = C.sbuf("lrun", [128, QG])
    rmx = C.sbuf("rmx", [128, 4])
    pst = C.sbuf("pstl", [128, 512], BF16)
    pT = C.sbuf("pTt", [128, 512], BF16)
    rs = C.sbuf("rst", [128, 1])
    xacc = C.sbuf("xacc", [128, G])
    xsc = C.sbuf("xsc", [128, G])
    PSA, PSB, P0, P1 = C.ps["A"], C.ps["B"], C.ps["p0"], C.ps["p1"]
    TB0 = C.ps["tb0"]
    kst = [kg[:].bitcast(BF16), vgT[:].bitcast(BF16)]
    kstk = ["kg", "vgT"]
    vst = [vtm[:].bitcast(BF16), xnb[:].bitcast(BF16)]
    vstk = ["vtm", "xnb"]

    emit_ln_stats_dram(C, xs_ap)
    mu, var, rstd = sb["mu"], sb["var"], sb["rstd"]
    a_b, b_b = ab[:, 0:T], ab[:, T:2 * T]
    em.op("v", lambda v: v.tensor_mul(a_b, rstd[:], rstd[:]), r=["rstd"], w=["ab:a"])
    em.op("v", lambda v: v.tensor_mul(a_b, a_b, var[:]), r=["ab:a", "var"], w=["ab:a"])
    em.op("s", lambda s: s.activation(a_b, a_b, AF.Ln, bias=sb["eps_pp"][:]),
          r=["ab:a", "eps_pp"], w=["ab:a"])
    em.op("s", lambda s: s.activation(a_b, a_b, AF.Exp, scale=-0.5),
          r=["ab:a"], w=["ab:a"])
    em.op("v", lambda v: v.tensor_mul(a_b, a_b, rstd[:]), r=["ab:a", "rstd"], w=["ab:a"])
    em.op("v", lambda v: v.scalar_tensor_tensor(b_b, mu[:], -1.0, a_b,
                                                ALU.mult, ALU.mult),
          r=["mu", "ab:a"], w=["ab:b"])
    em.dma("y", lambda y: y.dma_start(
        out=gmt[:], in_=W["gamma"].rearrange("(dc p) -> p dc", p=128),
        allow_slow_non_contiguous=True), w=["gmt"], slot="gmt")
    em.dma("y", lambda y: y.dma_start(
        out=bet[:], in_=W["beta"].rearrange("(dc p) -> p dc", p=128),
        allow_slow_non_contiguous=True), w=["bet"], slot="bet")
    em.dma("y", lambda y: y.dma_start(out=cid1[:], in_=cid_ap.rearrange("(a o) -> a o", a=1)),
           w=["cid1"], slot="cid1")
    emit_replicate_row(C, cid1[:], ["cid1"], cidr[:], ["cidr"], 1)
    em.op("v", lambda v: v.tensor_single_scalar(c1024[:], cidr[:], 1024.0, ALU.mult),
          r=["cidr"], w=["c1024"])
    em.op("g", lambda g: g.iota(iot512[:], [[-1, 512]], base=0, channel_multiplier=1,
                                allow_small_or_imprecise_dtypes=True), w=["iot512"])
    em.op("v", lambda v: v.memset(zer512[:], 0.0), w=["zer512"])

    qkvv = W["qkv"].rearrange("(dc p) f -> p dc f", p=128)
    opv = W["op"].rearrange("(dc p) f -> p dc f", p=128)

    for g in range(NG):
        t0 = g * G
        em.alias(["xg"], ["var"])
        xgv = xg[:].rearrange("p (dc t) -> p dc t", dc=DC)
        em.dma("y", lambda y, t0=t0: y.dma_start(
            out=xgv, in_=xs_ap.rearrange("(dc p) t -> p dc t", p=128)[:, :, t0:t0 + G]),
            w=["xg"], slot="xgld")
        abb = a_b[:, t0:t0 + G].unsqueeze(1).broadcast_to([128, DC, G])
        bbb = b_b[:, t0:t0 + G].unsqueeze(1).broadcast_to([128, DC, G])
        em.op("v", lambda v, m=abb: v.tensor_mul(xgv, xgv, m),
              r=["xg", "ab:a"], w=["xg"])
        em.op("v", lambda v, m=bbb: v.tensor_add(xgv, xgv, m),
              r=["xg", "ab:b"], w=["xg"])
        for dc in range(DC):
            em.op("v", lambda v, dc=dc: v.scalar_tensor_tensor(
                xnb[:, dc * G:(dc + 1) * G], xg[:, dc * G:(dc + 1) * G],
                gmt[:, dc:dc + 1], bet[:, dc:dc + 1].broadcast_to([128, G]),
                ALU.mult, ALU.add), r=["xg", "gmt", "bet"], w=["xnb"])
        for (dst, dstk, off, slotb) in ((qT, "qTf", 0, "q"), (kg, "kg", D, "k"),
                                        (vgT, "vgT", 2 * D, "v")):
            for oc in range(DC):
                wb = wbuf[oc % 2]
                wbk = f"wb{oc % 2}"
                em.dma("g", lambda gg, oc=oc, wb=wb, off=off: gg.dma_start(
                    out=wb[:].rearrange("p (dc f) -> p dc f", dc=DC),
                    in_=qkvv[:, :, off + oc * 128:off + (oc + 1) * 128]),
                    w=[wbk], slot=slotb + str(oc % 2))
                ps = [PSA, PSB, P0, P1][oc % 4]
                psk = ["psA", "psB", "psp0", "psp1"][oc % 4]
                for dinc in range(DC):
                    em.op("t", lambda t, oc=oc, dinc=dinc, wb=wb, ps=ps: t.matmul(
                        ps[:, 0:G], wb[:, dinc * 128:(dinc + 1) * 128],
                        xnb[:, dinc * G:(dinc + 1) * G],
                        start=(dinc == 0), stop=(dinc == DC - 1)),
                        r=[wbk, "xnb"], w=[psk])
                if dst is qT:
                    em.op("s", lambda s, oc=oc, ps=ps, t0=t0: s.copy(
                        qT[:, oc * T + t0:oc * T + t0 + G], ps[:, 0:G]),
                        r=[psk], w=["qTf"])
                else:
                    em.op("s", lambda s, oc=oc, ps=ps, dst=dst: s.copy(
                        dst[:, oc * G:(oc + 1) * G], ps[:, 0:G]),
                        r=[psk], w=[dstk])
        em.dma("g", lambda gg, t0=t0: gg.dma_start(
            out=kT_part.rearrange("(dc p) t -> p dc t", p=128)[:, :, t0:t0 + G],
            in_=kg[:].rearrange("p (dc t) -> p dc t", dc=DC)),
            r=["kg"], w=["ktp"], slot="ktp")
        for c4 in range(4):
            for oc in range(DC):
                em.op("t", lambda t, c4=c4, oc=oc: t.transpose(
                    TB0[0:128, 0:128],
                    vgT[:, oc * G + c4 * 128:oc * G + (c4 + 1) * 128], sb["idb"][:]),
                    r=["vgT", "idb"], w=["pstb0"])
                em.op("s", lambda s, c4=c4, oc=oc: s.copy(
                    vtm[:, c4 * D + oc * 128:c4 * D + (oc + 1) * 128],
                    TB0[0:128, 0:128]), r=["pstb0"], w=["vtm"])
        em.dma("g", lambda gg, g=g: gg.dma_start(
            out=v_part[g * G:(g + 1) * G, :].rearrange("(c4 p) f -> p c4 f", p=128),
            in_=vtm[:].rearrange("p (c4 f) -> p c4 f", c4=4)),
            r=["vtm"], w=["vtp"], slot="vtp")
    em.cc(lambda g: g.collective_compute(
        "AllGather", ALU.bypass, replica_groups=[list(range(NCORES))],
        ins=[kT_part.opt()], outs=[kT_all.opt()]), r=["ktp"], w=["ktall"], slot="ccK")
    em.cc(lambda g: g.collective_compute(
        "AllGather", ALU.bypass, replica_groups=[list(range(NCORES))],
        ins=[v_part.opt()], outs=[v_all.opt()]), r=["vtp"], w=["vall"], slot="ccV")

    em.alias(kstk + vstk, ["kg", "vgT", "vtm", "xnb"])
    em.alias(["accf"], ["xg"])
    for half in range(2):
        em.op("v", lambda v: v.memset(accf[:], 0.0), w=["accf"])
        em.op("v", lambda v: v.memset(mold[:], -3.0e38), w=["mold"])
        em.op("v", lambda v: v.memset(lrun[:], 0.0), w=["lrun"])
        for s_ in range(NSTRIP):
            kb, kbk = kst[s_ % 2], kstk[s_ % 2]
            vb, vbk = vst[s_ % 2], vstk[s_ % 2]
            slab, col = s_ // 2, (s_ % 2) * G
            em.dma("y", lambda y, slab=slab, col=col, kb=kb: y.dma_start(
                out=kb.rearrange("p (dc t) -> p dc t", dc=DC),
                in_=kT_all[slab * D:(slab + 1) * D, :].rearrange(
                    "(dc p) t -> p dc t", p=128)[:, :, col:col + G]),
                r=["ktall"], w=[kbk], slot="k" + str(s_ % 2) + "ld")
            em.dma("y", lambda y, s_=s_, vb=vb: y.dma_start(
                out=vb.rearrange("p (c4 f) -> p c4 f", c4=4),
                in_=v_all[s_ * G:(s_ + 1) * G, :].rearrange("(c4 p) f -> p c4 f", p=128)),
                r=["vall"], w=[vbk], slot="v" + str(s_ % 2) + "ld")
            for qi in range(QG):
                qt = half * QG + qi
                for dc in range(DC):
                    em.op("t", lambda t, dc=dc, qt=qt, kb=kb: t.matmul(
                        P0[:, 0:G], qT[:, dc * T + qt * 128:dc * T + (qt + 1) * 128],
                        kb[:, dc * G:(dc + 1) * G],
                        start=(dc == 0), stop=(dc == DC - 1)),
                        r=["qTf", kbk], w=["psp0"])
                em.op("v", lambda v, qt=qt, s_=s_: v.tensor_single_scalar(
                    cval[:], c1024[:], float(128 * qt - 512 * s_), ALU.add),
                    r=["c1024"], w=["cval"])
                em.op("v", lambda v: v.scalar_tensor_tensor(
                    pst[:].bitcast(F32)[:, 0:256].rearrange("p (a b) -> p a b", a=1).squeeze(1) if False else sb["qqmask"][:],
                    iot512[:], cval[:], zer512[:], ALU.add, ALU.is_lt),
                    r=["iot512", "cval", "zer512"], w=["qqmask"])
                em.op("v", lambda v: v.scalar_tensor_tensor(
                    P0[:, 0:G], sb["qqmask"][:], -1.0e30, P0[:, 0:G],
                    ALU.mult, ALU.add), r=["qqmask", "psp0"], w=["psp0"])
                em.op("v", lambda v: v.reduce_max(rmx[:, 0:1], P0[:, 0:G], axis=AX.X),
                      r=["psp0"], w=["rmx"])
                em.op("v", lambda v, qi=qi: v.tensor_max(
                    rmx[:, 1:2], rmx[:, 0:1], mold[:, qi:qi + 1]),
                      r=["rmx", "mold"], w=["rmx"])
                em.op("v", lambda v, qi=qi: v.tensor_sub(
                    rmx[:, 2:3], mold[:, qi:qi + 1], rmx[:, 1:2]),
                      r=["rmx", "mold"], w=["rmx"])
                em.op("s", lambda s: s.activation(rmx[:, 2:3], rmx[:, 2:3],
                                                  AF.Exp, scale=SCALE1),
                      r=["rmx"], w=["rmx"])
                em.op("v", lambda v: v.tensor_single_scalar(
                    rmx[:, 3:4], rmx[:, 1:2], -SCALE1, ALU.mult),
                    r=["rmx"], w=["rmx"])
                em.op("v", lambda v, qi=qi: v.tensor_copy(
                    mold[:, qi:qi + 1], rmx[:, 1:2]), r=["rmx"], w=["mold"])
                em.op("s", lambda s: s.activation(
                    pst[:], P0[:, 0:G], AF.Exp, bias=rmx[:, 3:4], scale=SCALE1),
                    r=["psp0", "rmx"], w=["pstl"])
                em.op("v", lambda v: v.reduce_sum(rs[:], pst[:], axis=AX.X),
                      r=["pstl"], w=["rst"])
                em.op("v", lambda v, qi=qi: v.scalar_tensor_tensor(
                    lrun[:, qi:qi + 1], lrun[:, qi:qi + 1], rmx[:, 2:3], rs[:],
                    ALU.mult, ALU.add), r=["lrun", "rmx", "rst"], w=["lrun"])
                for jt in range(4):
                    em.op("t", lambda t, jt=jt: t.transpose(
                        TB0[0:128, 0:128], pst[:, jt * 128:(jt + 1) * 128],
                        sb["idb"][:]), r=["pstl", "idb"], w=["pstb0"])
                    em.op("s", lambda s, jt=jt: s.copy(
                        pT[:, jt * 128:(jt + 1) * 128], TB0[0:128, 0:128]),
                        r=["pstb0"], w=["pTt"])
                for fs in range(4):
                    for jt in range(4):
                        em.op("t", lambda t, fs=fs, jt=jt, vb=vb: t.matmul(
                            P1[:, 0:G], pT[:, jt * 128:(jt + 1) * 128],
                            vb[:, jt * D + fs * G:jt * D + (fs + 1) * G],
                            start=(jt == 0), stop=(jt == 3)),
                            r=["pTt", vbk], w=["psp1"])
                    em.op("v", lambda v, fs=fs, qi=qi: v.scalar_tensor_tensor(
                        accf[:, qi * D + fs * G:qi * D + (fs + 1) * G],
                        accf[:, qi * D + fs * G:qi * D + (fs + 1) * G],
                        rmx[:, 2:3], P1[:, 0:G], ALU.mult, ALU.add),
                        r=["accf", "rmx", "psp1"], w=["accf"])
        em.op("v", lambda v: v.reciprocal(rmx[:, 0:1].broadcast_to([128, 1]) if False else sb["linv"][:], lrun[:]),
              r=["lrun"], w=["linv"])
        for qi in range(QG):
            qt = half * QG + qi
            em.op("v", lambda v, qi=qi: v.tensor_scalar(
                accf[:, qi * D:(qi + 1) * D], accf[:, qi * D:(qi + 1) * D],
                sb["linv"][:, qi:qi + 1], None, ALU.mult),
                r=["accf", "linv"], w=["accf"])
            if half == 0 and qi == 0:
                em.alias(["accT"], ["qTf"])
            for dc in range(DC):
                em.op("t", lambda t, qi=qi, dc=dc: t.transpose(
                    P0[:, 0:128], accf[:, qi * D + dc * 128:qi * D + (dc + 1) * 128],
                    sb["idf"][:]), r=["accf", "idf"], w=["psp0"])
                em.op("s", lambda s, qt=qt, dc=dc: s.copy(
                    qT[:].bitcast(BF16)[:, 0:DC * T].rearrange(
                        "p (dc t) -> p dc t", dc=DC)[:, dc, qt * 128:(qt + 1) * 128],
                    P0[:, 0:128]), r=["psp0"], w=["accT"])
    accT = qT[:].bitcast(BF16)[:, 0:DC * T]
    for g in range(NG):
        t0 = g * G
        for oc in range(DC):
            wb = wbuf[oc % 2]
            wbk = f"wb{oc % 2}"
            em.dma("g", lambda gg, oc=oc, wb=wb: gg.dma_start(
                out=wb[:].rearrange("p (dc f) -> p dc f", dc=DC),
                in_=opv[:, :, oc * 128:(oc + 1) * 128]),
                w=[wbk], slot="op" + str(oc % 2))
            ps = [PSA, PSB, P0, P1][oc % 4]
            psk = ["psA", "psB", "psp0", "psp1"][oc % 4]
            for dc in range(DC):
                em.op("t", lambda t, oc=oc, dc=dc, wb=wb, ps=ps, t0=t0: t.matmul(
                    ps[:, 0:G], wb[:, dc * 128:(dc + 1) * 128],
                    accT.rearrange("p (dc t) -> p dc t", dc=DC)[:, dc, t0:t0 + G],
                    start=(dc == 0), stop=(dc == DC - 1)),
                    r=[wbk, "accT"], w=[psk])
            em.dma("y", lambda y, oc=oc, t0=t0: y.dma_start(
                out=xsc[:], in_=xs_ap[oc * 128:(oc + 1) * 128, t0:t0 + G]),
                w=["xsc"], slot="xsc")
            em.op("v", lambda v, t0=t0: v.tensor_sub(
                xacc[:], xsc[:], mu[:, t0:t0 + G]), r=["xsc", "mu"], w=["xacc"])
            em.op("v", lambda v, t0=t0: v.tensor_mul(
                xacc[:], xacc[:], rstd[:, t0:t0 + G]), r=["xacc", "rstd"], w=["xacc"])
            em.op("v", lambda v: v.tensor_add(xacc[:], xacc[:], xsc[:]),
                  r=["xacc", "xsc"], w=["xacc"])
            em.op("v", lambda v, ps=ps: v.tensor_add(xacc[:], xacc[:], ps[:, 0:G]),
                  r=["xacc", psk], w=["xacc"])
            em.dma("g", lambda gg, oc=oc, t0=t0: gg.dma_start(
                out=ys_ap[oc * 128:(oc + 1) * 128, t0:t0 + G], in_=xacc[:]),
                r=["xacc"], w=[], slot="ysst")
    sem = em.dma_sems["ysst"]
    em.op("g", lambda g: g.engine_nop(), after=[(sem, em.cnt[sem])])


def build_l1attn_program():
    nc = bass.Bass("TRN2", target_bir_lowering=False)
    em = EM(nc)
    C = Ctx(nc, em)
    xs = nc.dram_tensor("xs", [D, T], F32, kind="ExternalInput").ap()
    cid = nc.dram_tensor("cid", [1], F32, kind="ExternalInput").ap()
    W = {
        "qkv": nc.dram_tensor("qkv", [D, 3 * D], F32, kind="ExternalInput").ap(),
        "op": nc.dram_tensor("op", [D, D], F32, kind="ExternalInput").ap(),
        "gamma": nc.dram_tensor("gamma", [D], F32, kind="ExternalInput").ap(),
        "beta": nc.dram_tensor("beta", [D], F32, kind="ExternalInput").ap(),
    }
    ys = nc.dram_tensor("ys", [D, T], F32, kind="ExternalOutput").ap()
    kT_part = nc.dram_tensor("ktp", [D, T], BF16).ap()
    kT_all = nc.dram_tensor("kta", [NCORES * D, T], BF16).ap()
    v_part = nc.dram_tensor("vtp", [T, D], BF16).ap()
    v_all = nc.dram_tensor("vta", [NCORES * T, D], BF16).ap()
    setup_common(C)
    C.sbuf("qqmask", [128, 512])
    C.sbuf("linv", [128, QG])
    emit_l1attn(C, W, xs, ys, cid, kT_part, kT_all, v_part, v_all)
    em.finalize()
    return nc


def run_l1attn(x, qkv, op, gamma, beta):
    nc = build_l1attn_program()
    shards = _shard_stream(x)
    in_maps = []
    for c in range(NCORES):
        in_maps.append({
            "xs": shards[c], "cid": np.array([c], np.float32),
            "qkv": np.ascontiguousarray(qkv), "op": np.ascontiguousarray(op),
            "gamma": np.ascontiguousarray(gamma), "beta": np.ascontiguousarray(beta)})
    res = run_bass_kernel_spmd(nc, in_maps, core_ids=list(range(NCORES)))
    return _unshard_stream([res.results[c]["ys"] for c in range(NCORES)])





NF4_BF = [float(np.float32(ml_dtypes.bfloat16(v))) for v in
          [-1.0, -0.6962, -0.5251, -0.3949, -0.2844, -0.1848, -0.0911, 0.0,
           0.0796, 0.1609, 0.2461, 0.3379, 0.4407, 0.5626, 0.723, 1.0]]
NF4_MID = [(NF4_BF[i] + NF4_BF[i + 1]) / 2.0 for i in range(15)]
NF4_DELTA = [NF4_BF[i + 1] - NF4_BF[i] for i in range(15)]
DFF = 5632
FC = DFF // 128
SH1 = DFF // NCORES
SH3 = D // NCORES
NBLK = 180224
PEL = SH1 * D // 128
NC64 = PEL // 64
NSB = 88


def emit_l1ffn_prep(C, wi, wshd, wqd, wtp, s1b, deqb, mrb_in, mrb_out, SHR, C_):
    nc, em, sb = C.nc, C.em, C.sb
    wsh, scb, val = sb["wsh"], sb["scb"], sb["valt"]
    s1, s1r, deql = sb["s1t"], sb["s1r"], sb["deql"]
    sbt, sbb, sbc = sb["sbt"], sb["sbb"], sb["sbc"]
    qf2, mpp = sb["qf2"], sb["mpp"]
    rowsum, msum, m11 = sb["rowsum"], sb["msum"], sb["m11"]
    stg, ttin = sb["stg"], sb["ttin"]
    P0, TB0 = C.ps["p0"], C.ps["tb0"]
    w = f"w{wi}:"

    em.dma("y", lambda y: y.dma_start(
        out=wsh[:, 0:PEL], in_=wshd.flatten().rearrange("(p i) -> p i", p=128)),
        w=["wsh"], slot="wshld")
    em.op("v", lambda v: v.reduce_max(
        s1[:, 0:NC64], wsh[:, 0:PEL].rearrange("p (b e) -> p b e", e=64),
        axis=AX.X, apply_absolute_value=True), r=["wsh"], w=["s1"])
    em.op("v", lambda v: v.reduce_sum(rowsum[:, 0:1], s1[:, 0:NC64], axis=AX.X),
          r=["s1"], w=["rs"])
    em.op("v", lambda v: v.memset(msum[:], 0.0), w=["ms"])
    em.op("t", lambda t: t.matmul(P0[0:1, 0:1], sb["ones128"][:, 0:1],
                                  rowsum[:, 0:1], start=True, stop=True),
          r=["ones128", "rs"], w=["psp0"])
    em.op("v", lambda v: v.tensor_copy(msum[:, 0:1], P0[0:1, 0:1]),
          r=["psp0"], w=["ms"])
    em.dma("g", lambda g: g.dma_start(out=mrb_in, in_=msum[:]),
           r=["ms"], w=[w + "mrb"], slot="mrb")
    em.cc(lambda g: g.collective_compute(
        "AllReduce", ALU.add, replica_groups=[list(range(NCORES))],
        ins=[mrb_in.opt()], outs=[mrb_out.opt()]),
        r=[w + "mrb"], w=[w + "mro"], slot="ccM")
    em.dma("y", lambda y: y.dma_start(out=msum[:], in_=mrb_out),
           r=[w + "mro"], w=["ms"], slot="mrb2")
    em.op("v", lambda v: v.tensor_single_scalar(
        m11[:].bitcast(BF16)[:, 0:1], msum[:, 0:1], 1.0 / NBLK, ALU.mult),
        r=["ms"], w=["m11"])
    em.op("v", lambda v: v.tensor_copy(m11[:, 1:2], m11[:].bitcast(BF16)[:, 0:1]),
          r=["m11"], w=["m11"])
    emit_replicate_row(C, m11[:, 1:2], ["m11"], mpp[:], ["mpp"], 1)
    em.dma("g", lambda g: g.dma_start(out=s1b[0:128 * NC64], in_=s1[:, 0:NC64]),
           r=["s1"], w=[w + "s1b"], slot="s1b")
    em.dma("y", lambda y: y.dma_start(
        out=sbt[0:NSB, :], in_=s1b.rearrange("(s j) -> s j", j=256)[0:NSB, :]),
        r=[w + "s1b"], w=["sbt"], slot="sbt")
    em.op("v", lambda v: v.tensor_scalar(sbb[0:NSB, :], sbt[0:NSB, :],
                                         mpp[0:NSB, 0:1], None, ALU.subtract),
          r=["sbt", "mpp"], w=["sbb"])
    em.op("v", lambda v: v.reduce_max(qf2[0:NSB, 0:1], sbb[0:NSB, :], axis=AX.X,
                                      apply_absolute_value=True),
          r=["sbb"], w=["qf2"])
    em.op("v", lambda v: v.reciprocal(qf2[0:NSB, 1:2], qf2[0:NSB, 0:1]),
          r=["qf2"], w=["qf2"])
    em.op("v", lambda v: v.tensor_single_scalar(
        qf2[:].bitcast(BF16)[0:NSB, 4:5], qf2[0:NSB, 1:2], 128.0, ALU.mult),
        r=["qf2"], w=["qf2"])
    em.op("v", lambda v: v.tensor_copy(qf2[0:NSB, 2:3],
                                       qf2[:].bitcast(BF16)[0:NSB, 4:5]),
          r=["qf2"], w=["qf2"])
    em.op("v", lambda v: v.reciprocal(qf2[0:NSB, 3:4], qf2[0:NSB, 2:3]),
          r=["qf2"], w=["qf2"])
    em.op("v", lambda v: v.tensor_scalar(sbc[0:NSB, :], sbb[0:NSB, :],
                                         qf2[0:NSB, 2:3], None, ALU.mult),
          r=["sbb", "qf2"], w=["sbc"])
    em.op("v", lambda v: v.tensor_scalar(sbt[0:NSB, :], sbc[0:NSB, :],
                                         MAGIC, MAGIC, ALU.add, ALU.subtract),
          r=["sbc"], w=["sbt"])
    em.op("v", lambda v: v.tensor_scalar(sbt[0:NSB, :], sbt[0:NSB, :],
                                         127.0, -128.0, ALU.min, ALU.max),
          r=["sbt"], w=["sbt"])
    em.op("v", lambda v: v.tensor_scalar(sbc[0:NSB, :], sbt[0:NSB, :],
                                         qf2[0:NSB, 3:4], None, ALU.mult),
          r=["sbt", "qf2"], w=["sbc"])
    em.op("v", lambda v: v.tensor_scalar(sbb[0:NSB, :], sbc[0:NSB, :],
                                         mpp[0:NSB, 0:1], None, ALU.add),
          r=["sbc", "mpp"], w=["sbb"])
    em.dma("g", lambda g: g.dma_start(
        out=deqb.rearrange("(s j) -> s j", j=256)[0:NSB, :], in_=sbb[0:NSB, :]),
        r=["sbb"], w=[w + "deqb"], slot="deqb")
    em.dma("y", lambda y: y.dma_start(
        out=deql[:, 0:NC64], in_=deqb.rearrange("(p i) -> p i", p=128)),
        r=[w + "deqb"], w=["deql"], slot="deql")
    em.op("v", lambda v: v.reciprocal(s1r[:, 0:NC64], s1[:, 0:NC64]),
          r=["s1"], w=["s1r"])
    em.op("v", lambda v: v.tensor_mul(
        scb[:, 0:PEL].rearrange("p (b e) -> p b e", e=64),
        wsh[:, 0:PEL].rearrange("p (b e) -> p b e", e=64),
        s1r[:, 0:NC64].unsqueeze(2).broadcast_to([128, NC64, 64])),
        r=["wsh", "s1r"], w=["scb"])
    em.op("v", lambda v: v.memset(val[:, 0:PEL], NF4_BF[0]), w=["val"])
    for k in range(15):
        em.op("g", lambda g, k=k: g.tensor_scalar(
            sb["indt"][:, 0:PEL], scb[:, 0:PEL], NF4_MID[k], None, ALU.is_gt),
            r=["scb"], w=["ind"])
        em.op("v", lambda v, k=k: v.scalar_tensor_tensor(
            val[:, 0:PEL], sb["indt"][:, 0:PEL], NF4_DELTA[k], val[:, 0:PEL],
            ALU.mult, ALU.add), r=["ind", "val"], w=["val"])
    em.op("v", lambda v: v.tensor_mul(
        wsh[:, 0:PEL].rearrange("p (b e) -> p b e", e=64),
        val[:, 0:PEL].rearrange("p (b e) -> p b e", e=64),
        deql[:, 0:NC64].unsqueeze(2).broadcast_to([128, NC64, 64])),
        r=["val", "deql", "wsh"], w=["wsh"])
    em.dma("g", lambda g: g.dma_start(
        out=wqd.flatten().rearrange("(p i) -> p i", p=128), in_=wsh[:, 0:PEL]),
        r=["wsh"], w=[w + "wqd"], slot="wqd")
    nrt = (SHR + 127) // 128
    for ct in range(C_ // 128):
        for rt in range(nrt):
            rows = min(128, SHR - rt * 128)
            em.dma("y", lambda y, ct=ct, rt=rt, rows=rows: y.dma_start(
                out=ttin[0:rows, :],
                in_=wqd[rt * 128:rt * 128 + rows, ct * 128:(ct + 1) * 128]),
                r=[w + "wqd"], w=["ttin"], slot="ttin" + str(rt % 2))
            em.op("t", lambda t, rows=rows: t.transpose(
                TB0[0:128, 0:rows], ttin[0:rows, :], sb["idb"][0:rows, 0:rows]),
                r=["ttin", "idb"], w=["pstb0"])
            em.op("s", lambda s, rt=rt, rows=rows: s.copy(
                stg[:, rt * 128:rt * 128 + rows], TB0[0:128, 0:rows]),
                r=["pstb0"], w=["stg"])
        em.dma("g", lambda g, ct=ct: g.dma_start(
            out=wtp[ct * 128:(ct + 1) * 128, :], in_=stg[:, 0:SHR]),
            r=["stg"], w=[w + "wtp"], slot="wtpst")
    return w + "wtp"


def emit_l1ffn_body(C, w1a, w2a, w3a, xs_ap, ys_ap, first_store=True):
    nc, em, sb = C.nc, C.em, C.sb
    G = 512
    NG = T // G
    em.alias(["xg"], ["wsh"])
    em.alias(["hbuf"], ["scb", "ind"])
    em.alias(["w3b0", "w3b1"], ["val"])
    xg, xnb = sb["xg"], sb["xnbg"]
    hbuf = sb["hbuf"]
    hsc = sb["hsc"]
    w12b = [sb["w12b0"], sb["w12b1"], sb["w12b2"], sb["w12b3"]]
    w3b = [sb["w3b0"], sb["w3b1"]]
    xacc, xsc = sb["xacc"], sb["xsc"]
    PSA, PSB = C.ps["A"], C.ps["B"]
    P0x, P1x = C.ps["p0"], C.ps["p1"]
    emit_ln_stats_dram(C, xs_ap)
    mu, rstd = sb["mu"], sb["rstd"]

    def seg_loads(dst, dstk, slot, wta, fc, SHR, rkey):
        f0, f1 = fc * 128, (fc + 1) * 128
        segs = []
        f = f0
        while f < f1:
            slab = min(f // SH1, 7)
            lo = SH1 * slab
            hi = SH1 * (slab + 1) if slab < 7 else DFF
            end = min(f1, hi)
            segs.append((slab, f - lo, f - f0, end - f))
            f = end
        for (slab, loc, off, wd) in segs:
            em.dma("y", lambda y, slab=slab, loc=loc, off=off, wd=wd, wta=wta: y.dma_start(
                out=dst[:].rearrange("p (dc f) -> p dc f", dc=DC)[:, :, off:off + wd],
                in_=wta[slab * D:(slab + 1) * D, :].rearrange(
                    "(dc p) s -> p dc s", p=128)[:, :, loc:loc + wd]),
                r=[rkey], w=[dstk], slot=slot)

    for g in range(NG):
        t0 = g * G
        em.alias(["xg"], ["var"])
        xgv = xg[:].rearrange("p (dc t) -> p dc t", dc=DC)
        em.dma("y", lambda y, t0=t0: y.dma_start(
            out=xgv, in_=xs_ap.rearrange("(dc p) t -> p dc t", p=128)[:, :, t0:t0 + G]),
            w=["xg"], slot="xgld")
        mub = mu[:, t0:t0 + G].unsqueeze(1).broadcast_to([128, DC, G])
        rsb = rstd[:, t0:t0 + G].unsqueeze(1).broadcast_to([128, DC, G])
        em.op("v", lambda v, m=mub: v.tensor_sub(xgv, xgv, m), r=["xg", "mu"], w=["xg"])
        em.op("v", lambda v, m=rsb: v.tensor_mul(xgv, xgv, m), r=["xg", "rstd"], w=["xg"])
        em.op("v", lambda v: v.tensor_copy(xnb[:], xg[:]), r=["xg"], w=["xnb"])
        for fc in range(FC):
            b1, b1k = w12b[fc % 2], f"w12b{fc % 2}"
            b2, b2k = w12b[2 + fc % 2], f"w12b{2 + fc % 2}"
            seg_loads(b1, b1k, "w1l" + str(fc % 2), w1a, fc, SH1, "w1:wta")
            seg_loads(b2, b2k, "w2l" + str(fc % 2), w2a, fc, SH1, "w2:wta")
            ps1, ps1k = [PSA, PSB, P0x, P1x][fc % 4], ["psA", "psB", "psp0", "psp1"][fc % 4]
            for dc in range(DC):
                em.op("t", lambda t, fc=fc, dc=dc, b1=b1, ps=ps1: t.matmul(
                    ps[:, 0:G], b1[:, dc * 128:(dc + 1) * 128],
                    xnb[:, dc * G:(dc + 1) * G],
                    start=(dc == 0), stop=(dc == DC - 1)),
                    r=[b1k, "xnb"], w=[ps1k])
            em.op("s", lambda s, fc=fc, ps=ps1: s.activation(
                hsc[:, 0:G], ps[:, 0:G], AF.Silu), r=[ps1k], w=["hsc"])
            for dc in range(DC):
                em.op("t", lambda t, fc=fc, dc=dc, b2=b2, ps=ps1: t.matmul(
                    ps[:, 0:G], b2[:, dc * 128:(dc + 1) * 128],
                    xnb[:, dc * G:(dc + 1) * G],
                    start=(dc == 0), stop=(dc == DC - 1)),
                    r=[b2k, "xnb"], w=[ps1k])
            em.op("v", lambda v, fc=fc, ps=ps1: v.tensor_mul(
                hbuf[:, fc * G:(fc + 1) * G], hsc[:, 0:G], ps[:, 0:G]),
                r=["hsc", ps1k], w=["hbuf"])
        for oc in range(DC):
            b3, b3k = w3b[oc % 2], f"w3b{oc % 2}"
            slab, loc = oc // 2, (oc % 2) * 128
            em.dma("y", lambda y, slab=slab, loc=loc, b3=b3, w3a=w3a: y.dma_start(
                out=b3[:].rearrange("p (fc f) -> p fc f", fc=FC),
                in_=w3a[slab * DFF:(slab + 1) * DFF, :].rearrange(
                    "(fc p) s -> p fc s", p=128)[:, :, loc:loc + 128]),
                r=["w3:wta"], w=[b3k], slot="w3l" + str(oc % 2))
            ps1, ps1k = [PSA, PSB, P0x, P1x][oc % 4], ["psA", "psB", "psp0", "psp1"][oc % 4]
            for fc in range(FC):
                em.op("t", lambda t, oc=oc, fc=fc, b3=b3, ps=ps1: t.matmul(
                    ps[:, 0:G], b3[:, fc * 128:(fc + 1) * 128],
                    hbuf[:, fc * G:(fc + 1) * G],
                    start=(fc == 0), stop=(fc == FC - 1)),
                    r=[b3k, "hbuf"], w=[ps1k])
            em.dma("y", lambda y, oc=oc, t0=t0: y.dma_start(
                out=xsc[:], in_=xs_ap[oc * 128:(oc + 1) * 128, t0:t0 + G]),
                w=["xsc"], slot="xsc")
            em.op("v", lambda v, ps=ps1: v.scalar_tensor_tensor(
                xacc[:], ps[:, 0:G], 0.5, xsc[:], ALU.mult, ALU.add),
                r=[ps1k, "xsc"], w=["xacc"])
            em.dma("g", lambda gg, oc=oc, t0=t0: gg.dma_start(
                out=ys_ap[oc * 128:(oc + 1) * 128, t0:t0 + G], in_=xacc[:]),
                r=["xacc"], w=[], slot="ysst")
    sem = em.dma_sems["ysst"]
    em.op("g", lambda g: g.engine_nop(), after=[(sem, em.cnt[sem])])


def alloc_l1ffn_small(C):
    C.sbuf("s1t", [128, NC64])
    C.sbuf("s1r", [128, NC64])
    C.sbuf("deql", [128, NC64], BF16)
    C.sbuf("sbt", [128, 256])
    C.sbuf("sbb", [128, 256], BF16)
    C.sbuf("sbc", [128, 256], BF16)
    C.sbuf("qf2", [128, 4])
    C.sbuf("mpp", [128, 1])
    C.sbuf("rowsum", [128, 2])
    C.sbuf("msum", [1, 128])
    C.sbuf("m11", [1, 2])
    C.sbuf("stg", [128, 768], BF16)
    C.sbuf("ttin", [128, 128], BF16)
    C.sbuf("cidr", [128, 1])
    C.sbuf("cid1", [1, 1])


def alloc_l1ffn_body(C):
    C.sbuf("xg", [128, DC * 512])
    C.sbuf("xnbg", [128, DC * 512], BF16)
    C.sbuf("hbuf", [128, FC * 512], BF16)
    C.sbuf("valt", [128, PEL], BF16)
    C.sbuf("hsc", [128, 512])
    for i in range(4):
        C.sbuf(f"w12b{i}", [128, DC * 128], BF16)
    C.sbuf("xacc", [128, 512])
    C.sbuf("xsc", [128, 512])
    sb = C.sb
    sb["wsh"] = None


def wire_l1ffn_aliases(C):
    sb = C.sb
    sb["wsh"] = _Slice(sb["xg"][:].bitcast(BF16)[:, 0:PEL])
    sb["scb"] = _Slice(sb["hbuf"][:, 0:PEL])
    sb["indt"] = _Slice(sb["hbuf"][:, PEL:2 * PEL])
    sb["w3b0"] = _Slice(sb["valt"][:, 0:FC * 128])
    sb["w3b1"] = _Slice(sb["valt"][:, FC * 128:2 * FC * 128])


def build_l1ffn_program():
    nc = bass.Bass("TRN2", target_bir_lowering=False)
    em = EM(nc)
    C = Ctx(nc, em)
    xs = nc.dram_tensor("xs", [D, T], F32, kind="ExternalInput").ap()
    w1s = nc.dram_tensor("w1s", [SH1, D], BF16, kind="ExternalInput").ap()
    w2s = nc.dram_tensor("w2s", [SH1, D], BF16, kind="ExternalInput").ap()
    w3s = nc.dram_tensor("w3s", [SH3, DFF], BF16, kind="ExternalInput").ap()
    ys = nc.dram_tensor("ys", [D, T], F32, kind="ExternalOutput").ap()
    wq1 = nc.dram_tensor("wq1", [SH1, D], BF16).ap()
    wq2 = nc.dram_tensor("wq2", [SH1, D], BF16).ap()
    wq3 = nc.dram_tensor("wq3", [SH3, DFF], BF16).ap()
    wtp1 = nc.dram_tensor("wtp1", [D, SH1], BF16).ap()
    wtp2 = nc.dram_tensor("wtp2", [D, SH1], BF16).ap()
    wtp3 = nc.dram_tensor("wtp3", [DFF, SH3], BF16).ap()
    wta1 = nc.dram_tensor("wta1", [NCORES * D, SH1], BF16).ap()
    wta2 = nc.dram_tensor("wta2", [NCORES * D, SH1], BF16).ap()
    wta3 = nc.dram_tensor("wta3", [NCORES * DFF, SH3], BF16).ap()
    s1b = nc.dram_tensor("s1b", [128 * NC64], F32).ap()
    deqb = nc.dram_tensor("deqb", [128 * NC64], BF16).ap()
    mrb_in = nc.dram_tensor("mrbi", [1, 128], F32).ap()
    mrb_out = nc.dram_tensor("mrbo", [1, 128], F32).ap()
    setup_common(C)
    alloc_l1ffn_small(C)
    alloc_l1ffn_body(C)
    wire_l1ffn_aliases(C)
    for (wi, wshd, wqd, wtp, wta, SHR, C_) in (
            (1, w1s, wq1, wtp1, wta1, SH1, D),
            (2, w2s, wq2, wtp2, wta2, SH1, D),
            (3, w3s, wq3, wtp3, wta3, SH3, DFF)):
        emit_l1ffn_prep(C, wi, wshd, wqd, wtp, s1b, deqb, mrb_in, mrb_out, SHR, C_)
        C.em.cc(lambda g, wtp=wtp, wta=wta: g.collective_compute(
            "AllGather", ALU.bypass, replica_groups=[list(range(NCORES))],
            ins=[wtp.opt()], outs=[wta.opt()]),
            r=[f"w{wi}:wtp"], w=[f"w{wi}:wta"], slot="ccW")
    emit_l1ffn_body(C, wta1, wta2, wta3, xs, ys)
    em.finalize()
    return nc


def run_l1ffn(x, w1, w2, w3):
    nc = build_l1ffn_program()
    shards = _shard_stream(x)
    in_maps = []
    for c in range(NCORES):
        in_maps.append({
            "xs": shards[c],
            "w1s": np.ascontiguousarray(w1[c * SH1:(c + 1) * SH1]),
            "w2s": np.ascontiguousarray(w2[c * SH1:(c + 1) * SH1]),
            "w3s": np.ascontiguousarray(w3[c * SH3:(c + 1) * SH3])})
    res = run_bass_kernel_spmd(nc, in_maps, core_ids=list(range(NCORES)))
    return _unshard_stream([res.results[c]["ys"] for c in range(NCORES)])



def build_l0ffn_program():
    nc = bass.Bass("TRN2", target_bir_lowering=False)
    em = EM(nc)
    C = Ctx(nc, em)
    xs = nc.dram_tensor("xs", [D, T], F32, kind="ExternalInput").ap()
    ker = nc.dram_tensor("k0", [D, D], F32, kind="ExternalInput").ap()
    bias = nc.dram_tensor("b0", [D], F32, kind="ExternalInput").ap()
    ys = nc.dram_tensor("ys", [D, T], F32, kind="ExternalOutput").ap()
    setup_common(C)
    alloc_l0ffn(C)
    emit_l0ffn_prep(C, ker, bias)
    emit_stream_load(C, xs)
    emit_ln_stats(C)
    emit_l0ffn_body(C)
    emit_stream_store(C, ys)
    em.finalize()
    return nc


def _shard_stream(x):
    xs = np.ascontiguousarray(np.asarray(x, np.float32).reshape(8192, 2048))
    return [np.ascontiguousarray(xs[c * T:(c + 1) * T].T) for c in range(NCORES)]


def _unshard_stream(parts):
    out = np.empty((8192, 2048), np.float32)
    for c in range(NCORES):
        out[c * T:(c + 1) * T] = parts[c].T
    return out.reshape(1, 8192, 2048)


def run_l0ffn(x, kernel_w, bias_w):
    nc = build_l0ffn_program()
    shards = _shard_stream(x)
    in_maps = [{"xs": s, "k0": np.ascontiguousarray(kernel_w),
                "b0": np.ascontiguousarray(bias_w)} for s in shards]
    res = run_bass_kernel_spmd(nc, in_maps, core_ids=list(range(NCORES)))
    return _unshard_stream([res.results[c]["ys"] for c in range(NCORES)])


def _as_bf16(a):
    a = np.asarray(a)
    if a.dtype == np.uint16:
        return a.view(ml_dtypes.bfloat16)
    return a.astype(ml_dtypes.bfloat16) if a.dtype != ml_dtypes.bfloat16 else a


def kernel_unfused(**inputs):
    f32 = lambda k2: np.ascontiguousarray(np.asarray(inputs[k2], np.float32))
    x = f32("L0_attn_x")
    shards = _shard_stream(x)

    def launch(nc, maps, outkey="ys"):
        res = run_bass_kernel_spmd(nc, maps, core_ids=list(range(NCORES)))
        return [np.ascontiguousarray(res.results[c][outkey]) for c in range(NCORES)]

    nc_ffn0 = build_l0ffn_program()
    k0, b0 = f32("L0_ffn_kernel"), f32("L0_ffn_bias")
    shards = launch(nc_ffn0, [{"xs": s, "k0": k0, "b0": b0} for s in shards])

    nc_att0 = build_l0attn_program()
    maps = []
    for c in range(NCORES):
        maps.append({"xs": shards[c], "cid": np.array([c], np.float32),
                     "wq": f32("L0_attn_Wq"), "wk": f32("L0_attn_Wk"),
                     "wv": f32("L0_attn_Wv"), "wo": f32("L0_attn_Wo"),
                     "wg1": f32("L0_attn_Wg1"), "wg2": f32("L0_attn_Wg2"),
                     "ld": f32("L0_attn_log_decay"), "nw": f32("L0_attn_norm_w")})
    shards = launch(nc_att0, maps)

    shards = launch(nc_ffn0, [{"xs": s, "k0": k0, "b0": b0} for s in shards])

    nc_ffn1 = build_l1ffn_program()
    w1 = np.ascontiguousarray(_as_bf16(inputs["L1_ffn_weight1"]))
    w2 = np.ascontiguousarray(_as_bf16(inputs["L1_ffn_weight2"]))
    w3 = np.ascontiguousarray(_as_bf16(inputs["L1_ffn_weight3"]))
    def ffn1_maps(sh):
        return [{"xs": sh[c],
                 "w1s": np.ascontiguousarray(w1[c * SH1:(c + 1) * SH1]),
                 "w2s": np.ascontiguousarray(w2[c * SH1:(c + 1) * SH1]),
                 "w3s": np.ascontiguousarray(w3[c * SH3:(c + 1) * SH3])}
                for c in range(NCORES)]
    shards = launch(nc_ffn1, ffn1_maps(shards))

    nc_att1 = build_l1attn_program()
    maps = []
    for c in range(NCORES):
        maps.append({"xs": shards[c], "cid": np.array([c], np.float32),
                     "qkv": f32("L1_attn_qkv"), "op": f32("L1_attn_o_proj"),
                     "gamma": f32("L1_attn_gamma"), "beta": f32("L1_attn_beta")})
    shards = launch(nc_att1, maps)

    shards = launch(nc_ffn1, ffn1_maps(shards))

    return _unshard_stream(shards).astype(np.float32)



def build_fused_program():
    nc = bass.Bass("TRN2", target_bir_lowering=False)
    em = EM(nc)
    C = Ctx(nc, em)
    xin = nc.dram_tensor("xin", [D, T], F32, kind="ExternalInput").ap()
    cid = nc.dram_tensor("cid", [1], F32, kind="ExternalInput").ap()
    k0 = nc.dram_tensor("k0", [D, D], F32, kind="ExternalInput").ap()
    b0 = nc.dram_tensor("b0", [D], F32, kind="ExternalInput").ap()
    W0 = {nm: nc.dram_tensor(nm, shp, F32, kind="ExternalInput").ap()
          for nm, shp in (("wq", [D, D]), ("wk", [D, D]), ("wv", [D, D]),
                          ("wo", [D, D]), ("wg1", [D, 128]), ("wg2", [128, D]),
                          ("ld", [H]), ("nw", [D]))}
    W1 = {nm: nc.dram_tensor(nm, shp, F32, kind="ExternalInput").ap()
          for nm, shp in (("qkv", [D, 3 * D]), ("op", [D, D]),
                          ("gamma", [D]), ("beta", [D]))}
    w1s = nc.dram_tensor("w1s", [SH1, D], BF16, kind="ExternalInput").ap()
    w2s = nc.dram_tensor("w2s", [SH1, D], BF16, kind="ExternalInput").ap()
    w3s = nc.dram_tensor("w3s", [SH3, DFF], BF16, kind="ExternalInput").ap()
    xout = nc.dram_tensor("xout", [D, T], F32, kind="ExternalOutput").ap()
    st = [nc.dram_tensor(f"strm{i}", [D, T], F32).ap() for i in range(5)]
    kv_dram = [nc.dram_tensor(f"kvs{i}", [128, DC * 512], BF16).ap() for i in range(4)]
    sl_dram = nc.dram_tensor("slg", [128, H * HD], F32).ap()
    sall_dram = nc.dram_tensor("sag", [NCORES * 128, H * HD], F32).ap()
    kT_part = nc.dram_tensor("ktp", [D, T], BF16).ap()
    kT_all = nc.dram_tensor("kta", [NCORES * D, T], BF16).ap()
    v_part = nc.dram_tensor("vtp", [T, D], BF16).ap()
    v_all = nc.dram_tensor("vta", [NCORES * T, D], BF16).ap()
    wq1 = nc.dram_tensor("wq1", [SH1, D], BF16).ap()
    wq2 = nc.dram_tensor("wq2", [SH1, D], BF16).ap()
    wq3 = nc.dram_tensor("wq3", [SH3, DFF], BF16).ap()
    wtp1 = nc.dram_tensor("wtp1", [D, SH1], BF16).ap()
    wtp2 = nc.dram_tensor("wtp2", [D, SH1], BF16).ap()
    wtp3 = nc.dram_tensor("wtp3", [DFF, SH3], BF16).ap()
    wta1 = nc.dram_tensor("wta1", [NCORES * D, SH1], BF16).ap()
    wta2 = nc.dram_tensor("wta2", [NCORES * D, SH1], BF16).ap()
    wta3 = nc.dram_tensor("wta3", [NCORES * DFF, SH3], BF16).ap()
    s1b = nc.dram_tensor("s1b", [128 * NC64], F32).ap()
    deqb = nc.dram_tensor("deqb", [128 * NC64], BF16).ap()
    mrb_in = nc.dram_tensor("mrbi", [1, 128], F32).ap()
    mrb_out = nc.dram_tensor("mrbo", [1, 128], F32).ap()

    setup_common(C)
    C.freeze_persist()
    NW = 52400
    arena = nc.alloc_sbuf_tensor("ARENA", [128, NW], F32)
    C.enter_arena(arena, NW)

    def phase_l0ffn(xs_ap, ys_ap):
        C.phase_reset()
        alloc_l0ffn(C)
        emit_l0ffn_prep(C, k0, b0)
        emit_stream_load(C, xs_ap)
        emit_ln_stats(C)
        emit_l0ffn_body(C)
        emit_stream_store(C, ys_ap)

    phase_l0ffn(xin, st[0])
    C.phase_reset()
    emit_l0attn(C, W0, st[0], st[1], cid, kv_dram, sl_dram, sall_dram)
    phase_l0ffn(st[1], st[2])
    C.phase_reset()
    alloc_l1ffn_small(C)
    alloc_l1ffn_body(C)
    wire_l1ffn_aliases(C)
    for (wi, wshd, wqd, wtp, wta, SHR, C_) in (
            (1, w1s, wq1, wtp1, wta1, SH1, D),
            (2, w2s, wq2, wtp2, wta2, SH1, D),
            (3, w3s, wq3, wtp3, wta3, SH3, DFF)):
        emit_l1ffn_prep(C, wi, wshd, wqd, wtp, s1b, deqb, mrb_in, mrb_out, SHR, C_)
        em.cc(lambda g, wtp=wtp, wta=wta: g.collective_compute(
            "AllGather", ALU.bypass, replica_groups=[list(range(NCORES))],
            ins=[wtp.opt()], outs=[wta.opt()]),
            r=[f"w{wi}:wtp"], w=[f"w{wi}:wta"], slot="ccW")
    emit_l1ffn_body(C, wta1, wta2, wta3, st[2], st[3])
    C.phase_reset()
    C.sbuf("qqmask", [128, 512])
    C.sbuf("linv", [128, QG])
    emit_l1attn(C, W1, st[3], st[4], cid, kT_part, kT_all, v_part, v_all)
    C.phase_reset()
    alloc_l1ffn_small(C)
    alloc_l1ffn_body(C)
    wire_l1ffn_aliases(C)
    emit_l1ffn_body(C, wta1, wta2, wta3, st[4], xout)
    em.finalize()
    return nc


def kernel(**inputs):
    f32 = lambda k2: np.ascontiguousarray(np.asarray(inputs[k2], np.float32))
    x = f32("L0_attn_x")
    shards = _shard_stream(x)
    w1 = np.ascontiguousarray(_as_bf16(inputs["L1_ffn_weight1"]))
    w2 = np.ascontiguousarray(_as_bf16(inputs["L1_ffn_weight2"]))
    w3 = np.ascontiguousarray(_as_bf16(inputs["L1_ffn_weight3"]))
    nc = build_fused_program()
    maps = []
    for c in range(NCORES):
        maps.append({
            "xin": shards[c], "cid": np.array([c], np.float32),
            "k0": f32("L0_ffn_kernel"), "b0": f32("L0_ffn_bias"),
            "wq": f32("L0_attn_Wq"), "wk": f32("L0_attn_Wk"),
            "wv": f32("L0_attn_Wv"), "wo": f32("L0_attn_Wo"),
            "wg1": f32("L0_attn_Wg1"), "wg2": f32("L0_attn_Wg2"),
            "ld": f32("L0_attn_log_decay"), "nw": f32("L0_attn_norm_w"),
            "qkv": f32("L1_attn_qkv"), "op": f32("L1_attn_o_proj"),
            "gamma": f32("L1_attn_gamma"), "beta": f32("L1_attn_beta"),
            "w1s": np.ascontiguousarray(w1[c * SH1:(c + 1) * SH1]),
            "w2s": np.ascontiguousarray(w2[c * SH1:(c + 1) * SH1]),
            "w3s": np.ascontiguousarray(w3[c * SH3:(c + 1) * SH3])})
    res = run_bass_kernel_spmd(nc, maps, core_ids=list(range(NCORES)))
    return _unshard_stream(
        [np.ascontiguousarray(res.results[c]["xout"]) for c in range(NCORES)]
    ).astype(np.float32)


# version 04 (final)
# speedup vs baseline: 1.0277x; 1.0264x over previous
from collections import defaultdict

import numpy as np
import ml_dtypes
import concourse.bass as bass
import concourse.mybir as mybir
from concourse.bass_utils import run_bass_kernel_spmd

F32 = mybir.dt.float32
BF16 = mybir.dt.bfloat16
AF = mybir.ActivationFunctionType
ALU = mybir.AluOpType
AX = mybir.AxisListType

MAGIC = 12582912.0
NCORES = 8
D = 2048
DC = 16
T = 1024
EPS = 1e-5

ENG_ATTR = {"v": "vector", "s": "scalar", "g": "gpsimd", "t": "tensor", "y": "sync"}


class _Op:
    __slots__ = ("fn", "waits", "semkey", "amt")
    def __init__(self, fn, waits, semkey, amt):
        self.fn, self.waits, self.semkey, self.amt = fn, waits, semkey, amt


class EM:

    def __init__(self, nc):
        self.nc = nc
        self.sems = {}
        for e in ENG_ATTR:
            self.sems[e] = nc.alloc_semaphore(f"dep_{e}")
        self.cnt = defaultdict(int)
        self.stream = {e: [] for e in ENG_ATTR}
        self.waited = {e: defaultdict(int) for e in ENG_ATTR}
        self.last_w = {}
        self.readers = defaultdict(dict)
        self.dma_sems = {}
        self.dma_last = defaultdict(int)
        self.n_ops = 0

    def _needs(self, r, w):
        deps = {}
        def add(dd):
            for sk, c in dd.items():
                deps[sk] = max(deps.get(sk, 0), c)
        for k in r:
            add(self.last_w.get(k, {}))
        for k in w:
            add(self.last_w.get(k, {}))
            add(self.readers.get(k, {}))
        return deps

    def _record(self, eng, fn, r, w, semkey, inc_amt, extra_waits=()):
        deps = self._needs(r, w)
        for sk, c in getattr(self, "pending_fence", {}).items():
            deps[sk] = max(deps.get(sk, 0), c)
        for sk, c in extra_waits:
            deps[sk] = max(deps.get(sk, 0), c)
        wl = []
        wd = self.waited[eng]
        for sk, c in sorted(deps.items()):
            if sk == "t" and eng == "t":
                continue
            if c <= wd[sk]:
                continue
            wd[sk] = c
            wl.append((sk, c))
        self.cnt[semkey] += inc_amt
        post = self.cnt[semkey]
        for k in w:
            self.last_w[k] = {semkey: post}
            self.readers[k] = {}
        for k in r:
            rd = self.readers[k]
            rd[semkey] = max(rd.get(semkey, 0), post)
        self.stream[eng].append(_Op(fn, wl, semkey, inc_amt))
        self.n_ops += 1
        return (semkey, post)

    def op(self, eng, fn, r=(), w=(), after=()):
        return self._record(eng, fn, r, w, eng, 1, after)

    def _slot_sem(self, slot):
        semk = self.dma_sems.get(slot)
        if semk is None:
            semk = f"D{len(self.dma_sems)}"
            self.sems[semk] = self.nc.alloc_semaphore(semk)
            self.dma_sems[slot] = semk
        return semk

    def dma(self, eng, fn, r=(), w=(), slot=None, after=()):
        if slot is None:
            slot = w[0] if w else f"anon{self.n_ops}"
        semk = self._slot_sem(slot)
        extra = list(after)
        if self.dma_last[semk]:
            extra.append((semk, self.dma_last[semk]))
        tok = self._record(eng, fn, r, w, semk, 16, extra)
        self.dma_last[semk] = tok[1]
        return tok

    def cc(self, fn, r=(), w=(), slot="cc", after=()):
        semk = self._slot_sem(slot)
        extra = list(after)
        if self.dma_last[semk]:
            extra.append((semk, self.dma_last[semk]))
        tok = self._record("g", fn, r, w, semk, 1, extra)
        self.dma_last[semk] = tok[1]
        return tok

    def fence(self):
        snap = {}
        for semk, c in self.cnt.items():
            if c:
                snap[semk] = c
        self.pending_fence = snap

    def alias(self, new_keys, old_keys):
        lw, rd = {}, {}
        for o in old_keys:
            for sk, c in self.last_w.get(o, {}).items():
                lw[sk] = max(lw.get(sk, 0), c)
            for sk, c in self.readers.get(o, {}).items():
                rd[sk] = max(rd.get(sk, 0), c)
        for k in new_keys:
            for sk, c in lw.items():
                d = self.last_w.setdefault(k, {})
                d[sk] = max(d.get(sk, 0), c)
            for sk, c in rd.items():
                d = self.readers[k]
                d[sk] = max(d.get(sk, 0), c)

    def finalize(self):
        nc = self.nc
        with nc.Block() as block:
            for ek, attr in ENG_ATTR.items():
                ops = self.stream[ek]
                if not ops:
                    continue
                def make_body(ops):
                    def body(eng):
                        for o in ops:
                            for sk, c in o.waits:
                                eng.wait_ge(self.sems[sk], c)
                            inst = o.fn(eng)
                            inst.then_inc(self.sems[o.semkey], o.amt)
                    return body
                getattr(block, attr)(make_body(ops))


class _Slice:
    def __init__(self, ap):
        self._ap = ap
    def __getitem__(self, key):
        if key == slice(None):
            return self._ap
        return self._ap[key]


class Ctx:
    def __init__(self, nc, em):
        self.nc, self.em = nc, em
        self.sb = {}
        self.ps = {}
        self.arena = None
        self.persist = None
        self.bump = 0

    def freeze_persist(self):
        self.persist = dict(self.sb)

    def enter_arena(self, handle, nwords):
        self.arena = handle
        self.nwords = nwords

    def phase_reset(self):
        self.sb = dict(self.persist)
        self.bump = 0
        self.em.fence()

    def sbuf(self, name, shape, dt=F32):
        if name in self.sb and self.sb[name] is not None:
            return self.sb[name]
        if self.arena is None:
            t = self.nc.alloc_sbuf_tensor("sb_" + name, list(shape), dt)
            self.sb[name] = t
            return t
        assert len(shape) == 2 and shape[0] in (1, 128), (name, shape)
        esz = 2 if dt == BF16 else 4
        nw = (shape[1] * esz + 3) // 4
        off = (self.bump + 15) & ~15
        assert off + nw <= self.nwords, f"arena overflow at {name}: {off + nw}"
        self.bump = off + nw
        base = self.arena[:]
        if dt == BF16:
            ap = base.bitcast(BF16)[0:shape[0], 2 * off:2 * off + shape[1]]
        else:
            ap = base[0:shape[0], off:off + shape[1]]
        t = _Slice(ap)
        self.sb[name] = t
        return t

    def psum(self, name, shape):
        t = self.nc.alloc_psum_tensor("ps_" + name, list(shape), F32)
        self.ps[name] = t
        return t


def setup_common(C):
    em = C.em
    C.sbuf("ones1", [1, 128])
    C.sbuf("ones128", [128, 128])
    C.sbuf("eps_pp", [128, 1])
    C.sbuf("idf", [128, 128])
    C.sbuf("idb", [128, 128], BF16)
    C.sbuf("iot_pmf", [128, 128])
    C.psum("A", [128, 1024])
    C.psum("B", [128, 1024])
    C.psum("p0", [128, 512])
    C.psum("p1", [128, 512])
    C.ps["tb0"] = C.nc.alloc_psum_tensor("ps_tb0", [128, 1024], BF16)
    C.ps["tb1"] = C.nc.alloc_psum_tensor("ps_tb1", [128, 1024], BF16)
    em.op("v", lambda v: v.memset(C.sb["ones1"][:], 1.0), w=["ones1"])
    em.op("v", lambda v: v.memset(C.sb["ones128"][:], 1.0), w=["ones128"])
    em.op("v", lambda v: v.memset(C.sb["eps_pp"][:], EPS), w=["eps_pp"])
    em.op("g", lambda g: g.iota(C.sb["iot_pmf"][:], [[-1, 128]], base=0,
                                channel_multiplier=1,
                                allow_small_or_imprecise_dtypes=True), w=["iot_pmf"])
    em.op("v", lambda v: v.tensor_single_scalar(C.sb["idf"][:], C.sb["iot_pmf"][:],
                                                0.0, ALU.is_equal),
          r=["iot_pmf"], w=["idf"])
    em.op("v", lambda v: v.tensor_copy(C.sb["idb"][:], C.sb["idf"][:]),
          r=["idf"], w=["idb"])


def x_keys():
    return [f"X{c}" for c in range(DC)]


def xview(C, cols=None):
    X = C.sb["X"]
    ap = X[:].rearrange("p (dc t) -> p dc t", dc=DC)
    if cols is not None:
        ap = ap[:, :, cols[0]:cols[1]]
    return ap


def emit_stream_load(C, xs_ap):
    em = C.em
    if "X" not in C.sb:
        C.sbuf("X", [128, DC * T])
    em.dma("y", lambda y, o=C.sb["X"], i=xs_ap: y.dma_start(
        out=o[:].rearrange("p (dc t) -> p dc t", dc=DC),
        in_=i.rearrange("(dc p) t -> p dc t", p=128)),
        w=x_keys(), slot="Xld")


def emit_stream_store(C, ys_ap):
    em = C.em
    em.dma("s", lambda g, o=ys_ap, i=C.sb["X"]: g.dma_start(
        out=o.rearrange("(dc p) t -> p dc t", p=128),
        in_=i[:].rearrange("p (dc t) -> p dc t", dc=DC)),
        r=x_keys(), w=[], slot="Xst")
    sem = em.dma_sems["Xst"]
    em.op("g", lambda g: g.engine_nop(), after=[(sem, em.cnt[sem])])


def emit_ln_stats(C):
    em, sb = C.em, C.sb
    mu = sb.get("mu") or C.sbuf("mu", [128, T])
    var = sb.get("var") or C.sbuf("var", [128, T])
    rstd = sb.get("rstd") or C.sbuf("rstd", [128, T])
    em.alias(["sq0", "sq1"], ["xng", "xng:a", "xng:b"])
    sqs = [sb["xng"][:, 0:T], sb["xng"][:, T:2 * T]]
    PSA, PSB = C.ps["A"], C.ps["B"]
    for c in range(DC):
        sqt = sqs[c % 2]
        sqk = f"sq{c % 2}"
        xv = sb["X"][:, c * T:(c + 1) * T]
        em.op("s", lambda s, o=sqt, i=xv: s.activation(o, i, AF.Square),
              r=[f"X{c}"], w=[sqk])
        for h in range(2):
            em.op("t", lambda t, c=c, h=h, i=xv: t.matmul(
                PSA[:, h * 512:(h + 1) * 512], sb["ones128"][:],
                i[:, h * 512:(h + 1) * 512], start=(c == 0), stop=(c == DC - 1)),
                r=["ones128", f"X{c}"], w=["psA"])
            em.op("t", lambda t, c=c, h=h, i=sqt: t.matmul(
                PSB[:, h * 512:(h + 1) * 512], sb["ones128"][:],
                i[:, h * 512:(h + 1) * 512], start=(c == 0), stop=(c == DC - 1)),
                r=["ones128", sqk], w=["psB"])
    em.alias(["xng"], ["sq0", "sq1"])
    em.op("v", lambda v: v.tensor_single_scalar(mu[:], PSA[:], 1.0 / D, ALU.mult),
          r=["psA"], w=["mu"])
    em.op("v", lambda v: v.tensor_single_scalar(var[:], PSB[:], 1.0 / D, ALU.mult),
          r=["psB"], w=["var"])
    em.op("v", lambda v: v.scalar_tensor_tensor(rstd[:], mu[:], -1.0, mu[:],
                                                ALU.mult, ALU.mult),
          r=["mu"], w=["rstd"])
    em.op("v", lambda v: v.tensor_add(var[:], var[:], rstd[:]),
          r=["var", "rstd"], w=["var"])
    em.op("s", lambda s: s.activation(rstd[:], var[:], AF.Ln, bias=sb["eps_pp"][:]),
          r=["var", "eps_pp"], w=["rstd"])
    em.op("s", lambda s: s.activation(rstd[:], rstd[:], AF.Exp, scale=-0.5),
          r=["rstd"], w=["rstd"])


def emit_transpose128(C, in_ap, in_keys, ps_name, out_ap, out_keys, dtype=F32,
                      evac_eng="v", evac_fn=None):
    em, sb = C.em, C.sb
    ident = sb["idf"] if dtype == F32 else sb["idb"]
    idk = "idf" if dtype == F32 else "idb"
    ps = C.ps[ps_name]
    pr, pc = in_ap.shape[0], in_ap.shape[1]
    em.op("t", lambda t, i=in_ap, d=ident: t.transpose(ps[0:pc, 0:pr], i, d[:]),
          r=list(in_keys) + [idk], w=["ps" + ps_name])
    if evac_fn is None:
        evac_fn = lambda e, o, p: e.tensor_copy(o, p)
    em.op(evac_eng, lambda e, o=out_ap: evac_fn(e, o, ps[0:pc, 0:pr]),
          r=["ps" + ps_name], w=list(out_keys))


def emit_replicate_row(C, row_ap, row_keys, out_ap, out_keys, n, ps_name="p1",
                       scale=None):
    em, sb = C.em, C.sb
    ps = C.ps[ps_name]
    for s0 in range(0, n, 512):
        w_ = min(512, n - s0)
        em.op("t", lambda t, s0=s0, w_=w_, r_=row_ap: t.matmul(
            ps[:, 0:w_], sb["ones1"][:], r_[:, s0:s0 + w_], start=True, stop=True),
            r=list(row_keys) + ["ones1"], w=["ps" + ps_name])
        if scale is None:
            em.op("v", lambda v, s0=s0, w_=w_, o=out_ap: v.tensor_copy(
                o[:, s0:s0 + w_], ps[:, 0:w_]),
                r=["ps" + ps_name], w=list(out_keys))
        else:
            em.op("v", lambda v, s0=s0, w_=w_, o=out_ap: v.tensor_single_scalar(
                o[:, s0:s0 + w_], ps[:, 0:w_], scale, ALU.mult),
                r=["ps" + ps_name], w=list(out_keys))



def alloc_l0ffn(C):
    C.sbuf("kq", [128, DC * D], BF16)
    C.sbuf("scf", [128, DC])
    C.sbuf("bq", [128, DC])
    C.sbuf("xng", [128, DC * 256])
    C.sbuf("qxb", [128, DC * 256], BF16)
    C.sbuf("yg", [128, DC * 256])
    C.sbuf("kamax", [128, DC])
    C.sbuf("kinv", [128, DC])
    C.sbuf("kinvrow", [1, D])
    for nm in ("qx", "qy"):
        C.sbuf(nm + "inv", [128, 512])
        C.sbuf(nm + "scl", [128, 512])
    C.sbuf("qqm1", [128, 512])
    C.sbuf("qqatm", [128, 4])
    C.sbuf("qqarow", [1, 512])


def emit_l0ffn_prep(C, ker_dram, bias_dram):
    em, sb = C.em, C.sb
    kq = sb["kq"]
    kerb = [sb["xng"][:, 0:D], sb["xng"][:, D:2 * D]]
    kerk = ["xng:a", "xng:b"]
    m = [sb["yg"][:, 0:D], sb["yg"][:, D:2 * D]]
    mk = ["yg:a", "yg:b"]
    scr = sb["qxb"][:].bitcast(F32)[:, 0:D]
    scrk = "qxb:pp"
    inv_rep = sb["yg"][:, 0:D]
    invrepk = "yg:a"
    amax_pm, inv_pm, invrow = sb["kamax"], sb["kinv"], sb["kinvrow"]

    kv = ker_dram.rearrange("(dc p) f -> dc p f", p=128)
    for c in range(DC):
        b, bk = kerb[c % 2], kerk[c % 2]
        em.dma("y", lambda y, c=c, b=b: y.dma_start(out=b, in_=kv[c]),
               w=[bk], slot=bk)
        dst, dstk = (m[0], mk[0]) if c == 0 else (scr, scrk)
        em.op("v", lambda v, b=b, dst=dst: v.scalar_tensor_tensor(
            dst, b, -1.0, b, ALU.mult, ALU.max), r=[bk], w=[dstk])
        if c > 0:
            src, srck = m[(c - 1) % 2], mk[(c - 1) % 2]
            dst2, dst2k = m[c % 2], mk[c % 2]
            em.op("v", lambda v, src=src, dst2=dst2: v.tensor_max(dst2, src, scr),
                  r=[srck, scrk], w=[dst2k])
    mfin, mfink = m[(DC - 1) % 2], mk[(DC - 1) % 2]
    for dc in range(DC):
        emit_transpose128(
            C, mfin[:, dc * 128:(dc + 1) * 128], [mfink], "p0",
            amax_pm[:, dc:dc + 1], ["kamax"],
            evac_fn=lambda e, o, p: e.reduce_max(o, p, axis=AX.X))
    em.op("v", lambda v: v.reciprocal(inv_pm[:], amax_pm[:]), r=["kamax"], w=["kinv"])
    em.op("v", lambda v: v.tensor_single_scalar(inv_pm[:], inv_pm[:], 127.0, ALU.mult),
          r=["kinv"], w=["kinv"])
    em.op("v", lambda v: v.tensor_single_scalar(sb["scf"][:], amax_pm[:],
                                                1.0 / 127.0, ALU.mult),
          r=["kamax"], w=["scf"])
    for dc in range(DC):
        emit_transpose128(C, inv_pm[:, dc:dc + 1], ["kinv"], "p0",
                          invrow[:, dc * 128:(dc + 1) * 128], ["kinvrow"])
    emit_replicate_row(C, invrow[:], ["kinvrow"], inv_rep, [invrepk], D)
    for c in range(DC):
        b, bk = kerb[c % 2], kerk[c % 2]
        em.dma("y", lambda y, c=c, b=b: y.dma_start(out=b, in_=kv[c]),
               w=[bk], slot=bk)
        em.op("v", lambda v, b=b: v.tensor_mul(scr, b, inv_rep),
              r=[bk, invrepk], w=[scrk])
        em.op("v", lambda v, c=c: v.tensor_scalar(
            kq[:, c * D:(c + 1) * D], scr, MAGIC, MAGIC, ALU.add, ALU.subtract),
            r=[scrk], w=[f"kq{c}"])
    bt = C.sbuf("bt", [128, DC])
    em.dma("y", lambda y: y.dma_start(
        out=bt[:], in_=bias_dram.rearrange("(dc p) -> p dc", p=128),
        allow_slow_non_contiguous=True), w=["bt"], slot="bt")
    babs = C.sbuf("babs", [128, 1])
    em.op("v", lambda v: v.reduce_max(babs[:], bt[:], axis=AX.X,
                                      apply_absolute_value=True),
          r=["bt"], w=["babs"])
    brow = C.sbuf("brow", [1, 128])
    emit_transpose128(C, babs[:], ["babs"], "p0", brow[:], ["brow"])
    bmax = C.sbuf("bmax", [1, 2])
    em.op("v", lambda v: v.reduce_max(bmax[:, 0:1], brow[:], axis=AX.X),
          r=["brow"], w=["bmax"])
    em.op("v", lambda v: v.reciprocal(bmax[:, 1:2], bmax[:, 0:1]),
          r=["bmax"], w=["bmax"])
    em.op("v", lambda v: v.tensor_single_scalar(bmax[:, 1:2], bmax[:, 1:2],
                                                127.0, ALU.mult),
          r=["bmax"], w=["bmax"])
    em.op("v", lambda v: v.tensor_single_scalar(bmax[:, 0:1], bmax[:, 0:1],
                                                1.0 / 127.0, ALU.mult),
          r=["bmax"], w=["bmax"])
    brep = C.sbuf("brep", [128, 2])
    emit_replicate_row(C, bmax[:], ["bmax"], brep[:], ["brep"], 2)
    em.op("v", lambda v: v.tensor_scalar(sb["bq"][:], bt[:], brep[:, 1:2],
                                         None, ALU.mult),
          r=["bt", "brep"], w=["bq"])
    em.op("v", lambda v: v.tensor_scalar(sb["bq"][:], sb["bq"][:], MAGIC, MAGIC,
                                         ALU.add, ALU.subtract),
          r=["bq"], w=["bq"])
    em.op("v", lambda v: v.tensor_scalar(sb["bq"][:], sb["bq"][:], brep[:, 0:1],
                                         None, ALU.mult),
          r=["bq", "brep"], w=["bq"])


def emit_quant_scales(C, src_view, src_keys, G, tag, extra_mult=1.0):
    em, sb = C.em, C.sb
    nm = tag
    m1, atm, arow = sb["qqm1"], sb["qqatm"], sb["qqarow"]
    inv, scl = sb[nm + "inv"], sb[nm + "scl"]
    ntile = G // 128
    em.op("v", lambda v, s=src_view: v.reduce_max(
        m1[:, 0:G], s.rearrange("p dc t -> p t dc"), axis=AX.X,
        apply_absolute_value=True), r=list(src_keys), w=["qqm1"])
    for i in range(ntile):
        emit_transpose128(
            C, m1[:, i * 128:(i + 1) * 128], ["qqm1"], "p0",
            atm[:, i:i + 1], ["qqatm"],
            evac_fn=lambda e, o, p: e.reduce_max(o, p, axis=AX.X))
    for i in range(ntile):
        emit_transpose128(C, atm[:, i:i + 1], ["qqatm"], "p0",
                          arow[:, i * 128:(i + 1) * 128], ["qqarow"])
    ps = C.ps["p1"]
    em.op("t", lambda t: t.matmul(ps[:, 0:G], sb["ones1"][:], arow[:, 0:G],
                                  start=True, stop=True),
          r=["qqarow", "ones1"], w=["psp1"])
    em.op("v", lambda v: v.reciprocal(inv[:, 0:G], ps[:, 0:G]),
          r=["psp1"], w=[nm + "inv"])
    em.op("v", lambda v: v.tensor_single_scalar(inv[:, 0:G], inv[:, 0:G],
                                                127.0, ALU.mult),
          r=[nm + "inv"], w=[nm + "inv"])
    em.op("v", lambda v: v.tensor_single_scalar(scl[:, 0:G], ps[:, 0:G],
                                                extra_mult / 127.0, ALU.mult),
          r=["psp1"], w=[nm + "scl"])
    return inv, scl, nm


def emit_l0ffn_body(C):
    em, sb = C.em, C.sb
    G = 256
    xng, qxb, yg = sb["xng"], sb["qxb"], sb["yg"]
    mu, rstd = sb["mu"], sb["rstd"]
    em.alias(["xng"], ["xng:a", "xng:b"])
    em.alias(["yg"], ["yg:a", "yg:b"])
    em.alias(["qxb"], ["qxb:pp"])
    for g in range(T // G):
        t0 = g * G
        xnv = xng[:].rearrange("p (dc t) -> p dc t", dc=DC)
        qxv = qxb[:].rearrange("p (dc t) -> p dc t", dc=DC)
        ygv = yg[:].rearrange("p (dc t) -> p dc t", dc=DC)
        xv = xview(C, (t0, t0 + G))
        mub = mu[:, t0:t0 + G].unsqueeze(1).broadcast_to([128, DC, G])
        rsb = rstd[:, t0:t0 + G].unsqueeze(1).broadcast_to([128, DC, G])
        em.op("v", lambda v, a=xnv, b=xv, m=mub: v.tensor_sub(a, b, m),
              r=x_keys() + ["mu"], w=["xng"])
        em.op("v", lambda v, a=xnv, m=rsb: v.tensor_mul(a, a, m),
              r=["xng", "rstd"], w=["xng"])
        inv, scl, qk = emit_quant_scales(C, xnv, ["xng"], G, "qx")
        invb = inv[:, 0:G].unsqueeze(1).broadcast_to([128, DC, G])
        em.op("v", lambda v, a=xnv, m=invb: v.tensor_mul(a, a, m),
              r=["xng", qk + "inv"], w=["xng"])
        em.op("v", lambda v, o=qxv, a=xnv: v.tensor_scalar(
            o, a, MAGIC, MAGIC, ALU.add, ALU.subtract),
            r=["xng"], w=["qxb"])
        for doc in range(DC):
            ps = C.ps[["A", "B", "p0", "p1"][doc % 4]]
            psk = ["psA", "psB", "psp0", "psp1"][doc % 4]
            for dinc in range(DC):
                em.op("t", lambda t, doc=doc, dinc=dinc, ps=ps: t.matmul(
                    ps[:, 0:G],
                    sb["kq"][:, dinc * D + doc * 128: dinc * D + doc * 128 + 128],
                    qxb[:, dinc * G:(dinc + 1) * G],
                    start=(dinc == 0), stop=(dinc == DC - 1)),
                    r=[f"kq{dinc}", "qxb"], w=[psk])
            em.op("v", lambda v, doc=doc, ps=ps, s2=scl: v.scalar_tensor_tensor(
                xng[:, doc * G:(doc + 1) * G], ps[:, 0:G],
                sb["scf"][:, doc:doc + 1], s2[:, 0:G], ALU.mult, ALU.mult),
                r=[psk, "scf", qk + "scl"], w=["xng"])
            em.op("s", lambda s, doc=doc: s.activation(
                yg[:, doc * G:(doc + 1) * G], xng[:, doc * G:(doc + 1) * G],
                AF.Relu, bias=sb["bq"][:, doc:doc + 1]),
                r=["xng", "bq"], w=["yg"])
        inv2, scl2, qk2 = emit_quant_scales(C, ygv, ["yg"], G, "qy",
                                            extra_mult=0.5)
        inv2b = inv2[:, 0:G].unsqueeze(1).broadcast_to([128, DC, G])
        scl2b = scl2[:, 0:G].unsqueeze(1).broadcast_to([128, DC, G])
        em.op("v", lambda v, o=xnv, a=ygv, m=inv2b: v.tensor_mul(o, a, m),
              r=["yg", qk2 + "inv"], w=["xng"])
        em.op("v", lambda v, a=xnv: v.tensor_scalar(
            a, a, MAGIC, MAGIC, ALU.add, ALU.subtract), r=["xng"], w=["xng"])
        em.op("v", lambda v, a=xnv, m=scl2b: v.tensor_mul(a, a, m),
              r=["xng", qk2 + "scl"], w=["xng"])
        em.op("v", lambda v, o=xv, a=xnv: v.tensor_add(o, o, a),
              r=["xng"] + x_keys(), w=x_keys())




H = 16
HD = 128
CH = 128
NCH = T // CH


def alloc_l0attn(C):
    nc = C.nc
    C.psum("tb0", [128, 1024])
    C.psum("tb1", [128, 1024])


def emit_l0attn(C, W, xs_ap, ys_ap, cid_ap, kv_dram, sl_dram, sall_dram):
    nc, em, sb = C.nc, C.em, C.sb
    G = 512
    NG = T // G
    C4 = G // CH
    xg = C.sbuf("xg", [128, DC * G])
    xnb = C.sbuf("xnbg", [128, DC * G], BF16)
    kg = C.sbuf("kg", [128, DC * G], BF16)
    vg = C.sbuf("vg", [128, DC * G], BF16)
    qg = C.sbuf("qg", [128, DC * G], BF16)
    sig = C.sbuf("sigg", [128, DC * G], BF16)
    ogb = C.sbuf("ogb", [128, DC * G], BF16)
    ogc = C.sbuf("ogc", [128, D])
    sqo = C.sbuf("sqo", [128, D])
    rstg = C.sbuf("rstg", [128, D])
    S = C.sbuf("S", [128, H * HD])
    Sbf = C.sbuf("Sbf", [128, H * HD], BF16)
    Dm = C.sbuf("Dm", [128, H * HD], BF16)
    qdec = C.sbuf("qdec", [128, H * HD], BF16)
    kdec = C.sbuf("kdec", [128, H])
    cdr = C.sbuf("cdr", [128, H])
    ldr = C.sbuf("ldr", [128, H])
    nldr = C.sbuf("nldr", [128, H])
    ldrow = C.sbuf("ldrow", [1, H])
    nwt = C.sbuf("nwt", [128, H])
    tri = C.sbuf("tri", [128, 128])
    ir = C.sbuf("irow", [1, 128])
    irep = C.sbuf("irep", [128, 128])
    i127 = C.sbuf("i127", [128, 1])
    cidr = C.sbuf("cidr", [128, 1])
    cid1 = C.sbuf("cid1", [1, 1])
    kts = C.sbuf("kts", [128, 128], BF16)
    vtile = C.sbuf("vtile", [128, 128], BF16)
    qs = C.sbuf("qsc", [128, 128], BF16)
    sdt = C.sbuf("sdt", [128, 128], BF16)
    wbuf = [C.sbuf("wb0", [128, DC * 128], BF16), C.sbuf("wb1", [128, DC * 128], BF16)]
    wg1t = C.sbuf("wg1t", [128, H * 128], BF16)
    wg2t = C.sbuf("wg2t", [128, D], BF16)
    acc = [xg[:, 0:H * HD], xg[:, H * HD:2 * H * HD]]
    stmp = xg[:, 2 * H * HD:3 * H * HD]
    xacc = C.sbuf("xacc", [128, G])
    PSA, PSB, P0, P1 = C.ps["A"], C.ps["B"], C.ps["p0"], C.ps["p1"]
    TB0, TB1 = C.ps["tb0"], C.ps["tb1"]

    emit_ln_stats_dram(C, xs_ap)
    wqv = W["wq"].rearrange("(dc p) f -> p dc f", p=128)
    wkv = W["wk"].rearrange("(dc p) f -> p dc f", p=128)
    wvv = W["wv"].rearrange("(dc p) f -> p dc f", p=128)
    wov = W["wo"].rearrange("(dc p) f -> p dc f", p=128)

    em.dma("y", lambda y: y.dma_start(out=ldrow[:], in_=W["ld"].rearrange("(o h) -> o h", o=1)),
           w=["ldrow"], slot="ldrow")
    emit_replicate_row(C, ldrow[:], ["ldrow"], ldr[:], ["ldr"], H)
    em.op("v", lambda v: v.tensor_single_scalar(nldr[:], ldr[:], -1.0, ALU.mult),
          r=["ldr"], w=["nldr"])
    em.op("s", lambda s: s.activation(cdr[:], ldr[:], AF.Exp, scale=float(CH)),
          r=["ldr"], w=["cdr"])
    em.dma("y", lambda y: y.dma_start(
        out=nwt[:], in_=W["nw"].rearrange("(dc p) -> p dc", p=128),
        allow_slow_non_contiguous=True), w=["nwt"], slot="nwt")
    em.dma("y", lambda y: y.dma_start(out=cid1[:], in_=cid_ap.rearrange("(a o) -> a o", a=1)),
           w=["cid1"], slot="cid1")
    emit_replicate_row(C, cid1[:], ["cid1"], cidr[:], ["cidr"], 1)
    em.op("g", lambda g: g.iota(ir[:], [[1, 128]], base=1, channel_multiplier=0,
                                allow_small_or_imprecise_dtypes=True), w=["irow"])
    emit_replicate_row(C, ir[:], ["irow"], irep[:], ["irep"], 128)
    em.op("g", lambda g: g.iota(i127[:], [[1, 1]], base=127, channel_multiplier=-1,
                                allow_small_or_imprecise_dtypes=True), w=["i127"])
    em.op("v", lambda v: v.tensor_single_scalar(tri[:], sb["iot_pmf"][:], 0.5,
                                                ALU.is_lt), r=["iot_pmf"], w=["tri"])
    miot = C.sbuf("miot", [128, 128])
    em.op("v", lambda v: v.tensor_scalar(miot[:], sb["iot_pmf"][:], -1.0, 0.0,
                                         ALU.mult, ALU.max),
          r=["iot_pmf"], w=["miot"])
    for h in range(H):
        em.op("s", lambda s, h=h: s.activation(
            sb["qdec"][:, h * HD:(h + 1) * HD], irep[:], AF.Exp,
            scale=ldr[:, h:h + 1]), r=["irep", "ldr"], w=["qdec"])
        em.op("s", lambda s, h=h: s.activation(
            kdec[:, h:h + 1], i127[:], AF.Exp, scale=ldr[:, h:h + 1]),
            r=["i127", "ldr"], w=["kdec"])
        em.op("s", lambda s, h=h: s.activation(
            ogc[:, h * HD:(h + 1) * HD], miot[:], AF.Exp,
            scale=ldr[:, h:h + 1]), r=["miot", "ldr"], w=["ogc:d"])
        em.op("v", lambda v, h=h: v.tensor_mul(
            Dm[:, h * HD:(h + 1) * HD], ogc[:, h * HD:(h + 1) * HD], tri[:]),
            r=["ogc:d", "tri"], w=["Dm"])
    em.alias(["ogc"], ["ogc:d"])
    em.dma("g", lambda g: g.dma_start(
        out=wg1t[:].rearrange("p (dc f) -> p dc f", dc=H),
        in_=W["wg1"].rearrange("(dc p) f -> p dc f", p=128)), w=["wg1t"], slot="wg1t")
    em.dma("g", lambda g: g.dma_start(out=wg2t[:], in_=W["wg2"]), w=["wg2t"], slot="wg2t")
    em.op("v", lambda v: v.memset(S[:], 0.0), w=["S"])

    def ln_group(g):
        t0 = g * G
        em.alias(["xg"], ["var"])
        xgv = xg[:].rearrange("p (dc t) -> p dc t", dc=DC)
        em.dma("y", lambda y, t0=t0: y.dma_start(
            out=xgv, in_=xs_ap.rearrange("(dc p) t -> p dc t", p=128)[:, :, t0:t0 + G]),
            w=["xg"], slot="xg")
        mub = sb["mu"][:, t0:t0 + G].unsqueeze(1).broadcast_to([128, DC, G])
        rsb = sb["rstd"][:, t0:t0 + G].unsqueeze(1).broadcast_to([128, DC, G])
        em.op("v", lambda v, m=mub: v.tensor_sub(xgv, xgv, m),
              r=["xg", "mu"], w=["xg"])
        em.op("v", lambda v, m=rsb: v.tensor_mul(xgv, xgv, m),
              r=["xg", "rstd"], w=["xg"])
        em.op("v", lambda v: v.tensor_copy(xnb[:], xg[:]), r=["xg"], w=["xnb"])

    def proj_fm(out_tile, out_key, w_view, wslotbase, act):
        for oc in range(DC):
            wb = wbuf[oc % 2]
            wbk = f"wb{oc % 2}"
            em.dma("g", lambda g, oc=oc, wb=wb: g.dma_start(
                out=wb[:].rearrange("p (dc f) -> p dc f", dc=DC),
                in_=w_view[:, :, oc * 128:(oc + 1) * 128]),
                w=[wbk], slot=wslotbase + str(oc % 2))
            ps = [PSA, PSB, P0, P1][oc % 4]
            psk = ["psA", "psB", "psp0", "psp1"][oc % 4]
            for dinc in range(DC):
                em.op("t", lambda t, oc=oc, dinc=dinc, wb=wb, ps=ps: t.matmul(
                    ps[:, 0:G], wb[:, dinc * 128:(dinc + 1) * 128],
                    xnb[:, dinc * G:(dinc + 1) * G],
                    start=(dinc == 0), stop=(dinc == DC - 1)),
                    r=[wbk, "xnb"], w=[psk])
            if act is None:
                em.op("s", lambda s, oc=oc, ps=ps: s.copy(
                    out_tile[:, oc * G:(oc + 1) * G], ps[:, 0:G]),
                    r=[psk], w=[out_key])
            else:
                em.op("s", lambda s, oc=oc, ps=ps, a=act: s.activation(
                    out_tile[:, oc * G:(oc + 1) * G], ps[:, 0:G], a),
                    r=[psk], w=[out_key])

    def s_chain(g, phase):
        for c4 in range(C4):
            for h in range(H):
                kv_sl = kg[:, h * G + c4 * CH: h * G + (c4 + 1) * CH]
                vv_sl = vg[:, h * G + c4 * CH: h * G + (c4 + 1) * CH]
                em.op("t", lambda t, s=kv_sl: t.transpose(
                    TB0[0:128, 0:128], s, sb["idb"][:]),
                    r=["kg", "idb"], w=["pstb0"])
                em.op("v", lambda v, h=h: v.tensor_scalar(
                    kts[:], TB0[0:128, 0:128], kdec[:, h:h + 1], None, ALU.mult),
                    r=["pstb0", "kdec"], w=["kts"])
                em.op("t", lambda t, s=vv_sl: t.transpose(
                    TB1[0:128, 0:128], s, sb["idb"][:]),
                    r=["vg", "idb"], w=["pstb1"])
                em.op("s", lambda s: s.copy(vtile[:], TB1[0:128, 0:128]),
                      r=["pstb1"], w=["vtile"])
                if phase == "B":
                    phase_b_inner(g, c4, h)
                em.op("t", lambda t: t.matmul(
                    PSA[:, 0:128], kts[:], vtile[:], start=True, stop=True),
                    r=["kts", "vtile"], w=["psA"])
                em.op("v", lambda v, h=h: v.scalar_tensor_tensor(
                    S[:, h * HD:(h + 1) * HD], S[:, h * HD:(h + 1) * HD],
                    cdr[:, h:h + 1], PSA[:, 0:128], ALU.mult, ALU.add),
                    r=["S", "cdr", "psA"], w=["S"])
                if phase == "B":
                    em.op("v", lambda v, h=h: v.tensor_copy(
                        Sbf[:, h * HD:(h + 1) * HD], S[:, h * HD:(h + 1) * HD]),
                        r=["S"], w=["Sbf"])
            if phase == "B":
                phase_b_chunk_tail(g, c4)

    def phase_b_inner(g, c4, h):
        ksl = kg[:, h * G + c4 * CH:h * G + (c4 + 1) * CH]
        qsl = qg[:, h * G + c4 * CH:h * G + (c4 + 1) * CH]
        em.op("t", lambda t, ksl=ksl, qsl=qsl: t.matmul(
            P0[:, 0:128], ksl, qsl, start=True, stop=True),
            r=["kg", "qg"], w=["psp0"])
        em.op("v", lambda v, h=h: v.tensor_mul(
            sdt[:], P0[:, 0:128], Dm[:, h * HD:(h + 1) * HD]),
            r=["psp0", "Dm"], w=["sdt"])
        em.op("v", lambda v, h=h, qsl=qsl: v.tensor_mul(
            qs[:], qsl, qdec[:, h * HD:(h + 1) * HD]),
            r=["qg", "qdec"], w=["qsc"])
        em.op("t", lambda t: t.matmul(P1[:, 0:128], vtile[:], sdt[:],
                                      start=True, stop=False),
              r=["vtile", "sdt"], w=["psp1"])
        em.op("t", lambda t, h=h: t.matmul(
            P1[:, 0:128], Sbf[:, h * HD:(h + 1) * HD], qs[:],
            start=False, stop=True), r=["Sbf", "qsc"], w=["psp1"])
        em.op("v", lambda v, h=h, g=g, c4=c4: v.tensor_mul(
            ogc[:, h * HD:(h + 1) * HD], P1[:, 0:128],
            sig[:, h * G + c4 * CH:h * G + (c4 + 1) * CH]),
            r=["psp1", "sigg"], w=["ogc"])

    def phase_b_chunk_tail(g, c4):
        em.op("s", lambda s: s.activation(sqo[:], ogc[:], AF.Square),
              r=["ogc"], w=["sqo"])
        for q in range(4):
            ps = [PSA, PSB][q // 2]
            psk = ["psA", "psB"][q // 2]
            em.op("t", lambda t, q=q, ps=ps: t.matmul(
                ps[:, (q % 2) * 512:(q % 2 + 1) * 512], sb["ones128"][:],
                sqo[:, q * 512:(q + 1) * 512], start=True, stop=True),
                r=["ones128", "sqo"], w=[psk])
        for q in range(4):
            ps = [PSA, PSB][q // 2]
            psk = ["psA", "psB"][q // 2]
            em.op("s", lambda s, q=q, ps=ps: s.activation(
                rstg[:, q * 512:(q + 1) * 512], ps[:, (q % 2) * 512:(q % 2 + 1) * 512],
                AF.Ln, bias=sb["eps_pp"][:], scale=1.0 / HD),
                r=[psk, "eps_pp"], w=["rstg"])
        em.op("s", lambda s: s.activation(rstg[:], rstg[:], AF.Exp, scale=-0.5),
              r=["rstg"], w=["rstg"])
        em.op("v", lambda v: v.tensor_mul(ogc[:], ogc[:], rstg[:]),
              r=["ogc", "rstg"], w=["ogc"])
        ogv = ogc[:].rearrange("p (h e) -> p h e", h=H)
        nb = nwt[:].unsqueeze(2).broadcast_to([128, H, HD])
        ob = ogb[:].rearrange("p (h t) -> p h t", h=H)[:, :, c4 * CH:(c4 + 1) * CH]
        em.op("v", lambda v, ob=ob: v.tensor_mul(ob, ogv, nb),
              r=["ogc", "nwt"], w=["ogb"])

    for g in range(NG):
        ln_group(g)
        proj_fm(kg, "kg", wkv, "wk", AF.Silu)
        proj_fm(vg, "vg", wvv, "wv", AF.Silu)
        s_chain(g, "A")
        em.dma("s", lambda gg, g=g: gg.dma_start(
            out=kv_dram[g * 2], in_=kg[:]), r=["kg"], w=[f"kvd{g}k"], slot="kvs")
        em.dma("s", lambda gg, g=g: gg.dma_start(
            out=kv_dram[g * 2 + 1], in_=vg[:]), r=["vg"], w=[f"kvd{g}v"], slot="kvs")

    em.dma("s", lambda g: g.dma_start(out=sl_dram, in_=S[:]), r=["S"],
           w=["sld"], slot="sld")
    em.cc(lambda g: g.collective_compute(
        "AllGather", ALU.bypass, replica_groups=[list(range(NCORES))],
        ins=[sl_dram.opt()], outs=[sall_dram.opt()]),
        r=["sld"], w=["sall"], slot="ccS")
    em.alias(["sacc0", "sacc1", "stmp"], ["xg"])
    em.op("v", lambda v: v.memset(acc[0], 0.0), w=["sacc0"])
    coef = C.sbuf("coef", [128, H])
    mj = C.sbuf("mjt", [128, 1])
    for j in range(NCORES - 1):
        em.dma("y", lambda y, j=j: y.dma_start(
            out=stmp, in_=sall_dram[j * 128:(j + 1) * 128, :]),
            r=["sall"], w=["stmp"], slot="stmp")
        em.op("v", lambda v, j=j: v.tensor_single_scalar(
            mj[:], cidr[:], float(-1 - j), ALU.add), r=["cidr"], w=["mjt"])
        em.op("v", lambda v: v.tensor_scalar(mj[:], mj[:], 1024.0, 0.0,
                                             ALU.mult, ALU.max),
              r=["mjt"], w=["mjt"])
        em.op("s", lambda s: s.activation(coef[:], ldr[:], AF.Exp, scale=mj[:]),
              r=["ldr", "mjt"], w=["coef"])
        em.op("v", lambda v, j=j: v.tensor_single_scalar(
            mj[:], cidr[:], float(j) + 0.5, ALU.is_gt), r=["cidr"], w=["mjt"])
        em.op("v", lambda v: v.tensor_scalar(coef[:], coef[:], mj[:], None,
                                             ALU.mult), r=["coef", "mjt"], w=["coef"])
        a0, a1 = acc[j % 2], acc[(j + 1) % 2]
        a0k, a1k = f"sacc{j % 2}", f"sacc{(j + 1) % 2}"
        for h in range(H):
            em.op("v", lambda v, h=h, a0=a0, a1=a1: v.scalar_tensor_tensor(
                a1[:, h * HD:(h + 1) * HD], stmp[:, h * HD:(h + 1) * HD],
                coef[:, h:h + 1], a0[:, h * HD:(h + 1) * HD], ALU.mult, ALU.add),
                r=["stmp", "coef", a0k], w=[a1k])
    afin, afink = acc[(NCORES - 1) % 2], f"sacc{(NCORES - 1) % 2}"
    em.op("v", lambda v, a=afin: v.tensor_copy(S[:], a), r=[afink], w=["S"])
    em.op("v", lambda v: v.tensor_copy(Sbf[:], S[:]), r=["S"], w=["Sbf"])
    em.alias(["xg"], ["sacc0", "sacc1", "stmp"])

    for g in range(NG):
        ln_group(g)
        em.dma("y", lambda y, g=g: y.dma_start(out=kg[:], in_=kv_dram[g * 2]),
               r=[f"kvd{g}k"], w=["kg"], slot="kgld")
        em.dma("y", lambda y, g=g: y.dma_start(out=vg[:], in_=kv_dram[g * 2 + 1]),
               r=[f"kvd{g}v"], w=["vg"], slot="vgld")
        proj_fm(qg, "qg", wqv, "wq", AF.Silu)
        for dinc in range(DC):
            em.op("t", lambda t, dinc=dinc: t.matmul(
                P0[:, 0:G], wg1t[:, dinc * 128:(dinc + 1) * 128],
                xnb[:, dinc * G:(dinc + 1) * G],
                start=(dinc == 0), stop=(dinc == DC - 1)),
                r=["wg1t", "xnb"], w=["psp0"])
        g1b = kts
        em.op("v", lambda v: v.tensor_copy(
            ogc[:].bitcast(BF16)[:, 0:G], P0[:, 0:G]), r=["psp0"], w=["g1b"])
        for oc in range(DC):
            em.op("t", lambda t, oc=oc: t.matmul(
                P1[:, 0:G], wg2t[:, oc * 128:(oc + 1) * 128],
                ogc[:].bitcast(BF16)[:, 0:G], start=True, stop=True),
                r=["wg2t", "g1b"], w=["psp1"])
            em.op("s", lambda s, oc=oc: s.activation(
                sig[:, oc * G:(oc + 1) * G], P1[:, 0:G], AF.Sigmoid),
                r=["psp1"], w=["sigg"])
        em.alias(["ogc"], ["g1b"])
        s_chain(g, "B")
        for oc in range(DC):
            ps = [PSA, PSB, P0, P1][oc % 4]
            psk = ["psA", "psB", "psp0", "psp1"][oc % 4]
            wb = wbuf[oc % 2]
            wbk = f"wb{oc % 2}"
            em.dma("g", lambda gg, oc=oc, wb=wb: gg.dma_start(
                out=wb[:].rearrange("p (dc f) -> p dc f", dc=DC),
                in_=wov[:, :, oc * 128:(oc + 1) * 128]),
                w=[wbk], slot="wo" + str(oc % 2))
            for h in range(H):
                em.op("t", lambda t, oc=oc, h=h, wb=wb, ps=ps: t.matmul(
                    ps[:, 0:G], wb[:, h * 128:(h + 1) * 128],
                    ogb[:, h * G:(h + 1) * G],
                    start=(h == 0), stop=(h == H - 1)),
                    r=[wbk, "ogb"], w=[psk])
            em.dma("y", lambda y, oc=oc, g=g: y.dma_start(
                out=xacc[:], in_=xs_ap[oc * 128:(oc + 1) * 128, g * G:(g + 1) * G]),
                w=["xacc"], slot="xacc")
            em.op("v", lambda v, ps=ps: v.tensor_add(xacc[:], xacc[:], ps[:, 0:G]),
                  r=["xacc", psk], w=["xacc"])
            em.dma("s", lambda gg, oc=oc, g=g: gg.dma_start(
                out=ys_ap[oc * 128:(oc + 1) * 128, g * G:(g + 1) * G], in_=xacc[:]),
                r=["xacc"], w=[], slot="ysst")
    sem = em.dma_sems["ysst"]
    em.op("g", lambda g: g.engine_nop(), after=[(sem, em.cnt[sem])])


def build_l0attn_program():
    nc = bass.Bass("TRN2", target_bir_lowering=False)
    em = EM(nc)
    C = Ctx(nc, em)
    xs = nc.dram_tensor("xs", [D, T], F32, kind="ExternalInput").ap()
    cid = nc.dram_tensor("cid", [1], F32, kind="ExternalInput").ap()
    W = {
        "wq": nc.dram_tensor("wq", [D, D], F32, kind="ExternalInput").ap(),
        "wk": nc.dram_tensor("wk", [D, D], F32, kind="ExternalInput").ap(),
        "wv": nc.dram_tensor("wv", [D, D], F32, kind="ExternalInput").ap(),
        "wo": nc.dram_tensor("wo", [D, D], F32, kind="ExternalInput").ap(),
        "wg1": nc.dram_tensor("wg1", [D, 128], F32, kind="ExternalInput").ap(),
        "wg2": nc.dram_tensor("wg2", [128, D], F32, kind="ExternalInput").ap(),
        "ld": nc.dram_tensor("ld", [H], F32, kind="ExternalInput").ap(),
        "nw": nc.dram_tensor("nw", [D], F32, kind="ExternalInput").ap(),
    }
    ys = nc.dram_tensor("ys", [D, T], F32, kind="ExternalOutput").ap()
    kv_dram = [nc.dram_tensor(f"kvs{i}", [128, DC * 512], BF16).ap() for i in range(4)]
    sl_dram = nc.dram_tensor("slg", [128, H * HD], F32).ap()
    sall_dram = nc.dram_tensor("sag", [NCORES * 128, H * HD], F32).ap()
    setup_common(C)
    emit_l0attn(C, W, xs, ys, cid, kv_dram, sl_dram, sall_dram)
    em.finalize()
    return nc


def emit_ln_stats_dram(C, xs_ap):
    em, sb = C.em, C.sb
    mu = sb.get("mu") or C.sbuf("mu", [128, T])
    rstd = sb.get("rstd") or C.sbuf("rstd", [128, T])
    xgt = sb["xg"]
    xc = [xgt[:, 0:T], xgt[:, T:2 * T]]
    sq = [xgt[:, 2 * T:3 * T], xgt[:, 3 * T:4 * T]]
    var = _Slice(xgt[:, 4 * T:5 * T])
    sb["var"] = var
    PSA, PSB = C.ps["A"], C.ps["B"]
    xv = xs_ap.rearrange("(dc p) t -> dc p t", p=128)
    for c in range(DC):
        xt, xk = xc[c % 2], f"lnx{c % 2}"
        st, sk = sq[c % 2], f"lnq{c % 2}"
        em.dma("y", lambda y, c=c, xt=xt: y.dma_start(out=xt, in_=xv[c]),
               w=[xk], slot=xk)
        em.op("s", lambda s, xt=xt, st=st: s.activation(st, xt, AF.Square),
              r=[xk], w=[sk])
        for h in range(2):
            em.op("t", lambda t, c=c, h=h, xt=xt: t.matmul(
                PSA[:, h * 512:(h + 1) * 512], sb["ones128"][:],
                xt[:, h * 512:(h + 1) * 512], start=(c == 0), stop=(c == DC - 1)),
                r=["ones128", xk], w=["psA"])
            em.op("t", lambda t, c=c, h=h, st=st: t.matmul(
                PSB[:, h * 512:(h + 1) * 512], sb["ones128"][:],
                st[:, h * 512:(h + 1) * 512], start=(c == 0), stop=(c == DC - 1)),
                r=["ones128", sk], w=["psB"])
    em.alias(["xg"], ["lnx0", "lnx1", "lnq0", "lnq1"])
    em.op("v", lambda v: v.tensor_single_scalar(mu[:], PSA[:], 1.0 / D, ALU.mult),
          r=["psA"], w=["mu"])
    em.op("v", lambda v: v.tensor_single_scalar(var[:], PSB[:], 1.0 / D, ALU.mult),
          r=["psB"], w=["var"])
    em.op("v", lambda v: v.scalar_tensor_tensor(rstd[:], mu[:], -1.0, mu[:],
                                                ALU.mult, ALU.mult),
          r=["mu"], w=["rstd"])
    em.op("v", lambda v: v.tensor_add(var[:], var[:], rstd[:]),
          r=["var", "rstd"], w=["var"])
    em.op("s", lambda s: s.activation(rstd[:], var[:], AF.Ln, bias=sb["eps_pp"][:]),
          r=["var", "eps_pp"], w=["rstd"])
    em.op("s", lambda s: s.activation(rstd[:], rstd[:], AF.Exp, scale=-0.5),
          r=["rstd"], w=["rstd"])


def run_l0attn(x, wq, wk, wv, wo, wg1, wg2, ld, nw):
    nc = build_l0attn_program()
    shards = _shard_stream(x)
    in_maps = []
    for c in range(NCORES):
        in_maps.append({
            "xs": shards[c], "cid": np.array([c], np.float32),
            "wq": np.ascontiguousarray(wq), "wk": np.ascontiguousarray(wk),
            "wv": np.ascontiguousarray(wv), "wo": np.ascontiguousarray(wo),
            "wg1": np.ascontiguousarray(wg1), "wg2": np.ascontiguousarray(wg2),
            "ld": np.ascontiguousarray(ld), "nw": np.ascontiguousarray(nw)})
    res = run_bass_kernel_spmd(nc, in_maps, core_ids=list(range(NCORES)))
    return _unshard_stream([res.results[c]["ys"] for c in range(NCORES)])




SCALE1 = float(1.0 / np.sqrt(np.float32(2048.0)))
NSTRIP = 16
QT = 8
QG = 4


def emit_l1attn(C, W, xs_ap, ys_ap, cid_ap, kT_part, kT_all, v_part, v_all):
    nc, em, sb = C.nc, C.em, C.sb
    G = 512
    NG = T // G
    xg = C.sbuf("xg", [128, DC * G])
    xnb = C.sbuf("xnbg", [128, DC * G], BF16)
    kg = C.sbuf("kg", [128, DC * G], BF16)
    vgT = C.sbuf("vgT", [128, DC * G], BF16)
    vtm = C.sbuf("vtm", [128, 4 * D], BF16)
    qT = C.sbuf("qTf", [128, DC * T], BF16)
    accf = _Slice(xg[:])
    ab = C.sbuf("abt", [128, 2 * T])
    gmt = C.sbuf("gmt", [128, DC])
    bet = C.sbuf("bet", [128, DC])
    wbuf = [C.sbuf("wb0", [128, DC * 128], BF16), C.sbuf("wb1", [128, DC * 128], BF16)]
    iot512 = C.sbuf("iot512", [128, 512])
    zer512 = C.sbuf("zer512", [128, 512])
    cidr = C.sbuf("cidr", [128, 1])
    cid1 = C.sbuf("cid1", [1, 1])
    c1024 = C.sbuf("c1024", [128, 1])
    cval = C.sbuf("cval", [128, 1])
    mold = C.sbuf("mold", [128, QG])
    lrun = C.sbuf("lrun", [128, QG])
    rmx = C.sbuf("rmx", [128, 4])
    pst = C.sbuf("pstl", [128, 512], BF16)
    pT = C.sbuf("pTt", [128, 512], BF16)
    rs = C.sbuf("rst", [128, 1])
    xacc = C.sbuf("xacc", [128, G])
    xsc = C.sbuf("xsc", [128, G])
    PSA, PSB, P0, P1 = C.ps["A"], C.ps["B"], C.ps["p0"], C.ps["p1"]
    TB0 = C.ps["tb0"]
    kst = [kg[:].bitcast(BF16), vgT[:].bitcast(BF16)]
    kstk = ["kg", "vgT"]
    vst = [vtm[:].bitcast(BF16), xnb[:].bitcast(BF16)]
    vstk = ["vtm", "xnb"]

    emit_ln_stats_dram(C, xs_ap)
    mu, var, rstd = sb["mu"], sb["var"], sb["rstd"]
    a_b, b_b = ab[:, 0:T], ab[:, T:2 * T]
    em.op("v", lambda v: v.tensor_mul(a_b, rstd[:], rstd[:]), r=["rstd"], w=["ab:a"])
    em.op("v", lambda v: v.tensor_mul(a_b, a_b, var[:]), r=["ab:a", "var"], w=["ab:a"])
    em.op("s", lambda s: s.activation(a_b, a_b, AF.Ln, bias=sb["eps_pp"][:]),
          r=["ab:a", "eps_pp"], w=["ab:a"])
    em.op("s", lambda s: s.activation(a_b, a_b, AF.Exp, scale=-0.5),
          r=["ab:a"], w=["ab:a"])
    em.op("v", lambda v: v.tensor_mul(a_b, a_b, rstd[:]), r=["ab:a", "rstd"], w=["ab:a"])
    em.op("v", lambda v: v.scalar_tensor_tensor(b_b, mu[:], -1.0, a_b,
                                                ALU.mult, ALU.mult),
          r=["mu", "ab:a"], w=["ab:b"])
    em.dma("y", lambda y: y.dma_start(
        out=gmt[:], in_=W["gamma"].rearrange("(dc p) -> p dc", p=128),
        allow_slow_non_contiguous=True), w=["gmt"], slot="gmt")
    em.dma("y", lambda y: y.dma_start(
        out=bet[:], in_=W["beta"].rearrange("(dc p) -> p dc", p=128),
        allow_slow_non_contiguous=True), w=["bet"], slot="bet")
    em.dma("y", lambda y: y.dma_start(out=cid1[:], in_=cid_ap.rearrange("(a o) -> a o", a=1)),
           w=["cid1"], slot="cid1")
    emit_replicate_row(C, cid1[:], ["cid1"], cidr[:], ["cidr"], 1)
    em.op("v", lambda v: v.tensor_single_scalar(c1024[:], cidr[:], 1024.0, ALU.mult),
          r=["cidr"], w=["c1024"])
    em.op("g", lambda g: g.iota(iot512[:], [[-1, 512]], base=0, channel_multiplier=1,
                                allow_small_or_imprecise_dtypes=True), w=["iot512"])
    em.op("v", lambda v: v.memset(zer512[:], 0.0), w=["zer512"])

    qkvv = W["qkv"].rearrange("(dc p) f -> p dc f", p=128)
    opv = W["op"].rearrange("(dc p) f -> p dc f", p=128)

    for g in range(NG):
        t0 = g * G
        em.alias(["xg"], ["var"])
        xgv = xg[:].rearrange("p (dc t) -> p dc t", dc=DC)
        em.dma("y", lambda y, t0=t0: y.dma_start(
            out=xgv, in_=xs_ap.rearrange("(dc p) t -> p dc t", p=128)[:, :, t0:t0 + G]),
            w=["xg"], slot="xgld")
        abb = a_b[:, t0:t0 + G].unsqueeze(1).broadcast_to([128, DC, G])
        bbb = b_b[:, t0:t0 + G].unsqueeze(1).broadcast_to([128, DC, G])
        em.op("v", lambda v, m=abb: v.tensor_mul(xgv, xgv, m),
              r=["xg", "ab:a"], w=["xg"])
        em.op("v", lambda v, m=bbb: v.tensor_add(xgv, xgv, m),
              r=["xg", "ab:b"], w=["xg"])
        for dc in range(DC):
            em.op("v", lambda v, dc=dc: v.scalar_tensor_tensor(
                xnb[:, dc * G:(dc + 1) * G], xg[:, dc * G:(dc + 1) * G],
                gmt[:, dc:dc + 1], bet[:, dc:dc + 1].broadcast_to([128, G]),
                ALU.mult, ALU.add), r=["xg", "gmt", "bet"], w=["xnb"])
        for (dst, dstk, off, slotb) in ((qT, "qTf", 0, "q"), (kg, "kg", D, "k"),
                                        (vgT, "vgT", 2 * D, "v")):
            for oc in range(DC):
                wb = wbuf[oc % 2]
                wbk = f"wb{oc % 2}"
                em.dma("g", lambda gg, oc=oc, wb=wb, off=off: gg.dma_start(
                    out=wb[:].rearrange("p (dc f) -> p dc f", dc=DC),
                    in_=qkvv[:, :, off + oc * 128:off + (oc + 1) * 128]),
                    w=[wbk], slot=slotb + str(oc % 2))
                ps = [PSA, PSB, P0, P1][oc % 4]
                psk = ["psA", "psB", "psp0", "psp1"][oc % 4]
                for dinc in range(DC):
                    em.op("t", lambda t, oc=oc, dinc=dinc, wb=wb, ps=ps: t.matmul(
                        ps[:, 0:G], wb[:, dinc * 128:(dinc + 1) * 128],
                        xnb[:, dinc * G:(dinc + 1) * G],
                        start=(dinc == 0), stop=(dinc == DC - 1)),
                        r=[wbk, "xnb"], w=[psk])
                if dst is qT:
                    em.op("s", lambda s, oc=oc, ps=ps, t0=t0: s.copy(
                        qT[:, oc * T + t0:oc * T + t0 + G], ps[:, 0:G]),
                        r=[psk], w=["qTf"])
                else:
                    em.op("s", lambda s, oc=oc, ps=ps, dst=dst: s.copy(
                        dst[:, oc * G:(oc + 1) * G], ps[:, 0:G]),
                        r=[psk], w=[dstk])
        em.dma("s", lambda gg, t0=t0: gg.dma_start(
            out=kT_part.rearrange("(dc p) t -> p dc t", p=128)[:, :, t0:t0 + G],
            in_=kg[:].rearrange("p (dc t) -> p dc t", dc=DC)),
            r=["kg"], w=["ktp"], slot="ktp")
        for c4 in range(4):
            for oc in range(DC):
                em.op("t", lambda t, c4=c4, oc=oc: t.transpose(
                    TB0[0:128, 0:128],
                    vgT[:, oc * G + c4 * 128:oc * G + (c4 + 1) * 128], sb["idb"][:]),
                    r=["vgT", "idb"], w=["pstb0"])
                em.op("s", lambda s, c4=c4, oc=oc: s.copy(
                    vtm[:, c4 * D + oc * 128:c4 * D + (oc + 1) * 128],
                    TB0[0:128, 0:128]), r=["pstb0"], w=["vtm"])
        em.dma("s", lambda gg, g=g: gg.dma_start(
            out=v_part[g * G:(g + 1) * G, :].rearrange("(c4 p) f -> p c4 f", p=128),
            in_=vtm[:].rearrange("p (c4 f) -> p c4 f", c4=4)),
            r=["vtm"], w=["vtp"], slot="vtp")
    em.cc(lambda g: g.collective_compute(
        "AllGather", ALU.bypass, replica_groups=[list(range(NCORES))],
        ins=[kT_part.opt()], outs=[kT_all.opt()]), r=["ktp"], w=["ktall"], slot="ccK")
    em.cc(lambda g: g.collective_compute(
        "AllGather", ALU.bypass, replica_groups=[list(range(NCORES))],
        ins=[v_part.opt()], outs=[v_all.opt()]), r=["vtp"], w=["vall"], slot="ccV")

    em.alias(kstk + vstk, ["kg", "vgT", "vtm", "xnb"])
    em.alias(["accf"], ["xg"])
    for half in range(2):
        em.op("v", lambda v: v.memset(accf[:], 0.0), w=["accf"])
        em.op("v", lambda v: v.memset(mold[:], -3.0e38), w=["mold"])
        em.op("v", lambda v: v.memset(lrun[:], 0.0), w=["lrun"])
        for s_ in range(NSTRIP):
            kb, kbk = kst[s_ % 2], kstk[s_ % 2]
            vb, vbk = vst[s_ % 2], vstk[s_ % 2]
            slab, col = s_ // 2, (s_ % 2) * G
            em.dma("y", lambda y, slab=slab, col=col, kb=kb: y.dma_start(
                out=kb.rearrange("p (dc t) -> p dc t", dc=DC),
                in_=kT_all[slab * D:(slab + 1) * D, :].rearrange(
                    "(dc p) t -> p dc t", p=128)[:, :, col:col + G]),
                r=["ktall"], w=[kbk], slot="k" + str(s_ % 2) + "ld")
            em.dma("y", lambda y, s_=s_, vb=vb: y.dma_start(
                out=vb.rearrange("p (c4 f) -> p c4 f", c4=4),
                in_=v_all[s_ * G:(s_ + 1) * G, :].rearrange("(c4 p) f -> p c4 f", p=128)),
                r=["vall"], w=[vbk], slot="v" + str(s_ % 2) + "ld")
            for qi in range(QG):
                qt = half * QG + qi
                for dc in range(DC):
                    em.op("t", lambda t, dc=dc, qt=qt, kb=kb: t.matmul(
                        P0[:, 0:G], qT[:, dc * T + qt * 128:dc * T + (qt + 1) * 128],
                        kb[:, dc * G:(dc + 1) * G],
                        start=(dc == 0), stop=(dc == DC - 1)),
                        r=["qTf", kbk], w=["psp0"])
                em.op("v", lambda v, qt=qt, s_=s_: v.tensor_single_scalar(
                    cval[:], c1024[:], float(128 * qt - 512 * s_), ALU.add),
                    r=["c1024"], w=["cval"])
                em.op("v", lambda v: v.scalar_tensor_tensor(
                    pst[:].bitcast(F32)[:, 0:256].rearrange("p (a b) -> p a b", a=1).squeeze(1) if False else sb["qqmask"][:],
                    iot512[:], cval[:], zer512[:], ALU.add, ALU.is_lt),
                    r=["iot512", "cval", "zer512"], w=["qqmask"])
                em.op("v", lambda v: v.scalar_tensor_tensor(
                    P0[:, 0:G], sb["qqmask"][:], -1.0e30, P0[:, 0:G],
                    ALU.mult, ALU.add), r=["qqmask", "psp0"], w=["psp0"])
                em.op("v", lambda v: v.reduce_max(rmx[:, 0:1], P0[:, 0:G], axis=AX.X),
                      r=["psp0"], w=["rmx"])
                em.op("v", lambda v, qi=qi: v.tensor_max(
                    rmx[:, 1:2], rmx[:, 0:1], mold[:, qi:qi + 1]),
                      r=["rmx", "mold"], w=["rmx"])
                em.op("v", lambda v, qi=qi: v.tensor_sub(
                    rmx[:, 2:3], mold[:, qi:qi + 1], rmx[:, 1:2]),
                      r=["rmx", "mold"], w=["rmx"])
                em.op("s", lambda s: s.activation(rmx[:, 2:3], rmx[:, 2:3],
                                                  AF.Exp, scale=SCALE1),
                      r=["rmx"], w=["rmx"])
                em.op("v", lambda v: v.tensor_single_scalar(
                    rmx[:, 3:4], rmx[:, 1:2], -SCALE1, ALU.mult),
                    r=["rmx"], w=["rmx"])
                em.op("v", lambda v, qi=qi: v.tensor_copy(
                    mold[:, qi:qi + 1], rmx[:, 1:2]), r=["rmx"], w=["mold"])
                em.op("s", lambda s: s.activation(
                    pst[:], P0[:, 0:G], AF.Exp, bias=rmx[:, 3:4], scale=SCALE1),
                    r=["psp0", "rmx"], w=["pstl"])
                em.op("v", lambda v: v.reduce_sum(rs[:], pst[:], axis=AX.X),
                      r=["pstl"], w=["rst"])
                em.op("v", lambda v, qi=qi: v.scalar_tensor_tensor(
                    lrun[:, qi:qi + 1], lrun[:, qi:qi + 1], rmx[:, 2:3], rs[:],
                    ALU.mult, ALU.add), r=["lrun", "rmx", "rst"], w=["lrun"])
                for jt in range(4):
                    em.op("t", lambda t, jt=jt: t.transpose(
                        TB0[0:128, 0:128], pst[:, jt * 128:(jt + 1) * 128],
                        sb["idb"][:]), r=["pstl", "idb"], w=["pstb0"])
                    em.op("s", lambda s, jt=jt: s.copy(
                        pT[:, jt * 128:(jt + 1) * 128], TB0[0:128, 0:128]),
                        r=["pstb0"], w=["pTt"])
                for fs in range(4):
                    for jt in range(4):
                        em.op("t", lambda t, fs=fs, jt=jt, vb=vb: t.matmul(
                            P1[:, 0:G], pT[:, jt * 128:(jt + 1) * 128],
                            vb[:, jt * D + fs * G:jt * D + (fs + 1) * G],
                            start=(jt == 0), stop=(jt == 3)),
                            r=["pTt", vbk], w=["psp1"])
                    em.op("v", lambda v, fs=fs, qi=qi: v.scalar_tensor_tensor(
                        accf[:, qi * D + fs * G:qi * D + (fs + 1) * G],
                        accf[:, qi * D + fs * G:qi * D + (fs + 1) * G],
                        rmx[:, 2:3], P1[:, 0:G], ALU.mult, ALU.add),
                        r=["accf", "rmx", "psp1"], w=["accf"])
        em.op("v", lambda v: v.reciprocal(rmx[:, 0:1].broadcast_to([128, 1]) if False else sb["linv"][:], lrun[:]),
              r=["lrun"], w=["linv"])
        for qi in range(QG):
            qt = half * QG + qi
            em.op("v", lambda v, qi=qi: v.tensor_scalar(
                accf[:, qi * D:(qi + 1) * D], accf[:, qi * D:(qi + 1) * D],
                sb["linv"][:, qi:qi + 1], None, ALU.mult),
                r=["accf", "linv"], w=["accf"])
            if half == 0 and qi == 0:
                em.alias(["accT"], ["qTf"])
            for dc in range(DC):
                em.op("t", lambda t, qi=qi, dc=dc: t.transpose(
                    P0[:, 0:128], accf[:, qi * D + dc * 128:qi * D + (dc + 1) * 128],
                    sb["idf"][:]), r=["accf", "idf"], w=["psp0"])
                em.op("s", lambda s, qt=qt, dc=dc: s.copy(
                    qT[:].bitcast(BF16)[:, 0:DC * T].rearrange(
                        "p (dc t) -> p dc t", dc=DC)[:, dc, qt * 128:(qt + 1) * 128],
                    P0[:, 0:128]), r=["psp0"], w=["accT"])
    accT = qT[:].bitcast(BF16)[:, 0:DC * T]
    for g in range(NG):
        t0 = g * G
        for oc in range(DC):
            wb = wbuf[oc % 2]
            wbk = f"wb{oc % 2}"
            em.dma("g", lambda gg, oc=oc, wb=wb: gg.dma_start(
                out=wb[:].rearrange("p (dc f) -> p dc f", dc=DC),
                in_=opv[:, :, oc * 128:(oc + 1) * 128]),
                w=[wbk], slot="op" + str(oc % 2))
            ps = [PSA, PSB, P0, P1][oc % 4]
            psk = ["psA", "psB", "psp0", "psp1"][oc % 4]
            for dc in range(DC):
                em.op("t", lambda t, oc=oc, dc=dc, wb=wb, ps=ps, t0=t0: t.matmul(
                    ps[:, 0:G], wb[:, dc * 128:(dc + 1) * 128],
                    accT.rearrange("p (dc t) -> p dc t", dc=DC)[:, dc, t0:t0 + G],
                    start=(dc == 0), stop=(dc == DC - 1)),
                    r=[wbk, "accT"], w=[psk])
            em.dma("y", lambda y, oc=oc, t0=t0: y.dma_start(
                out=xsc[:], in_=xs_ap[oc * 128:(oc + 1) * 128, t0:t0 + G]),
                w=["xsc"], slot="xsc")
            em.op("v", lambda v, t0=t0: v.tensor_sub(
                xacc[:], xsc[:], mu[:, t0:t0 + G]), r=["xsc", "mu"], w=["xacc"])
            em.op("v", lambda v, t0=t0: v.tensor_mul(
                xacc[:], xacc[:], rstd[:, t0:t0 + G]), r=["xacc", "rstd"], w=["xacc"])
            em.op("v", lambda v: v.tensor_add(xacc[:], xacc[:], xsc[:]),
                  r=["xacc", "xsc"], w=["xacc"])
            em.op("v", lambda v, ps=ps: v.tensor_add(xacc[:], xacc[:], ps[:, 0:G]),
                  r=["xacc", psk], w=["xacc"])
            em.dma("s", lambda gg, oc=oc, t0=t0: gg.dma_start(
                out=ys_ap[oc * 128:(oc + 1) * 128, t0:t0 + G], in_=xacc[:]),
                r=["xacc"], w=[], slot="ysst")
    sem = em.dma_sems["ysst"]
    em.op("g", lambda g: g.engine_nop(), after=[(sem, em.cnt[sem])])


def build_l1attn_program():
    nc = bass.Bass("TRN2", target_bir_lowering=False)
    em = EM(nc)
    C = Ctx(nc, em)
    xs = nc.dram_tensor("xs", [D, T], F32, kind="ExternalInput").ap()
    cid = nc.dram_tensor("cid", [1], F32, kind="ExternalInput").ap()
    W = {
        "qkv": nc.dram_tensor("qkv", [D, 3 * D], F32, kind="ExternalInput").ap(),
        "op": nc.dram_tensor("op", [D, D], F32, kind="ExternalInput").ap(),
        "gamma": nc.dram_tensor("gamma", [D], F32, kind="ExternalInput").ap(),
        "beta": nc.dram_tensor("beta", [D], F32, kind="ExternalInput").ap(),
    }
    ys = nc.dram_tensor("ys", [D, T], F32, kind="ExternalOutput").ap()
    kT_part = nc.dram_tensor("ktp", [D, T], BF16).ap()
    kT_all = nc.dram_tensor("kta", [NCORES * D, T], BF16).ap()
    v_part = nc.dram_tensor("vtp", [T, D], BF16).ap()
    v_all = nc.dram_tensor("vta", [NCORES * T, D], BF16).ap()
    setup_common(C)
    C.sbuf("qqmask", [128, 512])
    C.sbuf("linv", [128, QG])
    emit_l1attn(C, W, xs, ys, cid, kT_part, kT_all, v_part, v_all)
    em.finalize()
    return nc


def run_l1attn(x, qkv, op, gamma, beta):
    nc = build_l1attn_program()
    shards = _shard_stream(x)
    in_maps = []
    for c in range(NCORES):
        in_maps.append({
            "xs": shards[c], "cid": np.array([c], np.float32),
            "qkv": np.ascontiguousarray(qkv), "op": np.ascontiguousarray(op),
            "gamma": np.ascontiguousarray(gamma), "beta": np.ascontiguousarray(beta)})
    res = run_bass_kernel_spmd(nc, in_maps, core_ids=list(range(NCORES)))
    return _unshard_stream([res.results[c]["ys"] for c in range(NCORES)])





NF4_BF = [float(np.float32(ml_dtypes.bfloat16(v))) for v in
          [-1.0, -0.6962, -0.5251, -0.3949, -0.2844, -0.1848, -0.0911, 0.0,
           0.0796, 0.1609, 0.2461, 0.3379, 0.4407, 0.5626, 0.723, 1.0]]
NF4_MID = [(NF4_BF[i] + NF4_BF[i + 1]) / 2.0 for i in range(15)]
NF4_DELTA = [NF4_BF[i + 1] - NF4_BF[i] for i in range(15)]
DFF = 5632
FC = DFF // 128
SH1 = DFF // NCORES
SH3 = D // NCORES
NBLK = 180224
PEL = SH1 * D // 128
NC64 = PEL // 64
NSB = 88


def emit_l1ffn_prep(C, wi, wshd, wqd, wtp, s1b, deqb, mrb_in, mrb_out, SHR, C_):
    nc, em, sb = C.nc, C.em, C.sb
    wsh, scb, val = sb["wsh"], sb["scb"], sb["valt"]
    s1, s1r, deql = sb["s1t"], sb["s1r"], sb["deql"]
    sbt, sbb, sbc = sb["sbt"], sb["sbb"], sb["sbc"]
    qf2, mpp = sb["qf2"], sb["mpp"]
    rowsum, msum, m11 = sb["rowsum"], sb["msum"], sb["m11"]
    stg, ttin = sb["stg"], sb["ttin"]
    P0, TB0 = C.ps["p0"], C.ps["tb0"]
    w = f"w{wi}:"

    em.dma("y", lambda y: y.dma_start(
        out=wsh[:, 0:PEL], in_=wshd.flatten().rearrange("(p i) -> p i", p=128)),
        w=["wsh"], slot="wshld")
    em.op("v", lambda v: v.reduce_max(
        s1[:, 0:NC64], wsh[:, 0:PEL].rearrange("p (b e) -> p b e", e=64),
        axis=AX.X, apply_absolute_value=True), r=["wsh"], w=["s1"])
    em.op("v", lambda v: v.reduce_sum(rowsum[:, 0:1], s1[:, 0:NC64], axis=AX.X),
          r=["s1"], w=["rs"])
    em.op("v", lambda v: v.memset(msum[:], 0.0), w=["ms"])
    em.op("t", lambda t: t.matmul(P0[0:1, 0:1], sb["ones128"][:, 0:1],
                                  rowsum[:, 0:1], start=True, stop=True),
          r=["ones128", "rs"], w=["psp0"])
    em.op("v", lambda v: v.tensor_copy(msum[:, 0:1], P0[0:1, 0:1]),
          r=["psp0"], w=["ms"])
    em.dma("s", lambda g: g.dma_start(out=mrb_in, in_=msum[:]),
           r=["ms"], w=[w + "mrb"], slot="mrb")
    em.cc(lambda g: g.collective_compute(
        "AllReduce", ALU.add, replica_groups=[list(range(NCORES))],
        ins=[mrb_in.opt()], outs=[mrb_out.opt()]),
        r=[w + "mrb"], w=[w + "mro"], slot="ccM")
    em.dma("y", lambda y: y.dma_start(out=msum[:], in_=mrb_out),
           r=[w + "mro"], w=["ms"], slot="mrb2")
    em.op("v", lambda v: v.tensor_single_scalar(
        m11[:].bitcast(BF16)[:, 0:1], msum[:, 0:1], 1.0 / NBLK, ALU.mult),
        r=["ms"], w=["m11"])
    em.op("v", lambda v: v.tensor_copy(m11[:, 1:2], m11[:].bitcast(BF16)[:, 0:1]),
          r=["m11"], w=["m11"])
    emit_replicate_row(C, m11[:, 1:2], ["m11"], mpp[:], ["mpp"], 1)
    em.dma("s", lambda g: g.dma_start(out=s1b[0:128 * NC64], in_=s1[:, 0:NC64]),
           r=["s1"], w=[w + "s1b"], slot="s1b")
    em.dma("y", lambda y: y.dma_start(
        out=sbt[0:NSB, :], in_=s1b.rearrange("(s j) -> s j", j=256)[0:NSB, :]),
        r=[w + "s1b"], w=["sbt"], slot="sbt")
    em.op("v", lambda v: v.tensor_scalar(sbb[0:NSB, :], sbt[0:NSB, :],
                                         mpp[0:NSB, 0:1], None, ALU.subtract),
          r=["sbt", "mpp"], w=["sbb"])
    em.op("v", lambda v: v.reduce_max(qf2[0:NSB, 0:1], sbb[0:NSB, :], axis=AX.X,
                                      apply_absolute_value=True),
          r=["sbb"], w=["qf2"])
    em.op("v", lambda v: v.reciprocal(qf2[0:NSB, 1:2], qf2[0:NSB, 0:1]),
          r=["qf2"], w=["qf2"])
    em.op("v", lambda v: v.tensor_single_scalar(
        qf2[:].bitcast(BF16)[0:NSB, 4:5], qf2[0:NSB, 1:2], 128.0, ALU.mult),
        r=["qf2"], w=["qf2"])
    em.op("v", lambda v: v.tensor_copy(qf2[0:NSB, 2:3],
                                       qf2[:].bitcast(BF16)[0:NSB, 4:5]),
          r=["qf2"], w=["qf2"])
    em.op("v", lambda v: v.reciprocal(qf2[0:NSB, 3:4], qf2[0:NSB, 2:3]),
          r=["qf2"], w=["qf2"])
    em.op("v", lambda v: v.tensor_scalar(sbc[0:NSB, :], sbb[0:NSB, :],
                                         qf2[0:NSB, 2:3], None, ALU.mult),
          r=["sbb", "qf2"], w=["sbc"])
    em.op("v", lambda v: v.tensor_scalar(sbt[0:NSB, :], sbc[0:NSB, :],
                                         MAGIC, MAGIC, ALU.add, ALU.subtract),
          r=["sbc"], w=["sbt"])
    em.op("v", lambda v: v.tensor_scalar(sbt[0:NSB, :], sbt[0:NSB, :],
                                         127.0, -128.0, ALU.min, ALU.max),
          r=["sbt"], w=["sbt"])
    em.op("v", lambda v: v.tensor_scalar(sbc[0:NSB, :], sbt[0:NSB, :],
                                         qf2[0:NSB, 3:4], None, ALU.mult),
          r=["sbt", "qf2"], w=["sbc"])
    em.op("v", lambda v: v.tensor_scalar(sbb[0:NSB, :], sbc[0:NSB, :],
                                         mpp[0:NSB, 0:1], None, ALU.add),
          r=["sbc", "mpp"], w=["sbb"])
    em.dma("s", lambda g: g.dma_start(
        out=deqb.rearrange("(s j) -> s j", j=256)[0:NSB, :], in_=sbb[0:NSB, :]),
        r=["sbb"], w=[w + "deqb"], slot="deqb")
    em.dma("y", lambda y: y.dma_start(
        out=deql[:, 0:NC64], in_=deqb.rearrange("(p i) -> p i", p=128)),
        r=[w + "deqb"], w=["deql"], slot="deql")
    em.op("v", lambda v: v.reciprocal(s1r[:, 0:NC64], s1[:, 0:NC64]),
          r=["s1"], w=["s1r"])
    em.op("v", lambda v: v.tensor_mul(
        scb[:, 0:PEL].rearrange("p (b e) -> p b e", e=64),
        wsh[:, 0:PEL].rearrange("p (b e) -> p b e", e=64),
        s1r[:, 0:NC64].unsqueeze(2).broadcast_to([128, NC64, 64])),
        r=["wsh", "s1r"], w=["scb"])
    em.op("v", lambda v: v.memset(val[:, 0:PEL], NF4_BF[0]), w=["val"])
    for k in range(15):
        em.op("g", lambda g, k=k: g.tensor_scalar(
            sb["indt"][:, 0:PEL], scb[:, 0:PEL], NF4_MID[k], None, ALU.is_gt),
            r=["scb"], w=["ind"])
        em.op("v", lambda v, k=k: v.scalar_tensor_tensor(
            val[:, 0:PEL], sb["indt"][:, 0:PEL], NF4_DELTA[k], val[:, 0:PEL],
            ALU.mult, ALU.add), r=["ind", "val"], w=["val"])
    em.op("v", lambda v: v.tensor_mul(
        wsh[:, 0:PEL].rearrange("p (b e) -> p b e", e=64),
        val[:, 0:PEL].rearrange("p (b e) -> p b e", e=64),
        deql[:, 0:NC64].unsqueeze(2).broadcast_to([128, NC64, 64])),
        r=["val", "deql", "wsh"], w=["wsh"])
    em.dma("s", lambda g: g.dma_start(
        out=wqd.flatten().rearrange("(p i) -> p i", p=128), in_=wsh[:, 0:PEL]),
        r=["wsh"], w=[w + "wqd"], slot="wqd")
    nrt = (SHR + 127) // 128
    for ct in range(C_ // 128):
        for rt in range(nrt):
            rows = min(128, SHR - rt * 128)
            em.dma("y", lambda y, ct=ct, rt=rt, rows=rows: y.dma_start(
                out=ttin[0:rows, :],
                in_=wqd[rt * 128:rt * 128 + rows, ct * 128:(ct + 1) * 128]),
                r=[w + "wqd"], w=["ttin"], slot="ttin" + str(rt % 2))
            em.op("t", lambda t, rows=rows: t.transpose(
                TB0[0:128, 0:rows], ttin[0:rows, :], sb["idb"][0:rows, 0:rows]),
                r=["ttin", "idb"], w=["pstb0"])
            em.op("s", lambda s, rt=rt, rows=rows: s.copy(
                stg[:, rt * 128:rt * 128 + rows], TB0[0:128, 0:rows]),
                r=["pstb0"], w=["stg"])
        em.dma("s", lambda g, ct=ct: g.dma_start(
            out=wtp[ct * 128:(ct + 1) * 128, :], in_=stg[:, 0:SHR]),
            r=["stg"], w=[w + "wtp"], slot="wtpst")
    return w + "wtp"


def emit_l1ffn_body(C, w1a, w2a, w3a, xs_ap, ys_ap, first_store=True):
    nc, em, sb = C.nc, C.em, C.sb
    G = 512
    NG = T // G
    em.alias(["xg"], ["wsh"])
    em.alias(["hbuf"], ["scb", "ind"])
    em.alias(["w3b0", "w3b1"], ["val"])
    xg, xnb = sb["xg"], sb["xnbg"]
    hbuf = sb["hbuf"]
    hsc = sb["hsc"]
    w12b = [sb["w12b0"], sb["w12b1"], sb["w12b2"], sb["w12b3"]]
    w3b = [sb["w3b0"], sb["w3b1"]]
    xacc, xsc = sb["xacc"], sb["xsc"]
    PSA, PSB = C.ps["A"], C.ps["B"]
    P0x, P1x = C.ps["p0"], C.ps["p1"]
    emit_ln_stats_dram(C, xs_ap)
    mu, rstd = sb["mu"], sb["rstd"]

    def seg_loads(dst, dstk, slot, wta, fc, SHR, rkey):
        f0, f1 = fc * 128, (fc + 1) * 128
        segs = []
        f = f0
        while f < f1:
            slab = min(f // SH1, 7)
            lo = SH1 * slab
            hi = SH1 * (slab + 1) if slab < 7 else DFF
            end = min(f1, hi)
            segs.append((slab, f - lo, f - f0, end - f))
            f = end
        for (slab, loc, off, wd) in segs:
            em.dma("y", lambda y, slab=slab, loc=loc, off=off, wd=wd, wta=wta: y.dma_start(
                out=dst[:].rearrange("p (dc f) -> p dc f", dc=DC)[:, :, off:off + wd],
                in_=wta[slab * D:(slab + 1) * D, :].rearrange(
                    "(dc p) s -> p dc s", p=128)[:, :, loc:loc + wd]),
                r=[rkey], w=[dstk], slot=slot)

    for g in range(NG):
        t0 = g * G
        em.alias(["xg"], ["var"])
        xgv = xg[:].rearrange("p (dc t) -> p dc t", dc=DC)
        em.dma("y", lambda y, t0=t0: y.dma_start(
            out=xgv, in_=xs_ap.rearrange("(dc p) t -> p dc t", p=128)[:, :, t0:t0 + G]),
            w=["xg"], slot="xgld")
        mub = mu[:, t0:t0 + G].unsqueeze(1).broadcast_to([128, DC, G])
        rsb = rstd[:, t0:t0 + G].unsqueeze(1).broadcast_to([128, DC, G])
        em.op("v", lambda v, m=mub: v.tensor_sub(xgv, xgv, m), r=["xg", "mu"], w=["xg"])
        em.op("v", lambda v, m=rsb: v.tensor_mul(xgv, xgv, m), r=["xg", "rstd"], w=["xg"])
        em.op("v", lambda v: v.tensor_copy(xnb[:], xg[:]), r=["xg"], w=["xnb"])
        for fc in range(FC):
            b1, b1k = w12b[fc % 2], f"w12b{fc % 2}"
            b2, b2k = w12b[2 + fc % 2], f"w12b{2 + fc % 2}"
            seg_loads(b1, b1k, "w1l" + str(fc % 2), w1a, fc, SH1, "w1:wta")
            seg_loads(b2, b2k, "w2l" + str(fc % 2), w2a, fc, SH1, "w2:wta")
            ps1, ps1k = [PSA, PSB, P0x, P1x][fc % 4], ["psA", "psB", "psp0", "psp1"][fc % 4]
            for dc in range(DC):
                em.op("t", lambda t, fc=fc, dc=dc, b1=b1, ps=ps1: t.matmul(
                    ps[:, 0:G], b1[:, dc * 128:(dc + 1) * 128],
                    xnb[:, dc * G:(dc + 1) * G],
                    start=(dc == 0), stop=(dc == DC - 1)),
                    r=[b1k, "xnb"], w=[ps1k])
            em.op("s", lambda s, fc=fc, ps=ps1: s.activation(
                hsc[:, 0:G], ps[:, 0:G], AF.Silu), r=[ps1k], w=["hsc"])
            for dc in range(DC):
                em.op("t", lambda t, fc=fc, dc=dc, b2=b2, ps=ps1: t.matmul(
                    ps[:, 0:G], b2[:, dc * 128:(dc + 1) * 128],
                    xnb[:, dc * G:(dc + 1) * G],
                    start=(dc == 0), stop=(dc == DC - 1)),
                    r=[b2k, "xnb"], w=[ps1k])
            em.op("v", lambda v, fc=fc, ps=ps1: v.tensor_mul(
                hbuf[:, fc * G:(fc + 1) * G], hsc[:, 0:G], ps[:, 0:G]),
                r=["hsc", ps1k], w=["hbuf"])
        for oc in range(DC):
            b3, b3k = w3b[oc % 2], f"w3b{oc % 2}"
            slab, loc = oc // 2, (oc % 2) * 128
            em.dma("y", lambda y, slab=slab, loc=loc, b3=b3, w3a=w3a: y.dma_start(
                out=b3[:].rearrange("p (fc f) -> p fc f", fc=FC),
                in_=w3a[slab * DFF:(slab + 1) * DFF, :].rearrange(
                    "(fc p) s -> p fc s", p=128)[:, :, loc:loc + 128]),
                r=["w3:wta"], w=[b3k], slot="w3l" + str(oc % 2))
            ps1, ps1k = [PSA, PSB, P0x, P1x][oc % 4], ["psA", "psB", "psp0", "psp1"][oc % 4]
            for fc in range(FC):
                em.op("t", lambda t, oc=oc, fc=fc, b3=b3, ps=ps1: t.matmul(
                    ps[:, 0:G], b3[:, fc * 128:(fc + 1) * 128],
                    hbuf[:, fc * G:(fc + 1) * G],
                    start=(fc == 0), stop=(fc == FC - 1)),
                    r=[b3k, "hbuf"], w=[ps1k])
            em.dma("y", lambda y, oc=oc, t0=t0: y.dma_start(
                out=xsc[:], in_=xs_ap[oc * 128:(oc + 1) * 128, t0:t0 + G]),
                w=["xsc"], slot="xsc")
            em.op("v", lambda v, ps=ps1: v.scalar_tensor_tensor(
                xacc[:], ps[:, 0:G], 0.5, xsc[:], ALU.mult, ALU.add),
                r=[ps1k, "xsc"], w=["xacc"])
            em.dma("s", lambda gg, oc=oc, t0=t0: gg.dma_start(
                out=ys_ap[oc * 128:(oc + 1) * 128, t0:t0 + G], in_=xacc[:]),
                r=["xacc"], w=[], slot="ysst")
    sem = em.dma_sems["ysst"]
    em.op("g", lambda g: g.engine_nop(), after=[(sem, em.cnt[sem])])


def alloc_l1ffn_small(C):
    C.sbuf("s1t", [128, NC64])
    C.sbuf("s1r", [128, NC64])
    C.sbuf("deql", [128, NC64], BF16)
    C.sbuf("sbt", [128, 256])
    C.sbuf("sbb", [128, 256], BF16)
    C.sbuf("sbc", [128, 256], BF16)
    C.sbuf("qf2", [128, 4])
    C.sbuf("mpp", [128, 1])
    C.sbuf("rowsum", [128, 2])
    C.sbuf("msum", [1, 128])
    C.sbuf("m11", [1, 2])
    C.sbuf("stg", [128, 768], BF16)
    C.sbuf("ttin", [128, 128], BF16)
    C.sbuf("cidr", [128, 1])
    C.sbuf("cid1", [1, 1])


def alloc_l1ffn_body(C):
    C.sbuf("xg", [128, DC * 512])
    C.sbuf("xnbg", [128, DC * 512], BF16)
    C.sbuf("hbuf", [128, FC * 512], BF16)
    C.sbuf("valt", [128, PEL], BF16)
    C.sbuf("hsc", [128, 512])
    for i in range(4):
        C.sbuf(f"w12b{i}", [128, DC * 128], BF16)
    C.sbuf("xacc", [128, 512])
    C.sbuf("xsc", [128, 512])
    sb = C.sb
    sb["wsh"] = None


def wire_l1ffn_aliases(C):
    sb = C.sb
    sb["wsh"] = _Slice(sb["xg"][:].bitcast(BF16)[:, 0:PEL])
    sb["scb"] = _Slice(sb["hbuf"][:, 0:PEL])
    sb["indt"] = _Slice(sb["hbuf"][:, PEL:2 * PEL])
    sb["w3b0"] = _Slice(sb["valt"][:, 0:FC * 128])
    sb["w3b1"] = _Slice(sb["valt"][:, FC * 128:2 * FC * 128])


def build_l1ffn_program():
    nc = bass.Bass("TRN2", target_bir_lowering=False)
    em = EM(nc)
    C = Ctx(nc, em)
    xs = nc.dram_tensor("xs", [D, T], F32, kind="ExternalInput").ap()
    w1s = nc.dram_tensor("w1s", [SH1, D], BF16, kind="ExternalInput").ap()
    w2s = nc.dram_tensor("w2s", [SH1, D], BF16, kind="ExternalInput").ap()
    w3s = nc.dram_tensor("w3s", [SH3, DFF], BF16, kind="ExternalInput").ap()
    ys = nc.dram_tensor("ys", [D, T], F32, kind="ExternalOutput").ap()
    wq1 = nc.dram_tensor("wq1", [SH1, D], BF16).ap()
    wq2 = nc.dram_tensor("wq2", [SH1, D], BF16).ap()
    wq3 = nc.dram_tensor("wq3", [SH3, DFF], BF16).ap()
    wtp1 = nc.dram_tensor("wtp1", [D, SH1], BF16).ap()
    wtp2 = nc.dram_tensor("wtp2", [D, SH1], BF16).ap()
    wtp3 = nc.dram_tensor("wtp3", [DFF, SH3], BF16).ap()
    wta1 = nc.dram_tensor("wta1", [NCORES * D, SH1], BF16).ap()
    wta2 = nc.dram_tensor("wta2", [NCORES * D, SH1], BF16).ap()
    wta3 = nc.dram_tensor("wta3", [NCORES * DFF, SH3], BF16).ap()
    s1b = nc.dram_tensor("s1b", [128 * NC64], F32).ap()
    deqb = nc.dram_tensor("deqb", [128 * NC64], BF16).ap()
    mrb_in = nc.dram_tensor("mrbi", [1, 128], F32).ap()
    mrb_out = nc.dram_tensor("mrbo", [1, 128], F32).ap()
    setup_common(C)
    alloc_l1ffn_small(C)
    alloc_l1ffn_body(C)
    wire_l1ffn_aliases(C)
    for (wi, wshd, wqd, wtp, wta, SHR, C_) in (
            (1, w1s, wq1, wtp1, wta1, SH1, D),
            (2, w2s, wq2, wtp2, wta2, SH1, D),
            (3, w3s, wq3, wtp3, wta3, SH3, DFF)):
        emit_l1ffn_prep(C, wi, wshd, wqd, wtp, s1b, deqb, mrb_in, mrb_out, SHR, C_)
        C.em.cc(lambda g, wtp=wtp, wta=wta: g.collective_compute(
            "AllGather", ALU.bypass, replica_groups=[list(range(NCORES))],
            ins=[wtp.opt()], outs=[wta.opt()]),
            r=[f"w{wi}:wtp"], w=[f"w{wi}:wta"], slot="ccW")
    emit_l1ffn_body(C, wta1, wta2, wta3, xs, ys)
    em.finalize()
    return nc


def run_l1ffn(x, w1, w2, w3):
    nc = build_l1ffn_program()
    shards = _shard_stream(x)
    in_maps = []
    for c in range(NCORES):
        in_maps.append({
            "xs": shards[c],
            "w1s": np.ascontiguousarray(w1[c * SH1:(c + 1) * SH1]),
            "w2s": np.ascontiguousarray(w2[c * SH1:(c + 1) * SH1]),
            "w3s": np.ascontiguousarray(w3[c * SH3:(c + 1) * SH3])})
    res = run_bass_kernel_spmd(nc, in_maps, core_ids=list(range(NCORES)))
    return _unshard_stream([res.results[c]["ys"] for c in range(NCORES)])



def build_l0ffn_program():
    nc = bass.Bass("TRN2", target_bir_lowering=False)
    em = EM(nc)
    C = Ctx(nc, em)
    xs = nc.dram_tensor("xs", [D, T], F32, kind="ExternalInput").ap()
    ker = nc.dram_tensor("k0", [D, D], F32, kind="ExternalInput").ap()
    bias = nc.dram_tensor("b0", [D], F32, kind="ExternalInput").ap()
    ys = nc.dram_tensor("ys", [D, T], F32, kind="ExternalOutput").ap()
    setup_common(C)
    alloc_l0ffn(C)
    emit_l0ffn_prep(C, ker, bias)
    emit_stream_load(C, xs)
    emit_ln_stats(C)
    emit_l0ffn_body(C)
    emit_stream_store(C, ys)
    em.finalize()
    return nc


def _shard_stream(x):
    xs = np.ascontiguousarray(np.asarray(x, np.float32).reshape(8192, 2048))
    return [np.ascontiguousarray(xs[c * T:(c + 1) * T].T) for c in range(NCORES)]


def _unshard_stream(parts):
    out = np.empty((8192, 2048), np.float32)
    for c in range(NCORES):
        out[c * T:(c + 1) * T] = parts[c].T
    return out.reshape(1, 8192, 2048)


def run_l0ffn(x, kernel_w, bias_w):
    nc = build_l0ffn_program()
    shards = _shard_stream(x)
    in_maps = [{"xs": s, "k0": np.ascontiguousarray(kernel_w),
                "b0": np.ascontiguousarray(bias_w)} for s in shards]
    res = run_bass_kernel_spmd(nc, in_maps, core_ids=list(range(NCORES)))
    return _unshard_stream([res.results[c]["ys"] for c in range(NCORES)])


def _as_bf16(a):
    a = np.asarray(a)
    if a.dtype == np.uint16:
        return a.view(ml_dtypes.bfloat16)
    return a.astype(ml_dtypes.bfloat16) if a.dtype != ml_dtypes.bfloat16 else a


def kernel_unfused(**inputs):
    f32 = lambda k2: np.ascontiguousarray(np.asarray(inputs[k2], np.float32))
    x = f32("L0_attn_x")
    shards = _shard_stream(x)

    def launch(nc, maps, outkey="ys"):
        res = run_bass_kernel_spmd(nc, maps, core_ids=list(range(NCORES)))
        return [np.ascontiguousarray(res.results[c][outkey]) for c in range(NCORES)]

    nc_ffn0 = build_l0ffn_program()
    k0, b0 = f32("L0_ffn_kernel"), f32("L0_ffn_bias")
    shards = launch(nc_ffn0, [{"xs": s, "k0": k0, "b0": b0} for s in shards])

    nc_att0 = build_l0attn_program()
    maps = []
    for c in range(NCORES):
        maps.append({"xs": shards[c], "cid": np.array([c], np.float32),
                     "wq": f32("L0_attn_Wq"), "wk": f32("L0_attn_Wk"),
                     "wv": f32("L0_attn_Wv"), "wo": f32("L0_attn_Wo"),
                     "wg1": f32("L0_attn_Wg1"), "wg2": f32("L0_attn_Wg2"),
                     "ld": f32("L0_attn_log_decay"), "nw": f32("L0_attn_norm_w")})
    shards = launch(nc_att0, maps)

    shards = launch(nc_ffn0, [{"xs": s, "k0": k0, "b0": b0} for s in shards])

    nc_ffn1 = build_l1ffn_program()
    w1 = np.ascontiguousarray(_as_bf16(inputs["L1_ffn_weight1"]))
    w2 = np.ascontiguousarray(_as_bf16(inputs["L1_ffn_weight2"]))
    w3 = np.ascontiguousarray(_as_bf16(inputs["L1_ffn_weight3"]))
    def ffn1_maps(sh):
        return [{"xs": sh[c],
                 "w1s": np.ascontiguousarray(w1[c * SH1:(c + 1) * SH1]),
                 "w2s": np.ascontiguousarray(w2[c * SH1:(c + 1) * SH1]),
                 "w3s": np.ascontiguousarray(w3[c * SH3:(c + 1) * SH3])}
                for c in range(NCORES)]
    shards = launch(nc_ffn1, ffn1_maps(shards))

    nc_att1 = build_l1attn_program()
    maps = []
    for c in range(NCORES):
        maps.append({"xs": shards[c], "cid": np.array([c], np.float32),
                     "qkv": f32("L1_attn_qkv"), "op": f32("L1_attn_o_proj"),
                     "gamma": f32("L1_attn_gamma"), "beta": f32("L1_attn_beta")})
    shards = launch(nc_att1, maps)

    shards = launch(nc_ffn1, ffn1_maps(shards))

    return _unshard_stream(shards).astype(np.float32)



def build_fused_program():
    nc = bass.Bass("TRN2", target_bir_lowering=False)
    em = EM(nc)
    C = Ctx(nc, em)
    xin = nc.dram_tensor("xin", [D, T], F32, kind="ExternalInput").ap()
    cid = nc.dram_tensor("cid", [1], F32, kind="ExternalInput").ap()
    k0 = nc.dram_tensor("k0", [D, D], F32, kind="ExternalInput").ap()
    b0 = nc.dram_tensor("b0", [D], F32, kind="ExternalInput").ap()
    W0 = {nm: nc.dram_tensor(nm, shp, F32, kind="ExternalInput").ap()
          for nm, shp in (("wq", [D, D]), ("wk", [D, D]), ("wv", [D, D]),
                          ("wo", [D, D]), ("wg1", [D, 128]), ("wg2", [128, D]),
                          ("ld", [H]), ("nw", [D]))}
    W1 = {nm: nc.dram_tensor(nm, shp, F32, kind="ExternalInput").ap()
          for nm, shp in (("qkv", [D, 3 * D]), ("op", [D, D]),
                          ("gamma", [D]), ("beta", [D]))}
    w1s = nc.dram_tensor("w1s", [SH1, D], BF16, kind="ExternalInput").ap()
    w2s = nc.dram_tensor("w2s", [SH1, D], BF16, kind="ExternalInput").ap()
    w3s = nc.dram_tensor("w3s", [SH3, DFF], BF16, kind="ExternalInput").ap()
    xout = nc.dram_tensor("xout", [D, T], F32, kind="ExternalOutput").ap()
    st = [nc.dram_tensor(f"strm{i}", [D, T], F32).ap() for i in range(5)]
    kv_dram = [nc.dram_tensor(f"kvs{i}", [128, DC * 512], BF16).ap() for i in range(4)]
    sl_dram = nc.dram_tensor("slg", [128, H * HD], F32).ap()
    sall_dram = nc.dram_tensor("sag", [NCORES * 128, H * HD], F32).ap()
    kT_part = nc.dram_tensor("ktp", [D, T], BF16).ap()
    kT_all = nc.dram_tensor("kta", [NCORES * D, T], BF16).ap()
    v_part = nc.dram_tensor("vtp", [T, D], BF16).ap()
    v_all = nc.dram_tensor("vta", [NCORES * T, D], BF16).ap()
    wq1 = nc.dram_tensor("wq1", [SH1, D], BF16).ap()
    wq2 = nc.dram_tensor("wq2", [SH1, D], BF16).ap()
    wq3 = nc.dram_tensor("wq3", [SH3, DFF], BF16).ap()
    wtp1 = nc.dram_tensor("wtp1", [D, SH1], BF16).ap()
    wtp2 = nc.dram_tensor("wtp2", [D, SH1], BF16).ap()
    wtp3 = nc.dram_tensor("wtp3", [DFF, SH3], BF16).ap()
    wta1 = nc.dram_tensor("wta1", [NCORES * D, SH1], BF16).ap()
    wta2 = nc.dram_tensor("wta2", [NCORES * D, SH1], BF16).ap()
    wta3 = nc.dram_tensor("wta3", [NCORES * DFF, SH3], BF16).ap()
    s1b = nc.dram_tensor("s1b", [128 * NC64], F32).ap()
    deqb = nc.dram_tensor("deqb", [128 * NC64], BF16).ap()
    mrb_in = nc.dram_tensor("mrbi", [1, 128], F32).ap()
    mrb_out = nc.dram_tensor("mrbo", [1, 128], F32).ap()

    setup_common(C)
    C.freeze_persist()
    NW = 52400
    arena = nc.alloc_sbuf_tensor("ARENA", [128, NW], F32)
    C.enter_arena(arena, NW)

    def phase_l0ffn(xs_ap, ys_ap):
        C.phase_reset()
        alloc_l0ffn(C)
        emit_l0ffn_prep(C, k0, b0)
        emit_stream_load(C, xs_ap)
        emit_ln_stats(C)
        emit_l0ffn_body(C)
        emit_stream_store(C, ys_ap)

    phase_l0ffn(xin, st[0])
    C.phase_reset()
    emit_l0attn(C, W0, st[0], st[1], cid, kv_dram, sl_dram, sall_dram)
    phase_l0ffn(st[1], st[2])
    C.phase_reset()
    alloc_l1ffn_small(C)
    alloc_l1ffn_body(C)
    wire_l1ffn_aliases(C)
    for (wi, wshd, wqd, wtp, wta, SHR, C_) in (
            (1, w1s, wq1, wtp1, wta1, SH1, D),
            (2, w2s, wq2, wtp2, wta2, SH1, D),
            (3, w3s, wq3, wtp3, wta3, SH3, DFF)):
        emit_l1ffn_prep(C, wi, wshd, wqd, wtp, s1b, deqb, mrb_in, mrb_out, SHR, C_)
        em.cc(lambda g, wtp=wtp, wta=wta: g.collective_compute(
            "AllGather", ALU.bypass, replica_groups=[list(range(NCORES))],
            ins=[wtp.opt()], outs=[wta.opt()]),
            r=[f"w{wi}:wtp"], w=[f"w{wi}:wta"], slot="ccW")
    emit_l1ffn_body(C, wta1, wta2, wta3, st[2], st[3])
    C.phase_reset()
    C.sbuf("qqmask", [128, 512])
    C.sbuf("linv", [128, QG])
    emit_l1attn(C, W1, st[3], st[4], cid, kT_part, kT_all, v_part, v_all)
    C.phase_reset()
    alloc_l1ffn_small(C)
    alloc_l1ffn_body(C)
    wire_l1ffn_aliases(C)
    emit_l1ffn_body(C, wta1, wta2, wta3, st[4], xout)
    em.finalize()
    return nc


def kernel(**inputs):
    f32 = lambda k2: np.ascontiguousarray(np.asarray(inputs[k2], np.float32))
    x = f32("L0_attn_x")
    shards = _shard_stream(x)
    w1 = np.ascontiguousarray(_as_bf16(inputs["L1_ffn_weight1"]))
    w2 = np.ascontiguousarray(_as_bf16(inputs["L1_ffn_weight2"]))
    w3 = np.ascontiguousarray(_as_bf16(inputs["L1_ffn_weight3"]))
    nc = build_fused_program()
    maps = []
    for c in range(NCORES):
        maps.append({
            "xin": shards[c], "cid": np.array([c], np.float32),
            "k0": f32("L0_ffn_kernel"), "b0": f32("L0_ffn_bias"),
            "wq": f32("L0_attn_Wq"), "wk": f32("L0_attn_Wk"),
            "wv": f32("L0_attn_Wv"), "wo": f32("L0_attn_Wo"),
            "wg1": f32("L0_attn_Wg1"), "wg2": f32("L0_attn_Wg2"),
            "ld": f32("L0_attn_log_decay"), "nw": f32("L0_attn_norm_w"),
            "qkv": f32("L1_attn_qkv"), "op": f32("L1_attn_o_proj"),
            "gamma": f32("L1_attn_gamma"), "beta": f32("L1_attn_beta"),
            "w1s": np.ascontiguousarray(w1[c * SH1:(c + 1) * SH1]),
            "w2s": np.ascontiguousarray(w2[c * SH1:(c + 1) * SH1]),
            "w3s": np.ascontiguousarray(w3[c * SH3:(c + 1) * SH3])})
    res = run_bass_kernel_spmd(nc, maps, core_ids=list(range(NCORES)))
    return _unshard_stream(
        [np.ascontiguousarray(res.results[c]["xout"]) for c in range(NCORES)]
    ).astype(np.float32)
